# version 07
# baseline (speedup 1.0000x reference)
import math
import numpy as np
import ml_dtypes
from contextlib import ExitStack

import concourse.bass as bass
import concourse.mybir as mybir
from concourse.bass_utils import run_bass_kernel_spmd

F32 = mybir.dt.float32
BF16 = mybir.dt.bfloat16
AF = mybir.ActivationFunctionType
ALU = mybir.AluOpType
AX = mybir.AxisListType

D = 1024
SEQ = 8192
OWN = 2048
H = 16
NE = 8
NTT = SEQ // 128
EXP_SHIFT = -20.0
SILU_CLIP = 11.914 / (1.0 + math.exp(-11.914))
INV_1702 = 1.0 / 1.702
LN_EPS = 1e-5
C2_HEADS = H



def _wt_dma(eng, dst_sb, src_ap, nj):
    src3 = src_ap.rearrange("(j p) c -> p j c", p=128)
    dst3 = dst_sb[:, :].rearrange("p (j c) -> p j c", j=nj)
    return eng.dma_start(dst3, src3)

class Sem:
    def __init__(self, nc, es, name):
        self.h = es.enter_context(nc.semaphore(name))
        self.n = 0


def build_program(stop_after=None, taps=()):
    nc = bass.Bass("TRN2", target_bir_lowering=False)
    es = ExitStack()

    def din(name, shape, dt=F32):
        return nc.dram_tensor(name, shape, dt, kind="ExternalInput").ap()

    x_full = din("x_full", [SEQ, D])
    xT_own = din("xT_own", [D, OWN])
    qkv_wT = din("qkv_wT", [D, 3 * D])
    qkv_b = din("qkv_b", [128, 24])
    fc_wT = din("fc_wT", [D, D])
    fc_b = din("fc_b", [128, 8])
    rt_wT = din("rt_wT", [D, NE])
    rt_b = din("rt_b", [1, NE])
    gup_wT = din("gup_wT", [NE, D, 4096])
    gup_b = din("gup_b", [128, NE * 32])
    dn_wT = din("dn_wT", [NE, 2048, D])
    dn_b = din("dn_b", [NE, D])
    ident_in = din("ident", [128, 128], BF16)
    identf_in = din("identf", [128, 128], F32)
    sel_in = din("sel", [NE, NE * 128], BF16)
    sel16_in = din("sel16", [16, 16 * 64], BF16)

    out_T = nc.dram_tensor("out_T", [D, OWN], F32, kind="ExternalOutput").ap()

    def scratch(name, shape, dt, tap):
        kind = "ExternalOutput" if tap in taps else "Internal"
        return nc.dram_tensor(name, shape, dt, kind=kind).ap()

    K_hbm = scratch("K_hbm", [D, SEQ], BF16, "KV")
    V_hbm = scratch("V_hbm", [D, SEQ], BF16, "KV")
    g_hbm = scratch("g_hbm", [128, 8 * OWN], BF16, "g")
    dbg = {}
    for t, shape, dt in [("xn_own", [128, 8 * OWN], BF16), ("logits", [128, 128], F32),
                         ("comb", [8, OWN], BF16), ("y", [128, 8 * OWN], F32),
                         ("Q", [128, 8 * OWN], BF16), ("O", [128, 8 * OWN], BF16)]:
        if t in taps:
            dbg[t] = nc.dram_tensor("dbg_" + t, shape, dt, kind="ExternalOutput").ap()

    ld = Sem(nc, es, "ld")
    ldx = Sem(nc, es, "ldx")
    gs = Sem(nc, es, "gs")
    sp = Sem(nc, es, "sp")
    pe = Sem(nc, es, "pe")
    ve = Sem(nc, es, "ve")
    se = Sem(nc, es, "se")

    _tapn = [0]
    def tap_dump(name_pairs):
        _tapn[0] += 1
        with nc.Block(f"dbg{_tapn[0]}") as blk:
            @blk.sync
            def _(s):
                n = ld.n
                for dst, src in name_pairs:
                    s.dma_start(dst, src).then_inc(ld.h, 16)
                    n += 16
                s.wait_ge(ld.h, n)
        ld.n += 16 * len(name_pairs)

    def finish():
        with nc.Block("fin") as blk:
            @blk.sync
            def _(s):
                s.wait_ge(sp.h, sp.n)
        es.close()
        return nc, dbg

    sbp = lambda name, shape, dt: es.enter_context(nc.sbuf_tensor(name, shape, dt))
    qkv_b_sb = sbp("qkv_b_sb", [128, 24], F32)
    fc_b_sb = sbp("fc_b_sb", [128, 8], F32)
    rt_b_sb = sbp("rt_b_sb", [1, NE], BF16)
    gup_b_sb = sbp("gup_b_sb", [128, NE * 32], F32)
    dn_b_sb = sbp("dn_b_sb", [8, D], BF16)
    ident = sbp("ident_sb", [128, 128], BF16)
    identf = sbp("identf_sb", [128, 128], F32)
    sel_sb = sbp("sel_sb", [NE, NE * 128], BF16)
    ones1_bf = sbp("ones1_bf", [1, 128], BF16)
    ones1_f = sbp("ones1_f", [1, 128], F32)
    qbs = sbp("qbs", [128, 8], F32)
    xn_own = sbp("xn_own", [128, 8 * OWN], BF16)
    c_eps = sbp("c_eps", [128, 1], F32)
    c_shift = sbp("c_shift", [128, 1], F32)
    nc.const_aps.register(F32, LN_EPS, c_eps[:, 0:1])
    nc.const_aps.register(F32, EXP_SHIFT, c_shift[:, 0:1])

    pA = ExitStack()
    psb = lambda name, shape, dt: pA.enter_context(nc.sbuf_tensor(name, shape, dt))
    pps = lambda name, shape, dt: pA.enter_context(nc.psum_tensor(name, shape, dt))
    xnT = psb("xnT", [128, 8 * SEQ], BF16)
    x_buf = [psb(f"x_buf{i}", [128, D], F32) for i in range(2)]
    sq_buf = psb("sq_buf", [128, D], F32)
    xn_buf = [psb(f"xn_buf{i}", [128, D], BF16) for i in range(2)]
    st_s1 = psb("st_s1", [128, 1], F32)
    st_s2 = psb("st_s2", [128, 1], F32)
    st_m = psb("st_m", [128, 1], F32)
    st_v = psb("st_v", [128, 1], F32)
    st_msq = psb("st_msq", [128, 1], F32)
    st_sd = psb("st_sd", [128, 1], F32)
    st_rstd = psb("st_rstd", [128, 1], F32)
    st_mrn = psb("st_mrn", [128, 1], F32)
    pA1ps = ExitStack()
    ps_tr = [pA1ps.enter_context(nc.psum_tensor(f"ps_tr{i}", [128, D], BF16)) for i in range(2)]

    ld0, pe0, ve0, se0 = ld.n, pe.n, ve.n, se.n
    NSMALL = 6
    ldx0 = ldx.n
    ldv = lambda tt: ldx0 + 16 * (tt + 1)
    ve_sq = lambda tt: ve0 + 7 * tt + 1
    ve_red = lambda tt: ve0 + 7 * tt + 2
    ve_msq = lambda tt: ve0 + 7 * tt + 3
    ve_st1 = lambda tt: ve0 + 7 * tt + 4
    ve_rcp = lambda tt: ve0 + 7 * tt + 5
    ve_st2 = lambda tt: ve0 + 7 * tt + 6
    ve_evac = lambda tt: ve0 + 7 * tt + 7
    se_sd = lambda tt: se0 + 2 * tt + 1
    se_xn = lambda tt: se0 + 2 * tt + 2
    pe_tr = lambda tt: pe0 + tt + 1

    with nc.Block("A1") as blk:
        @blk.sync
        def _(s):
            s.dma_start(qkv_b_sb[:, :], qkv_b).then_inc(ld.h, 16)
            s.dma_start(fc_b_sb[:, :], fc_b).then_inc(ld.h, 16)
            s.dma_start(gup_b_sb[:, :], gup_b).then_inc(ld.h, 16)
            s.dma_start(ident[:, :], ident_in).then_inc(ld.h, 16)
            s.dma_start(identf[:, :], identf_in).then_inc(ld.h, 16)
            s.dma_start(sel_sb[:, :], sel_in).then_inc(ld.h, 16)
            for tt in range(NTT):
                if tt >= 2:
                    s.wait_ge(se.h, se_xn(tt - 2))
                s.dma_start(x_buf[tt % 2][:, :],
                            x_full[tt * 128:(tt + 1) * 128, :]).then_inc(ldx.h, 16)

        @blk.gpsimd
        def _(g):
            g.dma_start(rt_b_sb[:, :], rt_b).then_inc(gs.h, 16)
            g.dma_start(dn_b_sb[:, :], dn_b).then_inc(gs.h, 16)

        @blk.vector
        def _(v):
            v.memset(ones1_bf[:, :], 1.0)
            v.memset(ones1_f[:, :], 1.0)
            v.memset(c_eps[:, :], LN_EPS)
            v.memset(c_shift[:, :], EXP_SHIFT)
            v.wait_ge(ld.h, ld0 + 16 * NSMALL)
            v.tensor_scalar(qbs[:, :], qkv_b_sb[:, 0:8], 0.125, None, ALU.mult)
            for tt in range(NTT):
                x = x_buf[tt % 2]
                v.wait_ge(ldx.h, ldv(tt))
                v.tensor_tensor(sq_buf[:, :], x[:, :], x[:, :], ALU.mult).then_inc(ve.h, 1)
                v.reduce_sum(st_s1[:, :], x[:, :], axis=AX.X)
                v.wait_ge(ve.h, ve_sq(tt))
                v.reduce_sum(st_s2[:, :], sq_buf[:, :], axis=AX.X).then_inc(ve.h, 1)
                v.wait_ge(ve.h, ve_red(tt))
                v.tensor_scalar(st_m[:, :], st_s1[:, :], 1.0 / D, None, ALU.mult)
                v.tensor_scalar(st_v[:, :], st_s2[:, :], 1.0 / D, None, ALU.mult)
                v.tensor_tensor(st_msq[:, :], st_m[:, :], st_m[:, :],
                                ALU.mult).then_inc(ve.h, 1)
                v.wait_ge(ve.h, ve_msq(tt))
                v.tensor_tensor(st_v[:, :], st_v[:, :], st_msq[:, :],
                                ALU.subtract).then_inc(ve.h, 1)
                v.wait_ge(se.h, se_sd(tt))
                v.reciprocal(st_rstd[:, :], st_sd[:, :]).then_inc(ve.h, 1)
                v.wait_ge(ve.h, ve_rcp(tt))
                v.scalar_tensor_tensor(st_mrn[:, :], st_m[:, :], -1.0, st_rstd[:, :],
                                       ALU.mult, ALU.mult).then_inc(ve.h, 1)
                v.wait_ge(pe.h, pe_tr(tt))
                src = ps_tr[tt % 2][:, :].rearrange("p (j t) -> p j t", j=8)
                dst = xnT[:, :].rearrange("p (j t) -> p j t", j=8)[:, :, tt * 128:(tt + 1) * 128]
                v.tensor_copy(dst, src).then_inc(ve.h, 1)

        @blk.scalar
        def _(sc):
            for tt in range(NTT):
                sc.wait_ge(ve.h, ve_st1(tt))
                sc.activation(st_sd[:, :], st_v[:, :], AF.Sqrt, bias=LN_EPS,
                              scale=1.0).then_inc(se.h, 1)
                sc.wait_ge(ve.h, ve_st2(tt))
                if tt >= 2:
                    sc.wait_ge(pe.h, pe_tr(tt - 2))
                sc.activation(xn_buf[tt % 2][:, :], x_buf[tt % 2][:, :], AF.Identity,
                              bias=st_mrn[:, 0:1], scale=st_rstd[:, 0:1]
                              ).then_inc(se.h, 1)

        @blk.tensor
        def _(t):
            for tt in range(NTT):
                t.wait_ge(se.h, se_xn(tt))
                if tt >= 2:
                    t.wait_ge(ve.h, ve_evac(tt - 2))
                for j in range(8):
                    ins = t.transpose(ps_tr[tt % 2][:, j * 128:(j + 1) * 128],
                                      xn_buf[tt % 2][:, j * 128:(j + 1) * 128],
                                      ident[:, :])
                ins.then_inc(pe.h, 1)

    ld.n = ld0 + 16 * NSMALL
    ldx.n = ldx0 + 16 * NTT
    gs.n += 32
    ve.n = ve0 + 7 * NTT
    se.n = se0 + 2 * NTT
    pe.n = pe0 + NTT
    pA1ps.close()

    pA2 = ExitStack()
    wkv_sb = pA2.enter_context(nc.sbuf_tensor("wkv_sb", [128, 8 * D], BF16))
    kv_stage = [pA2.enter_context(nc.sbuf_tensor(f"kv_stage{i}", [128, 2048], BF16))
                for i in range(2)]
    ps_kv = [pA2.enter_context(nc.psum_tensor(f"ps_kv{i}", [128, 2048], F32))
             for i in range(2)]

    gs0, pe0, se0, sp0, ve0 = gs.n, pe.n, se.n, sp.n, ve.n
    NG = 32
    pe_g = lambda g_: pe0 + g_ + 1
    se_g = lambda g_: se0 + g_ + 1
    sp_g = lambda g_: sp0 + 16 * (g_ + 1)

    with nc.Block("A2") as blk:
        @blk.vector
        def _(v):
            src = xnT[:, :].rearrange("p (j t) -> p j t", j=8)[:, :, 0:OWN]
            dst = xn_own[:, :].rearrange("p (j t) -> p j t", j=8)
            v.tensor_copy(dst, src).then_inc(ve.h, 1)

        @blk.gpsimd
        def _(g):
            _wt_dma(g, wkv_sb, qkv_wT[:, D:2 * D], 8).then_inc(gs.h, 16)
            g.wait_ge(pe.h, pe_g(NG - 1))
            _wt_dma(g, wkv_sb, qkv_wT[:, 2 * D:3 * D], 8).then_inc(gs.h, 16)

        @blk.tensor
        def _(t):
            for gi in range(2 * NG):
                proj, jb, tcg = gi // NG, (gi % NG) // 4, (gi % NG) % 4
                t.wait_ge(gs.h, gs0 + 16 * (proj + 1))
                if gi >= 2:
                    t.wait_ge(se.h, se_g(gi - 2))
                for j in range(8):
                    lhsT = wkv_sb[:, j * D + jb * 128: j * D + (jb + 1) * 128]
                    for tc in range(4):
                        col = tcg * 2048 + tc * 512
                        ins = t.matmul(ps_kv[gi % 2][:, tc * 512:(tc + 1) * 512],
                                       lhsT, xnT[:, j * SEQ + col: j * SEQ + col + 512],
                                       start=(j == 0), stop=(j == 7))
                ins.then_inc(pe.h, 1)

        @blk.scalar
        def _(sc):
            for gi in range(2 * NG):
                proj, jb, tcg = gi // NG, (gi % NG) // 4, (gi % NG) % 4
                sc.wait_ge(pe.h, pe_g(gi))
                if gi >= 2:
                    sc.wait_ge(sp.h, sp_g(gi - 2))
                bias_col = 8 * (proj + 1) + jb
                sc.activation(kv_stage[gi % 2][:, :], ps_kv[gi % 2][:, :], AF.Identity,
                              bias=qkv_b_sb[:, bias_col:bias_col + 1],
                              scale=1.0).then_inc(se.h, 1)
                sc.wait_ge(se.h, se_g(gi))
                tgt = K_hbm if proj == 0 else V_hbm
                sc.dma_start(tgt[jb * 128:(jb + 1) * 128, tcg * 2048:(tcg + 1) * 2048],
                             kv_stage[gi % 2][:, :]).then_inc(sp.h, 16)

    gs.n = gs0 + 32
    ve.n = ve0 + 1
    pe.n = pe0 + 2 * NG
    se.n = se0 + 2 * NG
    sp.n = sp0 + 16 * 2 * NG
    SP_KV_DONE = sp.n
    pA2.close()
    pA.close()

    if "xn_own" in taps:
        tap_dump([(dbg["xn_own"], xn_own[:, :])])
    if stop_after == "A":
        return finish()

    pB = ExitStack()
    psb = lambda name, shape, dt: pB.enter_context(nc.sbuf_tensor(name, shape, dt))
    y_sb = psb("y_sb", [128, 8 * OWN], F32)
    A_sb = psb("A_sb", [128, 16 * OWN], BF16)
    gup_buf = [psb(f"gup_buf{i}", [128, 8 * 256], BF16) for i in range(2)]
    dn_buf = [psb(f"dn_buf{i}", [128, 16 * 128], BF16) for i in range(2)]
    gsl_buf = [psb(f"gsl_buf{i}", [64, OWN], BF16) for i in range(2)]
    uc_buf = psb("uc_buf", [64, OWN], BF16)
    combB = psb("combB", [128, OWN], BF16)
    combT = psb("combT", [8, OWN], BF16)
    rt_w_sb = psb("rt_w_sb", [128, 8 * NE], BF16)
    logits_sb = psb("logits_sb", [128, 128], F32)
    lwork = psb("lwork", [128, 8], F32)
    m4 = psb("m4", [128, 4], F32)
    mtmp = psb("mtmp", [128, 8], F32)
    exb = psb("exb", [128, 8], F32)
    negm = psb("negm", [128, 1], F32)
    den = psb("den", [128, 1], F32)
    rden = psb("rden", [128, 1], F32)
    bsig = psb("bsig", [64, 32], F32)

    pB1ps = ExitStack()
    ps_r = pB1ps.enter_context(nc.psum_tensor("ps_r", [128, 128], F32))
    ps_ct = pB1ps.enter_context(nc.psum_tensor("ps_ct", [8, 512], F32))
    ps_yi = pB1ps.enter_context(nc.psum_tensor("ps_yi", [128, OWN], F32))

    gs0, pe0, ve0, se0 = gs.n, pe.n, ve.n, se.n
    VB = 19
    tb_base = lambda tb: ve0 + 1 + VB * tb
    ve_negm = lambda tb: tb_base(tb) + 15
    ve_exm = lambda tb: tb_base(tb) + 16
    ve_comb_all = ve0 + 1 + VB * 16
    ve_ct = lambda tbg: ve_comb_all + tbg + 1
    ve_combT = ve_comb_all + 4
    ve_yev = lambda db: ve_combT + db + 1
    se_exp = lambda tb: se0 + tb + 1
    pe_ct = lambda tbg: pe0 + 1 + tbg + 1
    pe_yi = lambda db: pe0 + 5 + db + 1

    with nc.Block("B1") as blk:
        @blk.gpsimd
        def _(g):
            _wt_dma(g, rt_w_sb, rt_wT[:, :], 8).then_inc(gs.h, 16)

        @blk.tensor
        def _(t):
            t.wait_ge(gs.h, gs0 + 16)
            for tb in range(16):
                t.matmul(ps_r[:, tb * 8:(tb + 1) * 8], ones1_bf[:, :], rt_b_sb[:, :],
                         start=True, stop=False)
                for j in range(8):
                    ins = t.matmul(ps_r[:, tb * 8:(tb + 1) * 8],
                                   xn_own[:, j * OWN + tb * 128: j * OWN + (tb + 1) * 128],
                                   rt_w_sb[:, j * NE:(j + 1) * NE],
                                   start=False, stop=(j == 7))
            ins.then_inc(pe.h, 1)
            t.wait_ge(ve.h, ve_comb_all)
            for tbg in range(4):
                if tbg > 0:
                    t.wait_ge(ve.h, ve_ct(tbg - 1))
                for i in range(4):
                    tb = tbg * 4 + i
                    ins = t.transpose(ps_ct[:, i * 128:(i + 1) * 128],
                                      logits_sb[:, tb * 8:(tb + 1) * 8], identf[:, :])
                ins.then_inc(pe.h, 1)
            t.wait_ge(ve.h, ve_combT)
            for db in range(8):
                if db >= 1:
                    t.wait_ge(ve.h, ve_yev(db - 1))
                for tc in range(4):
                    ins = t.matmul(ps_yi[:, tc * 512:(tc + 1) * 512],
                                   dn_b_sb[:, db * 128:(db + 1) * 128],
                                   combT[:, tc * 512:(tc + 1) * 512],
                                   start=True, stop=True)
                ins.then_inc(pe.h, 1)

        @blk.vector
        def _(v):
            v.wait_ge(pe.h, pe0 + 1)
            v.tensor_copy(logits_sb[:, :], ps_r[:, :]).then_inc(ve.h, 1)
            for tb in range(16):
                lg = logits_sb[:, tb * 8:(tb + 1) * 8]
                v.tensor_copy(lwork[:, :], lg).then_inc(ve.h, 1)
                v.wait_ge(ve.h, tb_base(tb) + 1)
                for i in range(4):
                    v.reduce_max(m4[:, i:i + 1], lwork[:, :], axis=AX.X).then_inc(ve.h, 1)
                    v.wait_ge(ve.h, tb_base(tb) + 2 + 3 * i)
                    v.tensor_scalar(mtmp[:, :], lwork[:, :], m4[:, i:i + 1], -2e30,
                                    ALU.is_ge, ALU.mult).then_inc(ve.h, 1)
                    v.wait_ge(ve.h, tb_base(tb) + 3 + 3 * i)
                    v.tensor_tensor(lwork[:, :], lwork[:, :], mtmp[:, :],
                                    ALU.add).then_inc(ve.h, 1)
                    v.wait_ge(ve.h, tb_base(tb) + 4 + 3 * i)
                v.tensor_scalar(mtmp[:, :], lg, m4[:, 3:4], None,
                                ALU.is_ge).then_inc(ve.h, 1)
                v.tensor_scalar(negm[:, :], m4[:, 0:1], -1.0, None,
                                ALU.mult).then_inc(ve.h, 1)
                v.wait_ge(se.h, se_exp(tb))
                v.tensor_tensor(exb[:, :], exb[:, :], mtmp[:, :],
                                ALU.mult).then_inc(ve.h, 1)
                v.wait_ge(ve.h, tb_base(tb) + 16)
                v.reduce_sum(den[:, :], exb[:, :], axis=AX.X).then_inc(ve.h, 1)
                v.wait_ge(ve.h, tb_base(tb) + 17)
                v.reciprocal(rden[:, :], den[:, :]).then_inc(ve.h, 1)
                v.wait_ge(ve.h, tb_base(tb) + 18)
                v.tensor_scalar(lg, exb[:, :], rden[:, 0:1], None,
                                ALU.mult).then_inc(ve.h, 1)
            for tbg in range(4):
                v.wait_ge(pe.h, pe_ct(tbg))
                v.tensor_copy(combT[:, tbg * 512:(tbg + 1) * 512],
                              ps_ct[:, :]).then_inc(ve.h, 1)
            for db in range(8):
                v.wait_ge(pe.h, pe_yi(db))
                v.tensor_copy(y_sb[:, db * OWN:(db + 1) * OWN],
                              ps_yi[:, :]).then_inc(ve.h, 1)

        @blk.scalar
        def _(sc):
            for tb in range(16):
                sc.wait_ge(ve.h, ve_negm(tb))
                if tb >= 1:
                    sc.wait_ge(ve.h, ve_exm(tb - 1))
                sc.activation(exb[:, :], logits_sb[:, tb * 8:(tb + 1) * 8], AF.Exp,
                              bias=negm[:, 0:1], scale=1.0).then_inc(se.h, 1)

    gs.n = gs0 + 16
    pe.n = pe0 + 5 + 8
    ve.n = ve_yev(7)
    se.n = se0 + 16
    pB1ps.close()

    pairs = []
    if "logits" in taps:
        pairs.append((dbg["logits"], logits_sb[:, :]))
    if "comb" in taps:
        pairs.append((dbg["comb"], combT[:, :]))
    if pairs:
        tap_dump(pairs)

    pB2ps = ExitStack()
    ps_gu = [pB2ps.enter_context(nc.psum_tensor(f"ps_gu{i}", [128, OWN], F32))
             for i in range(2)]

    gs0, pe0, ve0, se0 = gs.n, pe.n, ve.n, se.n
    PEE = 41
    pe_bcast = lambda e: pe0 + PEE * e + 1
    pe_gup = lambda e, b: pe0 + PEE * e + 1 + (b + 1)
    pe_dn = lambda e, db: pe0 + PEE * e + 33 + (db + 1)
    VEE = 138
    ve_combB = lambda e: ve0 + VEE * e + 1
    ve_bsig = lambda e: ve0 + VEE * e + 2
    ve_act = lambda e, b: ve0 + VEE * e + 2 + 4 * (b + 1)
    ve_yacc = lambda e, db: ve0 + VEE * e + 130 + (db + 1)
    se_gs = lambda e, b: se0 + 32 * e + b + 1
    gs_gup = lambda e, ch: gs0 + 16 * (24 * e + ch + 1)
    gs_dn = lambda e, db: gs0 + 16 * (24 * e + 16 + db + 1)

    with nc.Block("B2") as blk:
        @blk.gpsimd
        def _(g):
            for e in range(NE):
                for ch in range(16):
                    pv = (e, ch - 2) if ch >= 2 else (e - 1, ch + 14)
                    if pv[0] >= 0:
                        g.wait_ge(pe.h, pe_gup(pv[0], pv[1] * 2 + 1))
                    _wt_dma(g, gup_buf[ch % 2], gup_wT[e, :, ch * 256:(ch + 1) * 256],
                            8).then_inc(gs.h, 16)
                for db in range(8):
                    pv = (e, db - 2) if db >= 2 else (e - 1, db + 6)
                    if pv[0] >= 0:
                        g.wait_ge(pe.h, pe_dn(pv[0], pv[1]))
                    _wt_dma(g, dn_buf[db % 2], dn_wT[e, :, db * 128:(db + 1) * 128],
                            16).then_inc(gs.h, 16)

        @blk.tensor
        def _(t):
            for e in range(NE):
                if e > 0:
                    t.wait_ge(ve.h, ve_yacc(e - 1, 7))
                for tc in range(4):
                    ins = t.matmul(ps_gu[0][:, tc * 512:(tc + 1) * 512],
                                   sel_sb[:, e * 128:(e + 1) * 128],
                                   combT[:, tc * 512:(tc + 1) * 512],
                                   start=True, stop=True)
                ins.then_inc(pe.h, 1)
                t.wait_ge(ve.h, ve_combB(e))
                for b in range(32):
                    ch = b // 2
                    t.wait_ge(gs.h, gs_gup(e, ch))
                    if b >= 2:
                        t.wait_ge(ve.h, ve_act(e, b - 2))
                    for j in range(8):
                        lhsT = gup_buf[ch % 2][:, j * 256 + (b % 2) * 128:
                                               j * 256 + (b % 2) * 128 + 128]
                        for tc in range(4):
                            ins = t.matmul(ps_gu[b % 2][:, tc * 512:(tc + 1) * 512],
                                           lhsT,
                                           xn_own[:, j * OWN + tc * 512: j * OWN + (tc + 1) * 512],
                                           start=(j == 0), stop=(j == 7))
                    ins.then_inc(pe.h, 1)
                t.wait_ge(ve.h, ve_act(e, 31))
                for db in range(8):
                    t.wait_ge(gs.h, gs_dn(e, db))
                    if db >= 2:
                        t.wait_ge(ve.h, ve_yacc(e, db - 2))
                    for ic in range(16):
                        lhsT = dn_buf[db % 2][:, ic * 128:(ic + 1) * 128]
                        for tc in range(4):
                            ins = t.matmul(ps_gu[db % 2][:, tc * 512:(tc + 1) * 512],
                                           lhsT,
                                           A_sb[:, ic * OWN + tc * 512: ic * OWN + (tc + 1) * 512],
                                           start=(ic == 0), stop=(ic == 15))
                    ins.then_inc(pe.h, 1)

        @blk.scalar
        def _(sc):
            for e in range(NE):
                sc.wait_ge(ve.h, ve_bsig(e))
                for b in range(32):
                    sc.wait_ge(pe.h, pe_gup(e, b))
                    if b >= 2:
                        sc.wait_ge(ve.h, ve_act(e, b - 2))
                    sc.activation(gsl_buf[b % 2][:, :], ps_gu[b % 2][0:64, :], AF.Silu,
                                  bias=bsig[:, b:b + 1], scale=1.702).then_inc(se.h, 1)

        @blk.vector
        def _(v):
            for e in range(NE):
                v.wait_ge(pe.h, pe_bcast(e))
                v.tensor_scalar(combB[:, :], ps_gu[0][:, :], INV_1702, None,
                                ALU.mult).then_inc(ve.h, 1)
                v.tensor_scalar(bsig[:, :], gup_b_sb[0:64, e * 32:(e + 1) * 32], 1.702,
                                None, ALU.mult).then_inc(ve.h, 1)
                for b in range(32):
                    jj = b // 2
                    p0 = (b % 2) * 64
                    v.wait_ge(pe.h, pe_gup(e, b))
                    v.wait_ge(se.h, se_gs(e, b))
                    v.tensor_scalar(uc_buf[:, :], ps_gu[b % 2][64:128, :],
                                    gup_b_sb[64:128, e * 32 + b:e * 32 + b + 1], 7.0,
                                    ALU.add, ALU.min).then_inc(ve.h, 1)
                    v.wait_ge(ve.h, ve_act(e, b) - 3)
                    v.tensor_scalar(uc_buf[:, :], uc_buf[:, :], -7.0, None, ALU.max)
                    v.tensor_scalar(gsl_buf[b % 2][:, :], gsl_buf[b % 2][:, :],
                                    SILU_CLIP, None, ALU.min).then_inc(ve.h, 1)
                    v.wait_ge(ve.h, ve_act(e, b) - 2)
                    v.tensor_tensor(gsl_buf[b % 2][:, :], gsl_buf[b % 2][:, :],
                                    combB[0:64, :], ALU.mult).then_inc(ve.h, 1)
                    v.wait_ge(ve.h, ve_act(e, b) - 1)
                    v.scalar_tensor_tensor(A_sb[p0:p0 + 64, jj * OWN:(jj + 1) * OWN],
                                           uc_buf[:, :], 1.0, gsl_buf[b % 2][:, :],
                                           ALU.add, ALU.mult).then_inc(ve.h, 1)
                for db in range(8):
                    v.wait_ge(pe.h, pe_dn(e, db))
                    v.tensor_tensor(y_sb[:, db * OWN:(db + 1) * OWN],
                                    y_sb[:, db * OWN:(db + 1) * OWN],
                                    ps_gu[db % 2][:, :], ALU.add).then_inc(ve.h, 1)

    gs.n = gs0 + 16 * 24 * NE
    pe.n = pe0 + PEE * NE
    ve.n = ve0 + VEE * NE
    se.n = se0 + 32 * NE

    se0, sp0 = se.n, sp.n
    with nc.Block("Bg") as blk:
        @blk.scalar
        def _(sc):
            for jb in range(8):
                if jb >= 1:
                    sc.wait_ge(sp.h, sp0 + 32 * jb)
                sc.activation(gsl_buf[jb % 2][:, :], y_sb[0:64, jb * OWN:(jb + 1) * OWN],
                              AF.Silu, bias=0.0, scale=1.0)
                sc.activation(uc_buf[:, :], y_sb[64:128, jb * OWN:(jb + 1) * OWN],
                              AF.Silu, bias=0.0, scale=1.0).then_inc(se.h, 1)
                sc.wait_ge(se.h, se0 + jb + 1)
                sc.dma_start(g_hbm[0:64, jb * OWN:(jb + 1) * OWN],
                             gsl_buf[jb % 2][:, :]).then_inc(sp.h, 16)
                sc.dma_start(g_hbm[64:128, jb * OWN:(jb + 1) * OWN],
                             uc_buf[:, :]).then_inc(sp.h, 16)
    se.n = se0 + 8
    sp.n = sp0 + 16 * 16
    SP_G_DONE = sp.n

    if "y" in taps:
        tap_dump([(dbg["y"], y_sb[:, :])])
    pB2ps.close()
    pB.close()
    if stop_after == "B":
        return finish()

    Qp = es.enter_context(nc.sbuf_tensor("Qp", [128, 8 * OWN], BF16))
    O_all = es.enter_context(nc.sbuf_tensor("O_all", [128, 8 * OWN], BF16))

    pC1 = ExitStack()
    wq_sb = pC1.enter_context(nc.sbuf_tensor("wq_sb", [128, 8 * D], BF16))
    ps_q = [pC1.enter_context(nc.psum_tensor(f"ps_q{i}", [128, OWN], F32))
            for i in range(2)]

    gs0, pe0, se0 = gs.n, pe.n, se.n
    with nc.Block("C1") as blk:
        @blk.gpsimd
        def _(g):
            _wt_dma(g, wq_sb, qkv_wT[:, 0:D], 8).then_inc(gs.h, 16)

        @blk.tensor
        def _(t):
            t.wait_ge(gs.h, gs0 + 16)
            for jb in range(8):
                if jb >= 2:
                    t.wait_ge(se.h, se0 + jb - 1)
                for j in range(8):
                    lhsT = wq_sb[:, j * D + jb * 128: j * D + (jb + 1) * 128]
                    for tc in range(4):
                        ins = t.matmul(ps_q[jb % 2][:, tc * 512:(tc + 1) * 512],
                                       lhsT,
                                       xn_own[:, j * OWN + tc * 512: j * OWN + (tc + 1) * 512],
                                       start=(j == 0), stop=(j == 7))
                ins.then_inc(pe.h, 1)

        @blk.scalar
        def _(sc):
            for jb in range(8):
                sc.wait_ge(pe.h, pe0 + jb + 1)
                sc.activation(Qp[:, jb * OWN:(jb + 1) * OWN], ps_q[jb % 2][:, :],
                              AF.Identity, bias=qbs[:, jb:jb + 1],
                              scale=0.125).then_inc(se.h, 1)
    gs.n = gs0 + 16
    pe.n = pe0 + 8
    se.n = se0 + 8
    pC1.close()

    if "Q" in taps:
        tap_dump([(dbg["Q"], Qp[:, :])])
    if stop_after == "C1":
        return finish()

    pC2 = ExitStack()
    psb = lambda name, shape, dt: pC2.enter_context(nc.sbuf_tensor(name, shape, dt))
    K_sb = [psb(f"K_sb{i}", [128, SEQ], BF16) for i in range(2)]
    V_sb = psb("V_sb", [128, SEQ], BF16)
    VT_e = psb("VT_e", [128, 64 * 80], BF16)
    VT_o = psb("VT_o", [128, 64 * 80], BF16)
    PT_e = [psb(f"PT_e{i}", [128, 1024], BF16) for i in range(2)]
    PT_o = [psb(f"PT_o{i}", [128, 1024], BF16) for i in range(2)]
    D_all = psb("D_all", [16, OWN], BF16)
    D_st = [psb(f"D_st{i}", [1, 1024], BF16) for i in range(4)]
    sel16_sb = psb("sel16_sb", [16, 16 * 64], BF16)
    pC2ps = ExitStack()
    ps_oe = pC2ps.enter_context(nc.psum_tensor("ps_oe", [65, 1024], F32))
    ps_oo = pC2ps.enter_context(nc.psum_tensor("ps_oo", [65, 1024], F32))
    ps_ste = pC2ps.enter_context(nc.psum_tensor("ps_ste", [128, 1024], F32))
    ps_sto = pC2ps.enter_context(nc.psum_tensor("ps_sto", [128, 1024], F32))
    trview = ps_ste[:, 0:512].bitcast(BF16)

    ld0, pe0, ve0, se0, gs0c = ld.n, pe.n, ve.n, se.n, gs.n
    NP = C2_HEADS // 2
    PEP = 8 + 2 * 256
    qcb = lambda qc: 8 + qc * 256
    pe_trb = lambda p, b2: pe0 + PEP * p + b2 + 1
    pe_ste = lambda p, qc, kt: pe0 + PEP * p + qcb(qc) + (1 if kt == 0 else 4 * kt + 1)
    pe_sto = lambda p, qc, kt: pe_ste(p, qc, kt) + 1
    pe_ave = lambda p, qc, kt: pe0 + PEP * p + qcb(qc) + 3 + 4 * kt
    pe_avo = lambda p, qc, kt: pe0 + PEP * p + qcb(qc) + 4 + 4 * kt
    SEP = 260
    se_expe = lambda p, qc, kt: se0 + SEP * p + 130 * qc + 2 * kt + 1
    se_expo = lambda p, qc, kt: se0 + SEP * p + 130 * qc + 2 * kt + 2
    se_oev = lambda p, qc, eo: se0 + SEP * p + 130 * qc + 128 + eo + 1
    VEP = 12
    ve_vt = lambda p, b2: ve0 + VEP * p + b2 + 1
    ve_dst = lambda p, qc, eo: ve0 + VEP * p + 8 + 2 * qc + eo + 1
    ld_kv = lambda p: ld0 + 16 + 16 * 2 * (p + 1)
    gs_d = lambda p, qc, eo: gs0c + 16 * (4 * p + 2 * qc + eo + 1)

    with nc.Block("C2") as blk:
        @blk.sync
        def _(s):
            s.dma_start(sel16_sb[:, :], sel16_in).then_inc(ld.h, 16)
            for p in range(NP):
                if p == 0:
                    s.wait_ge(sp.h, SP_KV_DONE)
                if p >= 2:
                    s.wait_ge(pe.h, pe_ste(p - 2, 1, 63))
                s.dma_start(K_sb[p % 2][:, :],
                            K_hbm[p * 128:(p + 1) * 128, :]).then_inc(ld.h, 16)
                if p >= 1:
                    s.wait_ge(pe.h, pe_trb(p - 1, 7))
                s.dma_start(V_sb[:, :],
                            V_hbm[p * 128:(p + 1) * 128, :]).then_inc(ld.h, 16)

        @blk.gpsimd
        def _(g):
            for p in range(NP):
                for qc in range(2):
                    for eo in range(2):
                        g.wait_ge(ve.h, ve_dst(p, qc, eo))
                        g.dma_start(D_all[2 * p + eo:2 * p + eo + 1,
                                          qc * 1024:(qc + 1) * 1024],
                                    D_st[2 * qc + eo][:, :]).then_inc(gs.h, 16)

        @blk.tensor
        def _(t):
            for p in range(NP):
                t.wait_ge(ld.h, ld_kv(p))
                for b2 in range(8):
                    if p > 0 and b2 == 0:
                        t.wait_ge(se.h, se_expe(p - 1, 1, 63))
                    if b2 > 0:
                        t.wait_ge(ve.h, ve_vt(p, b2 - 1))
                    half = b2 // 4
                    for i in range(16):
                        tt_ = (b2 % 4) * 16 + i
                        src_ = V_sb[half * 64:half * 64 + 64, tt_ * 128:(tt_ + 1) * 128]
                        idn = ident[0:64, 0:64] if half == 0 else ident[64:128, 64:128]
                        ins = t.transpose(trview[:, i * 64:(i + 1) * 64], src_, idn)
                    ins.then_inc(pe.h, 1)
                for qc in range(2):

                    def st_pair(kt):
                        for half, pst in ((0, ps_ste), (1, ps_sto)):
                            lhsT = K_sb[p % 2][half * 64:half * 64 + 64,
                                               kt * 128:(kt + 1) * 128]
                            for tc in range(2):
                                col = p * OWN + qc * 1024 + tc * 512
                                ins = t.matmul(pst[:, tc * 512:(tc + 1) * 512], lhsT,
                                               Qp[half * 64:half * 64 + 64, col:col + 512],
                                               start=True, stop=True)
                            ins.then_inc(pe.h, 1)

                    def av(kt, half):
                        vt = VT_e if half == 0 else VT_o
                        po = ps_oe if half == 0 else ps_oo
                        ptb = (PT_e if half == 0 else PT_o)[kt % 2]
                        lhsT = vt[:, kt * 80:kt * 80 + 65]
                        for tc in range(2):
                            ins = t.matmul(po[:, tc * 512:(tc + 1) * 512], lhsT,
                                           ptb[:, tc * 512:(tc + 1) * 512],
                                           start=(kt == 0), stop=(kt == 63))
                        ins.then_inc(pe.h, 1)

                    if qc == 0:
                        t.wait_ge(ve.h, ve_vt(p, 7))
                    st_pair(0)
                    for kt in range(64):
                        t.wait_ge(se.h, se_expe(p, qc, kt))
                        if kt == 0:
                            if qc == 1:
                                t.wait_ge(se.h, se_oev(p, 0, 1))
                                t.wait_ge(ve.h, ve_dst(p, 0, 1))
                            elif p > 0:
                                t.wait_ge(se.h, se_oev(p - 1, 1, 1))
                                t.wait_ge(ve.h, ve_dst(p - 1, 1, 1))
                        av(kt, 0)
                        t.wait_ge(se.h, se_expo(p, qc, kt))
                        av(kt, 1)
                        if kt + 1 < 64:
                            st_pair(kt + 1)

        @blk.scalar
        def _(sc):
            for p in range(NP):
                for qc in range(2):
                    for kt in range(64):
                        for eo in range(2):
                            sc.wait_ge(pe.h, (pe_ste if eo == 0 else pe_sto)(p, qc, kt))
                            gstep = (p * 2 + qc) * 64 + kt - 2
                            if gstep >= 0:
                                pp, r = divmod(gstep, 128)
                                qq, kk = divmod(r, 64)
                                sc.wait_ge(pe.h, (pe_ave if eo == 0 else pe_avo)(pp, qq, kk))
                            pst = ps_ste if eo == 0 else ps_sto
                            ptb = (PT_e if eo == 0 else PT_o)[kt % 2]
                            sc.activation(ptb[:, :], pst[:, :], AF.Exp,
                                          bias=EXP_SHIFT, scale=1.0).then_inc(se.h, 1)
                    for eo in range(2):
                        sc.wait_ge(pe.h, (pe_ave if eo == 0 else pe_avo)(p, qc, 63))
                        po = ps_oe if eo == 0 else ps_oo
                        p0 = eo * 64
                        sc.activation(O_all[p0:p0 + 64,
                                            p * OWN + qc * 1024: p * OWN + qc * 1024 + 1024],
                                      po[0:64, :], AF.Identity,
                                      bias=0.0, scale=1.0).then_inc(se.h, 1)

        @blk.vector
        def _(v):
            v.memset(VT_e[:, :].rearrange("p (t c) -> p t c", c=80)[:, :, 64:65], 1.0)
            v.memset(VT_o[:, :].rearrange("p (t c) -> p t c", c=80)[:, :, 64:65], 1.0)
            for p in range(NP):
                for b2 in range(8):
                    v.wait_ge(pe.h, pe_trb(p, b2))
                    vt = VT_e if b2 < 4 else VT_o
                    src = trview[:, :].rearrange("p (t c) -> p t c", c=64)
                    dst = vt[:, :].rearrange("p (t c) -> p t c", c=80)[
                        :, (b2 % 4) * 16:(b2 % 4 + 1) * 16, 0:64]
                    v.tensor_copy(dst, src).then_inc(ve.h, 1)
                for qc in range(2):
                    for eo in range(2):
                        v.wait_ge(pe.h, (pe_ave if eo == 0 else pe_avo)(p, qc, 63))
                        if p >= 1:
                            v.wait_ge(gs.h, gs_d(p - 1, qc, eo))
                        po = ps_oe if eo == 0 else ps_oo
                        with nc.allow_low_precision(reason="softmax denominators in bf16"):
                            v.tensor_copy(D_st[2 * qc + eo][:, :],
                                          po[64:65, :]).then_inc(ve.h, 1)

    ld.n = ld0 + 16 + 16 * 2 * NP
    gs.n = gs0c + 16 * 4 * NP
    pe.n = pe0 + PEP * NP
    ve.n = ve0 + VEP * NP
    se.n = se0 + SEP * NP

    pC2ps.close()
    pCE = ExitStack()
    ps_rb = pCE.enter_context(nc.psum_tensor("ps_rb", [64, OWN], F32))
    pe0e, ve0e = pe.n, ve.n
    with nc.Block("C2E") as blk:
        @blk.vector
        def _(v):
            with nc.allow_low_precision(reason="bf16 reciprocal of softmax denominators"):
                v.reciprocal(D_all[:, :], D_all[:, :]).then_inc(ve.h, 1)
            v.wait_ge(ve.h, ve0e + 1)
            v.engine_nop().then_inc(ve.h, 1)
            for h in range(2 * NP):
                jb, p0 = h // 2, (h % 2) * 64
                v.wait_ge(pe.h, pe0e + h + 1)
                v.tensor_tensor(O_all[p0:p0 + 64, jb * OWN:(jb + 1) * OWN],
                                O_all[p0:p0 + 64, jb * OWN:(jb + 1) * OWN],
                                ps_rb[:, :], ALU.mult).then_inc(ve.h, 1)

        @blk.tensor
        def _(t):
            t.wait_ge(ld.h, ld0 + 16)
            t.wait_ge(ve.h, ve0e + 2)
            for h in range(2 * NP):
                if h >= 1:
                    t.wait_ge(ve.h, ve0e + 2 + h)
                for tc in range(4):
                    ins = t.matmul(ps_rb[:, tc * 512:(tc + 1) * 512],
                                   sel16_sb[:, h * 64:(h + 1) * 64],
                                   D_all[:, tc * 512:(tc + 1) * 512],
                                   start=True, stop=True)
                ins.then_inc(pe.h, 1)
    pe.n = pe0e + 2 * NP
    ve.n = ve0e + 2 + 2 * NP
    pCE.close()
    pC2.close()

    if "O" in taps:
        tap_dump([(dbg["O"], O_all[:, :])])
    if stop_after == "C":
        return finish()

    pD = ExitStack()
    psb = lambda name, shape, dt: pD.enter_context(nc.sbuf_tensor(name, shape, dt))
    fc_sb = psb("fc_sb", [128, 8 * D], BF16)
    g_buf = [psb(f"g_buf{i}", [128, OWN], BF16) for i in range(2)]
    xT_buf = [psb(f"xT_buf{i}", [128, OWN], F32) for i in range(2)]
    s_buf = psb("s_buf", [128, OWN], F32)
    o_stage = [psb(f"o_stage{i}", [128, OWN], F32) for i in range(2)]
    ps_fc = [pD.enter_context(nc.psum_tensor(f"ps_fc{i}", [128, OWN], F32))
             for i in range(2)]

    ld0, gs0, pe0, ve0, sp0, se0 = ld.n, gs.n, pe.n, ve.n, sp.n, se.n
    ve_D = lambda jb: ve0 + 3 * jb + 3

    with nc.Block("D") as blk:
        @blk.gpsimd
        def _(g):
            _wt_dma(g, fc_sb, fc_wT[:, :], 8).then_inc(gs.h, 16)

        @blk.sync
        def _(s):
            for jb in range(8):
                if jb == 0:
                    s.wait_ge(sp.h, SP_G_DONE)
                if jb >= 2:
                    s.wait_ge(ve.h, ve_D(jb - 2))
                s.dma_start(g_buf[jb % 2][:, :],
                            g_hbm[:, jb * OWN:(jb + 1) * OWN]).then_inc(ld.h, 16)
                s.dma_start(xT_buf[jb % 2][:, :],
                            xT_own[jb * 128:(jb + 1) * 128, :]).then_inc(ld.h, 16)

        @blk.tensor
        def _(t):
            t.wait_ge(gs.h, gs0 + 16)
            for jb in range(8):
                if jb >= 2:
                    t.wait_ge(ve.h, ve_D(jb - 2) - 1)
                for j in range(8):
                    lhsT = fc_sb[:, j * D + jb * 128: j * D + (jb + 1) * 128]
                    for tc in range(4):
                        ins = t.matmul(ps_fc[jb % 2][:, tc * 512:(tc + 1) * 512],
                                       lhsT,
                                       O_all[:, j * OWN + tc * 512: j * OWN + (tc + 1) * 512],
                                       start=(j == 0), stop=(j == 7))
                ins.then_inc(pe.h, 1)

        @blk.vector
        def _(v):
            for jb in range(8):
                v.wait_ge(pe.h, pe0 + jb + 1)
                v.wait_ge(ld.h, ld0 + 16 * (2 * jb + 2))
                if jb >= 2:
                    v.wait_ge(sp.h, sp0 + 16 * (jb - 1))
                v.scalar_tensor_tensor(s_buf[:, :], ps_fc[jb % 2][:, :],
                                       fc_b_sb[:, jb:jb + 1],
                                       xn_own[:, jb * OWN:(jb + 1) * OWN],
                                       ALU.add, ALU.add).then_inc(ve.h, 1)
                v.wait_ge(ve.h, ve_D(jb) - 2)
                v.tensor_tensor(s_buf[:, :], s_buf[:, :], g_buf[jb % 2][:, :], ALU.mult
                                ).then_inc(ve.h, 1)
                v.wait_ge(ve.h, ve_D(jb) - 1)
                v.tensor_tensor(o_stage[jb % 2][:, :], s_buf[:, :], xT_buf[jb % 2][:, :],
                                ALU.add).then_inc(ve.h, 1)

        @blk.scalar
        def _(sc):
            for jb in range(8):
                sc.wait_ge(ve.h, ve_D(jb))
                sc.dma_start(out_T[jb * 128:(jb + 1) * 128, :],
                             o_stage[jb % 2][:, :]).then_inc(sp.h, 16)

    ld.n = ld0 + 16 * 16
    gs.n = gs0 + 16
    pe.n = pe0 + 8
    ve.n = ve0 + 24
    sp.n = sp0 + 16 * 8
    pD.close()
    return finish()


def _prep_weights(qkv_w, qkv_bv, fc_w, fc_bv, rt_w, rt_bv, gup, gup_bv, dn, dn_bv):
    qperm = np.concatenate([np.arange(h * 192, h * 192 + 64) for h in range(16)])
    perm = np.concatenate([qperm, qperm + 64, qperm + 128])
    qkv_wT = np.ascontiguousarray(qkv_w[perm].T)
    qkv_b_arr = np.ascontiguousarray(qkv_bv[perm].reshape(24, 128).T)
    fc_wT = np.ascontiguousarray(fc_w.T)
    fc_b_arr = np.ascontiguousarray(fc_bv.reshape(8, 128).T)
    rt_wT = np.ascontiguousarray(rt_w.T)
    rt_b_arr = np.ascontiguousarray(rt_bv.reshape(1, 8))
    iperm = np.empty(4096, dtype=np.int64)
    blk = np.arange(64)
    for b in range(32):
        iperm[b * 128: b * 128 + 64] = b * 64 + blk
        iperm[b * 128 + 64: (b + 1) * 128] = 2048 + b * 64 + blk
    gup_wT = np.ascontiguousarray(gup[:, iperm, :].transpose(0, 2, 1))
    gup_b_arr = np.ascontiguousarray(
        gup_bv[:, iperm].reshape(8, 32, 128).transpose(2, 0, 1).reshape(128, 256))
    dn_wT = np.ascontiguousarray(dn.transpose(0, 2, 1))
    return dict(qkv_wT=qkv_wT, qkv_b=qkv_b_arr, fc_wT=fc_wT, fc_b=fc_b_arr,
                rt_wT=rt_wT, rt_b=rt_b_arr, gup_wT=gup_wT, gup_b=gup_b_arr,
                dn_wT=dn_wT, dn_b=np.ascontiguousarray(dn_bv),
                ident=np.eye(128, dtype=np.float32).astype(ml_dtypes.bfloat16),
                identf=np.eye(128, dtype=np.float32),
                sel=np.repeat(np.eye(8, dtype=np.float32), 128, axis=1
                              ).astype(ml_dtypes.bfloat16),
                sel16=np.repeat(np.eye(16, dtype=np.float32), 64, axis=1
                                ).astype(ml_dtypes.bfloat16))


def make_in_maps(inputs):
    x = np.asarray(inputs["L0_attn_x"], np.float32)
    w = _prep_weights(
        np.asarray(inputs["L0_attn_qkv_w"], np.float32),
        np.asarray(inputs["L0_attn_qkv_b"], np.float32),
        np.asarray(inputs["L0_attn_fc_w"], np.float32),
        np.asarray(inputs["L0_attn_fc_b"], np.float32),
        np.asarray(inputs["L0_ffn_router_weight"], np.float32),
        np.asarray(inputs["L0_ffn_router_bias"], np.float32),
        np.asarray(inputs["L0_ffn_gate_up_proj"], np.float32),
        np.asarray(inputs["L0_ffn_gate_up_proj_bias"], np.float32),
        np.asarray(inputs["L0_ffn_down_proj"], np.float32),
        np.asarray(inputs["L0_ffn_down_proj_bias"], np.float32))
    in_maps = []
    for c in range(8):
        b, q0 = c // 4, (c % 4) * OWN
        xb = x[b]
        x_rot = np.ascontiguousarray(np.concatenate([xb[q0:], xb[:q0]], axis=0))
        m = dict(w)
        m["x_full"] = x_rot
        m["xT_own"] = np.ascontiguousarray(x_rot[:OWN].T)
        in_maps.append(m)
    return in_maps


def kernel(**inputs):
    in_maps = make_in_maps(inputs)
    nc, _ = build_program()
    res = run_bass_kernel_spmd(nc, in_maps, core_ids=list(range(8)))
    out = np.empty((2, SEQ, D), np.float32)
    for c in range(8):
        b, q0 = c // 4, (c % 4) * OWN
        out[b, q0:q0 + OWN, :] = np.ascontiguousarray(res.results[c]["out_T"]).T
    return out


# version 09
# speedup vs baseline: 1.0671x; 1.0671x over previous
import math
import numpy as np
import ml_dtypes
from contextlib import ExitStack

import concourse.bass as bass
import concourse.mybir as mybir
from concourse.bass_utils import run_bass_kernel_spmd

F32 = mybir.dt.float32
BF16 = mybir.dt.bfloat16
AF = mybir.ActivationFunctionType
ALU = mybir.AluOpType
AX = mybir.AxisListType

D = 1024
SEQ = 8192
OWN = 2048
H = 16
NE = 8
NTT = SEQ // 128
EXP_SHIFT = -20.0
SILU_CLIP = 11.914 / (1.0 + math.exp(-11.914))
INV_1702 = 1.0 / 1.702
LN_EPS = 1e-5
C2_HEADS = H



def _wt_dma(eng, dst_sb, src_ap, nj):
    src3 = src_ap.rearrange("(j p) c -> p j c", p=128)
    dst3 = dst_sb[:, :].rearrange("p (j c) -> p j c", j=nj)
    return eng.dma_start(dst3, src3)

class Sem:
    def __init__(self, nc, es, name):
        self.h = es.enter_context(nc.semaphore(name))
        self.n = 0


def build_program(stop_after=None, taps=()):
    nc = bass.Bass("TRN2", target_bir_lowering=False)
    es = ExitStack()

    def din(name, shape, dt=F32):
        return nc.dram_tensor(name, shape, dt, kind="ExternalInput").ap()

    x_full = din("x_full", [SEQ, D])
    xT_own = din("xT_own", [D, OWN])
    qkv_wT = din("qkv_wT", [D, 3 * D])
    qkv_b = din("qkv_b", [128, 24])
    fc_wT = din("fc_wT", [D, D])
    fc_b = din("fc_b", [128, 8])
    rt_wT = din("rt_wT", [D, NE])
    rt_b = din("rt_b", [1, NE])
    gup_wT = din("gup_wT", [NE, D, 4096])
    gup_b = din("gup_b", [128, NE * 32])
    dn_wT = din("dn_wT", [NE, 2048, D])
    dn_b = din("dn_b", [NE, D])
    ident_in = din("ident", [128, 128], BF16)
    identf_in = din("identf", [128, 128], F32)
    sel_in = din("sel", [NE, NE * 128], BF16)
    sel16_in = din("sel16", [16, 16 * 64], BF16)

    out_T = nc.dram_tensor("out_T", [D, OWN], F32, kind="ExternalOutput").ap()

    def scratch(name, shape, dt, tap):
        kind = "ExternalOutput" if tap in taps else "Internal"
        return nc.dram_tensor(name, shape, dt, kind=kind).ap()

    K_hbm = scratch("K_hbm", [D, SEQ], BF16, "KV")
    V_hbm = scratch("V_hbm", [D, SEQ], BF16, "KV")
    g_hbm = scratch("g_hbm", [128, 8 * OWN], BF16, "g")
    dbg = {}
    for t, shape, dt in [("xn_own", [128, 8 * OWN], BF16), ("logits", [128, 128], F32),
                         ("comb", [8, OWN], BF16), ("y", [128, 8 * OWN], F32),
                         ("Q", [128, 8 * OWN], BF16), ("O", [128, 8 * OWN], BF16)]:
        if t in taps:
            dbg[t] = nc.dram_tensor("dbg_" + t, shape, dt, kind="ExternalOutput").ap()

    ld = Sem(nc, es, "ld")
    ldx = Sem(nc, es, "ldx")
    gs = Sem(nc, es, "gs")
    sp = Sem(nc, es, "sp")
    pe = Sem(nc, es, "pe")
    ve = Sem(nc, es, "ve")
    se = Sem(nc, es, "se")

    _tapn = [0]
    def tap_dump(name_pairs):
        _tapn[0] += 1
        with nc.Block(f"dbg{_tapn[0]}") as blk:
            @blk.sync
            def _(s):
                n = ld.n
                for dst, src in name_pairs:
                    s.dma_start(dst, src).then_inc(ld.h, 16)
                    n += 16
                s.wait_ge(ld.h, n)
        ld.n += 16 * len(name_pairs)

    def finish():
        with nc.Block("fin") as blk:
            @blk.sync
            def _(s):
                s.wait_ge(sp.h, sp.n)
        es.close()
        return nc, dbg

    sbp = lambda name, shape, dt: es.enter_context(nc.sbuf_tensor(name, shape, dt))
    qkv_b_sb = sbp("qkv_b_sb", [128, 24], F32)
    fc_b_sb = sbp("fc_b_sb", [128, 8], F32)
    rt_b_sb = sbp("rt_b_sb", [1, NE], BF16)
    gup_b_sb = sbp("gup_b_sb", [128, NE * 32], F32)
    dn_b_sb = sbp("dn_b_sb", [8, D], BF16)
    ident = sbp("ident_sb", [128, 128], BF16)
    identf = sbp("identf_sb", [128, 128], F32)
    sel_sb = sbp("sel_sb", [NE, NE * 128], BF16)
    ones1_bf = sbp("ones1_bf", [1, 128], BF16)
    ones1_f = sbp("ones1_f", [1, 128], F32)
    qbs = sbp("qbs", [128, 8], F32)
    xn_own = sbp("xn_own", [128, 8 * OWN], BF16)
    c_eps = sbp("c_eps", [128, 1], F32)
    c_shift = sbp("c_shift", [128, 1], F32)
    nc.const_aps.register(F32, LN_EPS, c_eps[:, 0:1])
    nc.const_aps.register(F32, EXP_SHIFT, c_shift[:, 0:1])

    pA = ExitStack()
    psb = lambda name, shape, dt: pA.enter_context(nc.sbuf_tensor(name, shape, dt))
    pps = lambda name, shape, dt: pA.enter_context(nc.psum_tensor(name, shape, dt))
    xnT = psb("xnT", [128, 8 * SEQ], BF16)
    x_buf = [psb(f"x_buf{i}", [128, D], F32) for i in range(2)]
    sq_buf = psb("sq_buf", [128, D], F32)
    xn_buf = [psb(f"xn_buf{i}", [128, D], BF16) for i in range(2)]
    st_s1 = psb("st_s1", [128, 1], F32)
    st_s2 = psb("st_s2", [128, 1], F32)
    st_m = psb("st_m", [128, 1], F32)
    st_v = psb("st_v", [128, 1], F32)
    st_msq = psb("st_msq", [128, 1], F32)
    st_sd = psb("st_sd", [128, 1], F32)
    st_rstd = psb("st_rstd", [128, 1], F32)
    st_mrn = psb("st_mrn", [128, 1], F32)
    pA1ps = ExitStack()
    ps_tr = [pA1ps.enter_context(nc.psum_tensor(f"ps_tr{i}", [128, D], BF16)) for i in range(2)]

    ld0, pe0, ve0, se0 = ld.n, pe.n, ve.n, se.n
    NSMALL = 6
    ldx0 = ldx.n
    ldv = lambda tt: ldx0 + 16 * (tt + 1)
    ve_sq = lambda tt: ve0 + 7 * tt + 1
    ve_red = lambda tt: ve0 + 7 * tt + 2
    ve_msq = lambda tt: ve0 + 7 * tt + 3
    ve_st1 = lambda tt: ve0 + 7 * tt + 4
    ve_rcp = lambda tt: ve0 + 7 * tt + 5
    ve_st2 = lambda tt: ve0 + 7 * tt + 6
    ve_evac = lambda tt: ve0 + 7 * tt + 7
    se_sd = lambda tt: se0 + 2 * tt + 1
    se_xn = lambda tt: se0 + 2 * tt + 2
    pe_tr = lambda tt: pe0 + tt + 1

    with nc.Block("A1") as blk:
        @blk.sync
        def _(s):
            s.dma_start(qkv_b_sb[:, :], qkv_b).then_inc(ld.h, 16)
            s.dma_start(fc_b_sb[:, :], fc_b).then_inc(ld.h, 16)
            s.dma_start(gup_b_sb[:, :], gup_b).then_inc(ld.h, 16)
            s.dma_start(ident[:, :], ident_in).then_inc(ld.h, 16)
            s.dma_start(identf[:, :], identf_in).then_inc(ld.h, 16)
            s.dma_start(sel_sb[:, :], sel_in).then_inc(ld.h, 16)
            for tt in range(NTT):
                if tt >= 2:
                    s.wait_ge(se.h, se_xn(tt - 2))
                s.dma_start(x_buf[tt % 2][:, :],
                            x_full[tt * 128:(tt + 1) * 128, :]).then_inc(ldx.h, 16)

        @blk.gpsimd
        def _(g):
            g.dma_start(rt_b_sb[:, :], rt_b).then_inc(gs.h, 16)
            g.dma_start(dn_b_sb[:, :], dn_b).then_inc(gs.h, 16)

        @blk.vector
        def _(v):
            v.memset(ones1_bf[:, :], 1.0)
            v.memset(ones1_f[:, :], 1.0)
            v.memset(c_eps[:, :], LN_EPS)
            v.memset(c_shift[:, :], EXP_SHIFT)
            v.wait_ge(ld.h, ld0 + 16 * NSMALL)
            v.tensor_scalar(qbs[:, :], qkv_b_sb[:, 0:8], 0.125, None, ALU.mult)
            for tt in range(NTT):
                x = x_buf[tt % 2]
                v.wait_ge(ldx.h, ldv(tt))
                v.tensor_tensor(sq_buf[:, :], x[:, :], x[:, :], ALU.mult).then_inc(ve.h, 1)
                v.reduce_sum(st_s1[:, :], x[:, :], axis=AX.X)
                v.wait_ge(ve.h, ve_sq(tt))
                v.reduce_sum(st_s2[:, :], sq_buf[:, :], axis=AX.X).then_inc(ve.h, 1)
                v.wait_ge(ve.h, ve_red(tt))
                v.tensor_scalar(st_m[:, :], st_s1[:, :], 1.0 / D, None, ALU.mult)
                v.tensor_scalar(st_v[:, :], st_s2[:, :], 1.0 / D, None, ALU.mult)
                v.tensor_tensor(st_msq[:, :], st_m[:, :], st_m[:, :],
                                ALU.mult).then_inc(ve.h, 1)
                v.wait_ge(ve.h, ve_msq(tt))
                v.tensor_tensor(st_v[:, :], st_v[:, :], st_msq[:, :],
                                ALU.subtract).then_inc(ve.h, 1)
                v.wait_ge(se.h, se_sd(tt))
                v.reciprocal(st_rstd[:, :], st_sd[:, :]).then_inc(ve.h, 1)
                v.wait_ge(ve.h, ve_rcp(tt))
                v.scalar_tensor_tensor(st_mrn[:, :], st_m[:, :], -1.0, st_rstd[:, :],
                                       ALU.mult, ALU.mult).then_inc(ve.h, 1)
                v.wait_ge(pe.h, pe_tr(tt))
                src = ps_tr[tt % 2][:, :].rearrange("p (j t) -> p j t", j=8)
                dst = xnT[:, :].rearrange("p (j t) -> p j t", j=8)[:, :, tt * 128:(tt + 1) * 128]
                v.tensor_copy(dst, src).then_inc(ve.h, 1)

        @blk.scalar
        def _(sc):
            for tt in range(NTT):
                sc.wait_ge(ve.h, ve_st1(tt))
                sc.activation(st_sd[:, :], st_v[:, :], AF.Sqrt, bias=LN_EPS,
                              scale=1.0).then_inc(se.h, 1)
                sc.wait_ge(ve.h, ve_st2(tt))
                if tt >= 2:
                    sc.wait_ge(pe.h, pe_tr(tt - 2))
                sc.activation(xn_buf[tt % 2][:, :], x_buf[tt % 2][:, :], AF.Identity,
                              bias=st_mrn[:, 0:1], scale=st_rstd[:, 0:1]
                              ).then_inc(se.h, 1)

        @blk.tensor
        def _(t):
            for tt in range(NTT):
                t.wait_ge(se.h, se_xn(tt))
                if tt >= 2:
                    t.wait_ge(ve.h, ve_evac(tt - 2))
                for j in range(8):
                    ins = t.transpose(ps_tr[tt % 2][:, j * 128:(j + 1) * 128],
                                      xn_buf[tt % 2][:, j * 128:(j + 1) * 128],
                                      ident[:, :])
                ins.then_inc(pe.h, 1)

    ld.n = ld0 + 16 * NSMALL
    ldx.n = ldx0 + 16 * NTT
    gs.n += 32
    ve.n = ve0 + 7 * NTT
    se.n = se0 + 2 * NTT
    pe.n = pe0 + NTT
    pA1ps.close()

    pA2 = ExitStack()
    wkv_sb = pA2.enter_context(nc.sbuf_tensor("wkv_sb", [128, 8 * D], BF16))
    kv_stage = [pA2.enter_context(nc.sbuf_tensor(f"kv_stage{i}", [128, 2048], BF16))
                for i in range(2)]
    ps_kv = [pA2.enter_context(nc.psum_tensor(f"ps_kv{i}", [128, 2048], F32))
             for i in range(2)]

    gs0, pe0, se0, sp0, ve0 = gs.n, pe.n, se.n, sp.n, ve.n
    NG = 32
    pe_g = lambda g_: pe0 + g_ + 1
    se_g = lambda g_: se0 + g_ + 1
    sp_g = lambda g_: sp0 + 16 * (g_ + 1)

    with nc.Block("A2") as blk:
        @blk.vector
        def _(v):
            src = xnT[:, :].rearrange("p (j t) -> p j t", j=8)[:, :, 0:OWN]
            dst = xn_own[:, :].rearrange("p (j t) -> p j t", j=8)
            v.tensor_copy(dst, src).then_inc(ve.h, 1)

        @blk.gpsimd
        def _(g):
            _wt_dma(g, wkv_sb, qkv_wT[:, D:2 * D], 8).then_inc(gs.h, 16)
            g.wait_ge(pe.h, pe_g(NG - 1))
            _wt_dma(g, wkv_sb, qkv_wT[:, 2 * D:3 * D], 8).then_inc(gs.h, 16)

        @blk.tensor
        def _(t):
            for gi in range(2 * NG):
                proj, jb, tcg = gi // NG, (gi % NG) // 4, (gi % NG) % 4
                t.wait_ge(gs.h, gs0 + 16 * (proj + 1))
                if gi >= 2:
                    t.wait_ge(se.h, se_g(gi - 2))
                for j in range(8):
                    lhsT = wkv_sb[:, j * D + jb * 128: j * D + (jb + 1) * 128]
                    for tc in range(4):
                        col = tcg * 2048 + tc * 512
                        ins = t.matmul(ps_kv[gi % 2][:, tc * 512:(tc + 1) * 512],
                                       lhsT, xnT[:, j * SEQ + col: j * SEQ + col + 512],
                                       start=(j == 0), stop=(j == 7))
                ins.then_inc(pe.h, 1)

        @blk.scalar
        def _(sc):
            for gi in range(2 * NG):
                proj, jb, tcg = gi // NG, (gi % NG) // 4, (gi % NG) % 4
                sc.wait_ge(pe.h, pe_g(gi))
                if gi >= 2:
                    sc.wait_ge(sp.h, sp_g(gi - 2))
                bias_col = 8 * (proj + 1) + jb
                sc.activation(kv_stage[gi % 2][:, :], ps_kv[gi % 2][:, :], AF.Identity,
                              bias=qkv_b_sb[:, bias_col:bias_col + 1],
                              scale=1.0).then_inc(se.h, 1)
                sc.wait_ge(se.h, se_g(gi))
                tgt = K_hbm if proj == 0 else V_hbm
                sc.dma_start(tgt[jb * 128:(jb + 1) * 128, tcg * 2048:(tcg + 1) * 2048],
                             kv_stage[gi % 2][:, :]).then_inc(sp.h, 16)

    gs.n = gs0 + 32
    ve.n = ve0 + 1
    pe.n = pe0 + 2 * NG
    se.n = se0 + 2 * NG
    sp.n = sp0 + 16 * 2 * NG
    SP_KV_DONE = sp.n
    pA2.close()
    pA.close()

    if "xn_own" in taps:
        tap_dump([(dbg["xn_own"], xn_own[:, :])])
    if stop_after == "A":
        return finish()

    pB = ExitStack()
    psb = lambda name, shape, dt: pB.enter_context(nc.sbuf_tensor(name, shape, dt))
    y_sb = psb("y_sb", [128, 8 * OWN], F32)
    A_sb = psb("A_sb", [128, 16 * OWN], BF16)
    gup_buf = [psb(f"gup_buf{i}", [128, 8 * 256], BF16) for i in range(2)]
    dn_buf = [psb(f"dn_buf{i}", [128, 16 * 128], BF16) for i in range(2)]
    gsl_buf = [psb(f"gsl_buf{i}", [64, OWN], BF16) for i in range(2)]
    uc_buf = psb("uc_buf", [64, OWN], BF16)
    combB = psb("combB", [128, OWN], BF16)
    combT = psb("combT", [8, OWN], BF16)
    rt_w_sb = psb("rt_w_sb", [128, 8 * NE], BF16)
    logits_sb = psb("logits_sb", [128, 128], F32)
    lwork = psb("lwork", [128, 8], F32)
    m4 = psb("m4", [128, 4], F32)
    mtmp = psb("mtmp", [128, 8], F32)
    exb = psb("exb", [128, 8], F32)
    negm = psb("negm", [128, 1], F32)
    den = psb("den", [128, 1], F32)
    rden = psb("rden", [128, 1], F32)
    bsig = psb("bsig", [64, 32], F32)

    pB1ps = ExitStack()
    ps_r = pB1ps.enter_context(nc.psum_tensor("ps_r", [128, 128], F32))
    ps_ct = pB1ps.enter_context(nc.psum_tensor("ps_ct", [8, 512], F32))
    ps_yi = pB1ps.enter_context(nc.psum_tensor("ps_yi", [128, OWN], F32))

    gs0, pe0, ve0, se0 = gs.n, pe.n, ve.n, se.n
    VB = 19
    tb_base = lambda tb: ve0 + 1 + VB * tb
    ve_negm = lambda tb: tb_base(tb) + 15
    ve_exm = lambda tb: tb_base(tb) + 16
    ve_comb_all = ve0 + 1 + VB * 16
    ve_ct = lambda tbg: ve_comb_all + tbg + 1
    ve_combT = ve_comb_all + 4
    ve_yev = lambda db: ve_combT + db + 1
    se_exp = lambda tb: se0 + tb + 1
    pe_ct = lambda tbg: pe0 + 1 + tbg + 1
    pe_yi = lambda db: pe0 + 5 + db + 1

    with nc.Block("B1") as blk:
        @blk.gpsimd
        def _(g):
            _wt_dma(g, rt_w_sb, rt_wT[:, :], 8).then_inc(gs.h, 16)

        @blk.tensor
        def _(t):
            t.wait_ge(gs.h, gs0 + 16)
            for tb in range(16):
                t.matmul(ps_r[:, tb * 8:(tb + 1) * 8], ones1_bf[:, :], rt_b_sb[:, :],
                         start=True, stop=False)
                for j in range(8):
                    ins = t.matmul(ps_r[:, tb * 8:(tb + 1) * 8],
                                   xn_own[:, j * OWN + tb * 128: j * OWN + (tb + 1) * 128],
                                   rt_w_sb[:, j * NE:(j + 1) * NE],
                                   start=False, stop=(j == 7))
            ins.then_inc(pe.h, 1)
            t.wait_ge(ve.h, ve_comb_all)
            for tbg in range(4):
                if tbg > 0:
                    t.wait_ge(ve.h, ve_ct(tbg - 1))
                for i in range(4):
                    tb = tbg * 4 + i
                    ins = t.transpose(ps_ct[:, i * 128:(i + 1) * 128],
                                      logits_sb[:, tb * 8:(tb + 1) * 8], identf[:, :])
                ins.then_inc(pe.h, 1)
            t.wait_ge(ve.h, ve_combT)
            for db in range(8):
                if db >= 1:
                    t.wait_ge(ve.h, ve_yev(db - 1))
                for tc in range(4):
                    ins = t.matmul(ps_yi[:, tc * 512:(tc + 1) * 512],
                                   dn_b_sb[:, db * 128:(db + 1) * 128],
                                   combT[:, tc * 512:(tc + 1) * 512],
                                   start=True, stop=True)
                ins.then_inc(pe.h, 1)

        @blk.vector
        def _(v):
            v.wait_ge(pe.h, pe0 + 1)
            v.tensor_copy(logits_sb[:, :], ps_r[:, :]).then_inc(ve.h, 1)
            for tb in range(16):
                lg = logits_sb[:, tb * 8:(tb + 1) * 8]
                v.tensor_copy(lwork[:, :], lg).then_inc(ve.h, 1)
                v.wait_ge(ve.h, tb_base(tb) + 1)
                for i in range(4):
                    v.reduce_max(m4[:, i:i + 1], lwork[:, :], axis=AX.X).then_inc(ve.h, 1)
                    v.wait_ge(ve.h, tb_base(tb) + 2 + 3 * i)
                    v.tensor_scalar(mtmp[:, :], lwork[:, :], m4[:, i:i + 1], -2e30,
                                    ALU.is_ge, ALU.mult).then_inc(ve.h, 1)
                    v.wait_ge(ve.h, tb_base(tb) + 3 + 3 * i)
                    v.tensor_tensor(lwork[:, :], lwork[:, :], mtmp[:, :],
                                    ALU.add).then_inc(ve.h, 1)
                    v.wait_ge(ve.h, tb_base(tb) + 4 + 3 * i)
                v.tensor_scalar(mtmp[:, :], lg, m4[:, 3:4], None,
                                ALU.is_ge).then_inc(ve.h, 1)
                v.tensor_scalar(negm[:, :], m4[:, 0:1], -1.0, None,
                                ALU.mult).then_inc(ve.h, 1)
                v.wait_ge(se.h, se_exp(tb))
                v.tensor_tensor(exb[:, :], exb[:, :], mtmp[:, :],
                                ALU.mult).then_inc(ve.h, 1)
                v.wait_ge(ve.h, tb_base(tb) + 16)
                v.reduce_sum(den[:, :], exb[:, :], axis=AX.X).then_inc(ve.h, 1)
                v.wait_ge(ve.h, tb_base(tb) + 17)
                v.reciprocal(rden[:, :], den[:, :]).then_inc(ve.h, 1)
                v.wait_ge(ve.h, tb_base(tb) + 18)
                v.tensor_scalar(lg, exb[:, :], rden[:, 0:1], None,
                                ALU.mult).then_inc(ve.h, 1)
            for tbg in range(4):
                v.wait_ge(pe.h, pe_ct(tbg))
                v.tensor_copy(combT[:, tbg * 512:(tbg + 1) * 512],
                              ps_ct[:, :]).then_inc(ve.h, 1)
            for db in range(8):
                v.wait_ge(pe.h, pe_yi(db))
                v.tensor_copy(y_sb[:, db * OWN:(db + 1) * OWN],
                              ps_yi[:, :]).then_inc(ve.h, 1)

        @blk.scalar
        def _(sc):
            for tb in range(16):
                sc.wait_ge(ve.h, ve_negm(tb))
                if tb >= 1:
                    sc.wait_ge(ve.h, ve_exm(tb - 1))
                sc.activation(exb[:, :], logits_sb[:, tb * 8:(tb + 1) * 8], AF.Exp,
                              bias=negm[:, 0:1], scale=1.0).then_inc(se.h, 1)

    gs.n = gs0 + 16
    pe.n = pe0 + 5 + 8
    ve.n = ve_yev(7)
    se.n = se0 + 16
    pB1ps.close()

    pairs = []
    if "logits" in taps:
        pairs.append((dbg["logits"], logits_sb[:, :]))
    if "comb" in taps:
        pairs.append((dbg["comb"], combT[:, :]))
    if pairs:
        tap_dump(pairs)

    pB2ps = ExitStack()
    ps_gu = [pB2ps.enter_context(nc.psum_tensor(f"ps_gu{i}", [128, OWN], F32))
             for i in range(2)]

    gs0, pe0, ve0, se0 = gs.n, pe.n, ve.n, se.n
    PEE = 41
    pe_bcast = lambda e: pe0 + PEE * e + 1
    pe_gup = lambda e, b: pe0 + PEE * e + 1 + (b + 1)
    pe_dn = lambda e, db: pe0 + PEE * e + 33 + (db + 1)
    VEE = 138
    ve_combB = lambda e: ve0 + VEE * e + 1
    ve_bsig = lambda e: ve0 + VEE * e + 2
    ve_act = lambda e, b: ve0 + VEE * e + 2 + 4 * (b + 1)
    ve_yacc = lambda e, db: ve0 + VEE * e + 130 + (db + 1)
    se_gs = lambda e, b: se0 + 32 * e + b + 1
    gs_gup = lambda e, ch: gs0 + 16 * (24 * e + ch + 1)
    gs_dn = lambda e, db: gs0 + 16 * (24 * e + 16 + db + 1)

    with nc.Block("B2") as blk:
        @blk.gpsimd
        def _(g):
            for e in range(NE):
                for ch in range(16):
                    pv = (e, ch - 2) if ch >= 2 else (e - 1, ch + 14)
                    if pv[0] >= 0:
                        g.wait_ge(pe.h, pe_gup(pv[0], pv[1] * 2 + 1))
                    _wt_dma(g, gup_buf[ch % 2], gup_wT[e, :, ch * 256:(ch + 1) * 256],
                            8).then_inc(gs.h, 16)
                for db in range(8):
                    pv = (e, db - 2) if db >= 2 else (e - 1, db + 6)
                    if pv[0] >= 0:
                        g.wait_ge(pe.h, pe_dn(pv[0], pv[1]))
                    _wt_dma(g, dn_buf[db % 2], dn_wT[e, :, db * 128:(db + 1) * 128],
                            16).then_inc(gs.h, 16)

        @blk.tensor
        def _(t):
            for e in range(NE):
                if e > 0:
                    t.wait_ge(ve.h, ve_yacc(e - 1, 7))
                for tc in range(4):
                    ins = t.matmul(ps_gu[0][:, tc * 512:(tc + 1) * 512],
                                   sel_sb[:, e * 128:(e + 1) * 128],
                                   combT[:, tc * 512:(tc + 1) * 512],
                                   start=True, stop=True)
                ins.then_inc(pe.h, 1)
                t.wait_ge(ve.h, ve_combB(e))
                for b in range(32):
                    ch = b // 2
                    t.wait_ge(gs.h, gs_gup(e, ch))
                    if b >= 2:
                        t.wait_ge(ve.h, ve_act(e, b - 2))
                    for j in range(8):
                        lhsT = gup_buf[ch % 2][:, j * 256 + (b % 2) * 128:
                                               j * 256 + (b % 2) * 128 + 128]
                        for tc in range(4):
                            ins = t.matmul(ps_gu[b % 2][:, tc * 512:(tc + 1) * 512],
                                           lhsT,
                                           xn_own[:, j * OWN + tc * 512: j * OWN + (tc + 1) * 512],
                                           start=(j == 0), stop=(j == 7))
                    ins.then_inc(pe.h, 1)
                t.wait_ge(ve.h, ve_act(e, 31))
                for db in range(8):
                    t.wait_ge(gs.h, gs_dn(e, db))
                    if db >= 2:
                        t.wait_ge(ve.h, ve_yacc(e, db - 2))
                    for ic in range(16):
                        lhsT = dn_buf[db % 2][:, ic * 128:(ic + 1) * 128]
                        for tc in range(4):
                            ins = t.matmul(ps_gu[db % 2][:, tc * 512:(tc + 1) * 512],
                                           lhsT,
                                           A_sb[:, ic * OWN + tc * 512: ic * OWN + (tc + 1) * 512],
                                           start=(ic == 0), stop=(ic == 15))
                    ins.then_inc(pe.h, 1)

        @blk.scalar
        def _(sc):
            for e in range(NE):
                sc.wait_ge(ve.h, ve_bsig(e))
                for b in range(32):
                    sc.wait_ge(pe.h, pe_gup(e, b))
                    if b >= 2:
                        sc.wait_ge(ve.h, ve_act(e, b - 2))
                    sc.activation(gsl_buf[b % 2][:, :], ps_gu[b % 2][0:64, :], AF.Silu,
                                  bias=bsig[:, b:b + 1], scale=1.702).then_inc(se.h, 1)

        @blk.vector
        def _(v):
            for e in range(NE):
                v.wait_ge(pe.h, pe_bcast(e))
                v.tensor_scalar(combB[:, :], ps_gu[0][:, :], INV_1702, None,
                                ALU.mult).then_inc(ve.h, 1)
                v.tensor_scalar(bsig[:, :], gup_b_sb[0:64, e * 32:(e + 1) * 32], 1.702,
                                None, ALU.mult).then_inc(ve.h, 1)
                for b in range(32):
                    jj = b // 2
                    p0 = (b % 2) * 64
                    v.wait_ge(pe.h, pe_gup(e, b))
                    v.wait_ge(se.h, se_gs(e, b))
                    v.tensor_scalar(uc_buf[:, :], ps_gu[b % 2][64:128, :],
                                    gup_b_sb[64:128, e * 32 + b:e * 32 + b + 1], 7.0,
                                    ALU.add, ALU.min).then_inc(ve.h, 1)
                    v.wait_ge(ve.h, ve_act(e, b) - 3)
                    v.tensor_scalar(uc_buf[:, :], uc_buf[:, :], -7.0, None, ALU.max)
                    v.tensor_scalar(gsl_buf[b % 2][:, :], gsl_buf[b % 2][:, :],
                                    SILU_CLIP, None, ALU.min).then_inc(ve.h, 1)
                    v.wait_ge(ve.h, ve_act(e, b) - 2)
                    v.tensor_tensor(gsl_buf[b % 2][:, :], gsl_buf[b % 2][:, :],
                                    combB[0:64, :], ALU.mult).then_inc(ve.h, 1)
                    v.wait_ge(ve.h, ve_act(e, b) - 1)
                    v.scalar_tensor_tensor(A_sb[p0:p0 + 64, jj * OWN:(jj + 1) * OWN],
                                           uc_buf[:, :], 1.0, gsl_buf[b % 2][:, :],
                                           ALU.add, ALU.mult).then_inc(ve.h, 1)
                for db in range(8):
                    v.wait_ge(pe.h, pe_dn(e, db))
                    v.tensor_tensor(y_sb[:, db * OWN:(db + 1) * OWN],
                                    y_sb[:, db * OWN:(db + 1) * OWN],
                                    ps_gu[db % 2][:, :], ALU.add).then_inc(ve.h, 1)

    gs.n = gs0 + 16 * 24 * NE
    pe.n = pe0 + PEE * NE
    ve.n = ve0 + VEE * NE
    se.n = se0 + 32 * NE

    se0, sp0 = se.n, sp.n
    with nc.Block("Bg") as blk:
        @blk.scalar
        def _(sc):
            for jb in range(8):
                if jb >= 1:
                    sc.wait_ge(sp.h, sp0 + 32 * jb)
                sc.activation(gsl_buf[jb % 2][:, :], y_sb[0:64, jb * OWN:(jb + 1) * OWN],
                              AF.Silu, bias=0.0, scale=1.0)
                sc.activation(uc_buf[:, :], y_sb[64:128, jb * OWN:(jb + 1) * OWN],
                              AF.Silu, bias=0.0, scale=1.0).then_inc(se.h, 1)
                sc.wait_ge(se.h, se0 + jb + 1)
                sc.dma_start(g_hbm[0:64, jb * OWN:(jb + 1) * OWN],
                             gsl_buf[jb % 2][:, :]).then_inc(sp.h, 16)
                sc.dma_start(g_hbm[64:128, jb * OWN:(jb + 1) * OWN],
                             uc_buf[:, :]).then_inc(sp.h, 16)
    se.n = se0 + 8
    sp.n = sp0 + 16 * 16
    SP_G_DONE = sp.n

    if "y" in taps:
        tap_dump([(dbg["y"], y_sb[:, :])])
    pB2ps.close()
    pB.close()
    if stop_after == "B":
        return finish()

    Qp = es.enter_context(nc.sbuf_tensor("Qp", [128, 8 * OWN], BF16))
    O_all = es.enter_context(nc.sbuf_tensor("O_all", [128, 8 * OWN], BF16))

    pC1 = ExitStack()
    wq_sb = pC1.enter_context(nc.sbuf_tensor("wq_sb", [128, 8 * D], BF16))
    ps_q = [pC1.enter_context(nc.psum_tensor(f"ps_q{i}", [128, OWN], F32))
            for i in range(2)]

    gs0, pe0, se0 = gs.n, pe.n, se.n
    with nc.Block("C1") as blk:
        @blk.gpsimd
        def _(g):
            _wt_dma(g, wq_sb, qkv_wT[:, 0:D], 8).then_inc(gs.h, 16)

        @blk.tensor
        def _(t):
            t.wait_ge(gs.h, gs0 + 16)
            for jb in range(8):
                if jb >= 2:
                    t.wait_ge(se.h, se0 + jb - 1)
                for j in range(8):
                    lhsT = wq_sb[:, j * D + jb * 128: j * D + (jb + 1) * 128]
                    for tc in range(4):
                        ins = t.matmul(ps_q[jb % 2][:, tc * 512:(tc + 1) * 512],
                                       lhsT,
                                       xn_own[:, j * OWN + tc * 512: j * OWN + (tc + 1) * 512],
                                       start=(j == 0), stop=(j == 7))
                ins.then_inc(pe.h, 1)

        @blk.scalar
        def _(sc):
            for jb in range(8):
                sc.wait_ge(pe.h, pe0 + jb + 1)
                sc.activation(Qp[:, jb * OWN:(jb + 1) * OWN], ps_q[jb % 2][:, :],
                              AF.Identity, bias=qbs[:, jb:jb + 1],
                              scale=0.125).then_inc(se.h, 1)
    gs.n = gs0 + 16
    pe.n = pe0 + 8
    se.n = se0 + 8
    pC1.close()

    if "Q" in taps:
        tap_dump([(dbg["Q"], Qp[:, :])])
    if stop_after == "C1":
        return finish()

    pC2 = ExitStack()
    psb = lambda name, shape, dt: pC2.enter_context(nc.sbuf_tensor(name, shape, dt))
    K_sb = [psb(f"K_sb{i}", [128, SEQ], BF16) for i in range(2)]
    V_sb = psb("V_sb", [128, SEQ], BF16)
    VT_e = psb("VT_e", [128, 64 * 80], BF16)
    VT_o = psb("VT_o", [128, 64 * 80], BF16)
    PT_e = [psb(f"PT_e{i}", [128, 1024], BF16) for i in range(2)]
    PT_o = [psb(f"PT_o{i}", [128, 1024], BF16) for i in range(2)]
    D_all = psb("D_all", [16, OWN], BF16)
    D_st = [psb(f"D_st{i}", [1, 1024], BF16) for i in range(4)]
    sel16_sb = psb("sel16_sb", [16, 16 * 64], BF16)
    pC2ps = ExitStack()
    ps_oe = pC2ps.enter_context(nc.psum_tensor("ps_oe", [65, 1024], F32))
    ps_oo = pC2ps.enter_context(nc.psum_tensor("ps_oo", [65, 1024], F32))
    ps_ste = pC2ps.enter_context(nc.psum_tensor("ps_ste", [128, 1024], F32))
    ps_sto = pC2ps.enter_context(nc.psum_tensor("ps_sto", [128, 1024], F32))
    trview = ps_ste[:, 0:512].bitcast(BF16)

    ld0, pe0, ve0, se0, gs0c = ld.n, pe.n, ve.n, se.n, gs.n
    NP = C2_HEADS // 2
    PEP = 8 + 2 * 256
    qcb = lambda qc: 8 + qc * 256
    pe_trb = lambda p, b2: pe0 + PEP * p + b2 + 1
    pe_ste = lambda p, qc, kt: pe0 + PEP * p + qcb(qc) + (1 if kt == 0 else 4 * kt)
    pe_sto = lambda p, qc, kt: pe0 + PEP * p + qcb(qc) + (2 if kt == 0 else 4 * kt + 2)
    pe_ave = lambda p, qc, kt: pe0 + PEP * p + qcb(qc) + 3 + 4 * kt
    pe_avo = lambda p, qc, kt: pe0 + PEP * p + qcb(qc) + (5 + 4 * kt if kt < 63 else 256)
    SEP = 260
    se_expe = lambda p, qc, kt: se0 + SEP * p + 130 * qc + 2 * kt + 1
    se_expo = lambda p, qc, kt: se0 + SEP * p + 130 * qc + 2 * kt + 2
    se_oev = lambda p, qc, eo: se0 + SEP * p + 130 * qc + 128 + eo + 1
    VEP = 12
    ve_vt = lambda p, b2: ve0 + VEP * p + b2 + 1
    ve_dst = lambda p, qc, eo: ve0 + VEP * p + 8 + 2 * qc + eo + 1
    ld_kv = lambda p: ld0 + 16 + 16 * 2 * (p + 1)
    gs_d = lambda p, qc, eo: gs0c + 16 * (4 * p + 2 * qc + eo + 1)

    with nc.Block("C2") as blk:
        @blk.sync
        def _(s):
            s.dma_start(sel16_sb[:, :], sel16_in).then_inc(ld.h, 16)
            for p in range(NP):
                if p == 0:
                    s.wait_ge(sp.h, SP_KV_DONE)
                if p >= 2:
                    s.wait_ge(pe.h, pe_ste(p - 2, 1, 63))
                s.dma_start(K_sb[p % 2][:, :],
                            K_hbm[p * 128:(p + 1) * 128, :]).then_inc(ld.h, 16)
                if p >= 1:
                    s.wait_ge(pe.h, pe_trb(p - 1, 7))
                s.dma_start(V_sb[:, :],
                            V_hbm[p * 128:(p + 1) * 128, :]).then_inc(ld.h, 16)

        @blk.gpsimd
        def _(g):
            for p in range(NP):
                for qc in range(2):
                    for eo in range(2):
                        g.wait_ge(ve.h, ve_dst(p, qc, eo))
                        g.dma_start(D_all[2 * p + eo:2 * p + eo + 1,
                                          qc * 1024:(qc + 1) * 1024],
                                    D_st[2 * qc + eo][:, :]).then_inc(gs.h, 16)

        @blk.tensor
        def _(t):
            for p in range(NP):
                t.wait_ge(ld.h, ld_kv(p))
                for b2 in range(8):
                    if p > 0 and b2 == 0:
                        t.wait_ge(se.h, se_expe(p - 1, 1, 63))
                    if b2 > 0:
                        t.wait_ge(ve.h, ve_vt(p, b2 - 1))
                    half = b2 // 4
                    for i in range(16):
                        tt_ = (b2 % 4) * 16 + i
                        src_ = V_sb[half * 64:half * 64 + 64, tt_ * 128:(tt_ + 1) * 128]
                        idn = ident[0:64, 0:64] if half == 0 else ident[64:128, 64:128]
                        ins = t.transpose(trview[:, i * 64:(i + 1) * 64], src_, idn)
                    ins.then_inc(pe.h, 1)
                for qc in range(2):

                    def st_one(kt, half):
                        pst = ps_ste if half == 0 else ps_sto
                        lhsT = K_sb[p % 2][half * 64:half * 64 + 64,
                                           kt * 128:(kt + 1) * 128]
                        for tc in range(2):
                            col = p * OWN + qc * 1024 + tc * 512
                            ins = t.matmul(pst[:, tc * 512:(tc + 1) * 512], lhsT,
                                           Qp[half * 64:half * 64 + 64, col:col + 512],
                                           start=True, stop=True)
                        ins.then_inc(pe.h, 1)

                    def av(kt, half):
                        vt = VT_e if half == 0 else VT_o
                        po = ps_oe if half == 0 else ps_oo
                        ptb = (PT_e if half == 0 else PT_o)[kt % 2]
                        lhsT = vt[:, kt * 80:kt * 80 + 65]
                        for tc in range(2):
                            ins = t.matmul(po[:, tc * 512:(tc + 1) * 512], lhsT,
                                           ptb[:, tc * 512:(tc + 1) * 512],
                                           start=(kt == 0), stop=(kt == 63))
                        ins.then_inc(pe.h, 1)

                    if qc == 0:
                        t.wait_ge(ve.h, ve_vt(p, 7))
                    st_one(0, 0)
                    st_one(0, 1)
                    for kt in range(64):
                        t.wait_ge(se.h, se_expe(p, qc, kt))
                        if kt == 0:
                            if qc == 1:
                                t.wait_ge(se.h, se_oev(p, 0, 1))
                                t.wait_ge(ve.h, ve_dst(p, 0, 1))
                            elif p > 0:
                                t.wait_ge(se.h, se_oev(p - 1, 1, 1))
                                t.wait_ge(ve.h, ve_dst(p - 1, 1, 1))
                        av(kt, 0)
                        if kt + 1 < 64:
                            st_one(kt + 1, 0)
                        t.wait_ge(se.h, se_expo(p, qc, kt))
                        av(kt, 1)
                        if kt + 1 < 64:
                            st_one(kt + 1, 1)

        @blk.scalar
        def _(sc):
            for p in range(NP):
                for qc in range(2):
                    for kt in range(64):
                        for eo in range(2):
                            sc.wait_ge(pe.h, (pe_ste if eo == 0 else pe_sto)(p, qc, kt))
                            gstep = (p * 2 + qc) * 64 + kt - 2
                            if gstep >= 0:
                                pp, r = divmod(gstep, 128)
                                qq, kk = divmod(r, 64)
                                sc.wait_ge(pe.h, (pe_ave if eo == 0 else pe_avo)(pp, qq, kk))
                            pst = ps_ste if eo == 0 else ps_sto
                            ptb = (PT_e if eo == 0 else PT_o)[kt % 2]
                            sc.activation(ptb[:, :], pst[:, :], AF.Exp,
                                          bias=EXP_SHIFT, scale=1.0).then_inc(se.h, 1)
                    for eo in range(2):
                        sc.wait_ge(pe.h, (pe_ave if eo == 0 else pe_avo)(p, qc, 63))
                        po = ps_oe if eo == 0 else ps_oo
                        p0 = eo * 64
                        sc.activation(O_all[p0:p0 + 64,
                                            p * OWN + qc * 1024: p * OWN + qc * 1024 + 1024],
                                      po[0:64, :], AF.Identity,
                                      bias=0.0, scale=1.0).then_inc(se.h, 1)

        @blk.vector
        def _(v):
            v.memset(VT_e[:, :].rearrange("p (t c) -> p t c", c=80)[:, :, 64:65], 1.0)
            v.memset(VT_o[:, :].rearrange("p (t c) -> p t c", c=80)[:, :, 64:65], 1.0)
            for p in range(NP):
                for b2 in range(8):
                    v.wait_ge(pe.h, pe_trb(p, b2))
                    vt = VT_e if b2 < 4 else VT_o
                    src = trview[:, :].rearrange("p (t c) -> p t c", c=64)
                    dst = vt[:, :].rearrange("p (t c) -> p t c", c=80)[
                        :, (b2 % 4) * 16:(b2 % 4 + 1) * 16, 0:64]
                    v.tensor_copy(dst, src).then_inc(ve.h, 1)
                for qc in range(2):
                    for eo in range(2):
                        v.wait_ge(pe.h, (pe_ave if eo == 0 else pe_avo)(p, qc, 63))
                        if p >= 1:
                            v.wait_ge(gs.h, gs_d(p - 1, qc, eo))
                        po = ps_oe if eo == 0 else ps_oo
                        with nc.allow_low_precision(reason="softmax denominators in bf16"):
                            v.tensor_copy(D_st[2 * qc + eo][:, :],
                                          po[64:65, :]).then_inc(ve.h, 1)

    ld.n = ld0 + 16 + 16 * 2 * NP
    gs.n = gs0c + 16 * 4 * NP
    pe.n = pe0 + PEP * NP
    ve.n = ve0 + VEP * NP
    se.n = se0 + SEP * NP

    pC2ps.close()
    pCE = ExitStack()
    ps_rb = pCE.enter_context(nc.psum_tensor("ps_rb", [64, OWN], F32))
    pe0e, ve0e = pe.n, ve.n
    with nc.Block("C2E") as blk:
        @blk.vector
        def _(v):
            with nc.allow_low_precision(reason="bf16 reciprocal of softmax denominators"):
                v.reciprocal(D_all[:, :], D_all[:, :]).then_inc(ve.h, 1)
            v.wait_ge(ve.h, ve0e + 1)
            v.engine_nop().then_inc(ve.h, 1)
            for h in range(2 * NP):
                jb, p0 = h // 2, (h % 2) * 64
                v.wait_ge(pe.h, pe0e + h + 1)
                v.tensor_tensor(O_all[p0:p0 + 64, jb * OWN:(jb + 1) * OWN],
                                O_all[p0:p0 + 64, jb * OWN:(jb + 1) * OWN],
                                ps_rb[:, :], ALU.mult).then_inc(ve.h, 1)

        @blk.tensor
        def _(t):
            t.wait_ge(ld.h, ld0 + 16)
            t.wait_ge(ve.h, ve0e + 2)
            for h in range(2 * NP):
                if h >= 1:
                    t.wait_ge(ve.h, ve0e + 2 + h)
                for tc in range(4):
                    ins = t.matmul(ps_rb[:, tc * 512:(tc + 1) * 512],
                                   sel16_sb[:, h * 64:(h + 1) * 64],
                                   D_all[:, tc * 512:(tc + 1) * 512],
                                   start=True, stop=True)
                ins.then_inc(pe.h, 1)
    pe.n = pe0e + 2 * NP
    ve.n = ve0e + 2 + 2 * NP
    pCE.close()
    pC2.close()

    if "O" in taps:
        tap_dump([(dbg["O"], O_all[:, :])])
    if stop_after == "C":
        return finish()

    pD = ExitStack()
    psb = lambda name, shape, dt: pD.enter_context(nc.sbuf_tensor(name, shape, dt))
    fc_sb = psb("fc_sb", [128, 8 * D], BF16)
    g_buf = [psb(f"g_buf{i}", [128, OWN], BF16) for i in range(2)]
    xT_buf = [psb(f"xT_buf{i}", [128, OWN], F32) for i in range(2)]
    s_buf = psb("s_buf", [128, OWN], F32)
    o_stage = [psb(f"o_stage{i}", [128, OWN], F32) for i in range(2)]
    ps_fc = [pD.enter_context(nc.psum_tensor(f"ps_fc{i}", [128, OWN], F32))
             for i in range(2)]

    ld0, gs0, pe0, ve0, sp0, se0 = ld.n, gs.n, pe.n, ve.n, sp.n, se.n
    ve_D = lambda jb: ve0 + 3 * jb + 3

    with nc.Block("D") as blk:
        @blk.gpsimd
        def _(g):
            _wt_dma(g, fc_sb, fc_wT[:, :], 8).then_inc(gs.h, 16)

        @blk.sync
        def _(s):
            for jb in range(8):
                if jb == 0:
                    s.wait_ge(sp.h, SP_G_DONE)
                if jb >= 2:
                    s.wait_ge(ve.h, ve_D(jb - 2))
                s.dma_start(g_buf[jb % 2][:, :],
                            g_hbm[:, jb * OWN:(jb + 1) * OWN]).then_inc(ld.h, 16)
                s.dma_start(xT_buf[jb % 2][:, :],
                            xT_own[jb * 128:(jb + 1) * 128, :]).then_inc(ld.h, 16)

        @blk.tensor
        def _(t):
            t.wait_ge(gs.h, gs0 + 16)
            for jb in range(8):
                if jb >= 2:
                    t.wait_ge(ve.h, ve_D(jb - 2) - 1)
                for j in range(8):
                    lhsT = fc_sb[:, j * D + jb * 128: j * D + (jb + 1) * 128]
                    for tc in range(4):
                        ins = t.matmul(ps_fc[jb % 2][:, tc * 512:(tc + 1) * 512],
                                       lhsT,
                                       O_all[:, j * OWN + tc * 512: j * OWN + (tc + 1) * 512],
                                       start=(j == 0), stop=(j == 7))
                ins.then_inc(pe.h, 1)

        @blk.vector
        def _(v):
            for jb in range(8):
                v.wait_ge(pe.h, pe0 + jb + 1)
                v.wait_ge(ld.h, ld0 + 16 * (2 * jb + 2))
                if jb >= 2:
                    v.wait_ge(sp.h, sp0 + 16 * (jb - 1))
                v.scalar_tensor_tensor(s_buf[:, :], ps_fc[jb % 2][:, :],
                                       fc_b_sb[:, jb:jb + 1],
                                       xn_own[:, jb * OWN:(jb + 1) * OWN],
                                       ALU.add, ALU.add).then_inc(ve.h, 1)
                v.wait_ge(ve.h, ve_D(jb) - 2)
                v.tensor_tensor(s_buf[:, :], s_buf[:, :], g_buf[jb % 2][:, :], ALU.mult
                                ).then_inc(ve.h, 1)
                v.wait_ge(ve.h, ve_D(jb) - 1)
                v.tensor_tensor(o_stage[jb % 2][:, :], s_buf[:, :], xT_buf[jb % 2][:, :],
                                ALU.add).then_inc(ve.h, 1)

        @blk.scalar
        def _(sc):
            for jb in range(8):
                sc.wait_ge(ve.h, ve_D(jb))
                sc.dma_start(out_T[jb * 128:(jb + 1) * 128, :],
                             o_stage[jb % 2][:, :]).then_inc(sp.h, 16)

    ld.n = ld0 + 16 * 16
    gs.n = gs0 + 16
    pe.n = pe0 + 8
    ve.n = ve0 + 24
    sp.n = sp0 + 16 * 8
    pD.close()
    return finish()


def _prep_weights(qkv_w, qkv_bv, fc_w, fc_bv, rt_w, rt_bv, gup, gup_bv, dn, dn_bv):
    qperm = np.concatenate([np.arange(h * 192, h * 192 + 64) for h in range(16)])
    perm = np.concatenate([qperm, qperm + 64, qperm + 128])
    qkv_wT = np.ascontiguousarray(qkv_w[perm].T)
    qkv_b_arr = np.ascontiguousarray(qkv_bv[perm].reshape(24, 128).T)
    fc_wT = np.ascontiguousarray(fc_w.T)
    fc_b_arr = np.ascontiguousarray(fc_bv.reshape(8, 128).T)
    rt_wT = np.ascontiguousarray(rt_w.T)
    rt_b_arr = np.ascontiguousarray(rt_bv.reshape(1, 8))
    iperm = np.empty(4096, dtype=np.int64)
    blk = np.arange(64)
    for b in range(32):
        iperm[b * 128: b * 128 + 64] = b * 64 + blk
        iperm[b * 128 + 64: (b + 1) * 128] = 2048 + b * 64 + blk
    gup_wT = np.ascontiguousarray(gup[:, iperm, :].transpose(0, 2, 1))
    gup_b_arr = np.ascontiguousarray(
        gup_bv[:, iperm].reshape(8, 32, 128).transpose(2, 0, 1).reshape(128, 256))
    dn_wT = np.ascontiguousarray(dn.transpose(0, 2, 1))
    return dict(qkv_wT=qkv_wT, qkv_b=qkv_b_arr, fc_wT=fc_wT, fc_b=fc_b_arr,
                rt_wT=rt_wT, rt_b=rt_b_arr, gup_wT=gup_wT, gup_b=gup_b_arr,
                dn_wT=dn_wT, dn_b=np.ascontiguousarray(dn_bv),
                ident=np.eye(128, dtype=np.float32).astype(ml_dtypes.bfloat16),
                identf=np.eye(128, dtype=np.float32),
                sel=np.repeat(np.eye(8, dtype=np.float32), 128, axis=1
                              ).astype(ml_dtypes.bfloat16),
                sel16=np.repeat(np.eye(16, dtype=np.float32), 64, axis=1
                                ).astype(ml_dtypes.bfloat16))


def make_in_maps(inputs):
    x = np.asarray(inputs["L0_attn_x"], np.float32)
    w = _prep_weights(
        np.asarray(inputs["L0_attn_qkv_w"], np.float32),
        np.asarray(inputs["L0_attn_qkv_b"], np.float32),
        np.asarray(inputs["L0_attn_fc_w"], np.float32),
        np.asarray(inputs["L0_attn_fc_b"], np.float32),
        np.asarray(inputs["L0_ffn_router_weight"], np.float32),
        np.asarray(inputs["L0_ffn_router_bias"], np.float32),
        np.asarray(inputs["L0_ffn_gate_up_proj"], np.float32),
        np.asarray(inputs["L0_ffn_gate_up_proj_bias"], np.float32),
        np.asarray(inputs["L0_ffn_down_proj"], np.float32),
        np.asarray(inputs["L0_ffn_down_proj_bias"], np.float32))
    in_maps = []
    for c in range(8):
        b, q0 = c // 4, (c % 4) * OWN
        xb = x[b]
        x_rot = np.ascontiguousarray(np.concatenate([xb[q0:], xb[:q0]], axis=0))
        m = dict(w)
        m["x_full"] = x_rot
        m["xT_own"] = np.ascontiguousarray(x_rot[:OWN].T)
        in_maps.append(m)
    return in_maps


def kernel(**inputs):
    in_maps = make_in_maps(inputs)
    nc, _ = build_program()
    res = run_bass_kernel_spmd(nc, in_maps, core_ids=list(range(8)))
    out = np.empty((2, SEQ, D), np.float32)
    for c in range(8):
        b, q0 = c // 4, (c % 4) * OWN
        out[b, q0:q0 + OWN, :] = np.ascontiguousarray(res.results[c]["out_T"]).T
    return out


# version 10
# speedup vs baseline: 1.2813x; 1.2007x over previous
import math
import numpy as np
import ml_dtypes
from contextlib import ExitStack

import concourse.bass as bass
import concourse.mybir as mybir
from concourse.bass_utils import run_bass_kernel_spmd

F32 = mybir.dt.float32
BF16 = mybir.dt.bfloat16
AF = mybir.ActivationFunctionType
ALU = mybir.AluOpType
AX = mybir.AxisListType

D = 1024
SEQ = 8192
OWN = 2048
H = 16
NE = 8
NTT = SEQ // 128
EXP_SHIFT = -20.0
SILU_CLIP = 11.914 / (1.0 + math.exp(-11.914))
INV_1702 = 1.0 / 1.702
LN_EPS = 1e-5
C2_HEADS = H



def _wt_dma(eng, dst_sb, src_ap, nj):
    src3 = src_ap.rearrange("(j p) c -> p j c", p=128)
    dst3 = dst_sb[:, :].rearrange("p (j c) -> p j c", j=nj)
    return eng.dma_start(dst3, src3)

class Sem:
    def __init__(self, nc, es, name):
        self.h = es.enter_context(nc.semaphore(name))
        self.n = 0


def build_program(stop_after=None, taps=()):
    nc = bass.Bass("TRN2", target_bir_lowering=False)
    es = ExitStack()

    def din(name, shape, dt=F32):
        return nc.dram_tensor(name, shape, dt, kind="ExternalInput").ap()

    x_full = din("x_full", [SEQ, D])
    xT_own = din("xT_own", [D, OWN])
    qkv_wT = din("qkv_wT", [D, 3 * D])
    qkv_b = din("qkv_b", [128, 24])
    fc_wT = din("fc_wT", [D, D])
    fc_b = din("fc_b", [128, 8])
    rt_wT = din("rt_wT", [D, NE])
    rt_b = din("rt_b", [1, NE])
    gup_wT = din("gup_wT", [NE, D, 4096])
    gup_b = din("gup_b", [128, NE * 32])
    dn_wT = din("dn_wT", [NE, 2048, D])
    dn_b = din("dn_b", [NE, D])
    ident_in = din("ident", [128, 128], BF16)
    identf_in = din("identf", [128, 128], F32)
    sel_in = din("sel", [NE, NE * 128], BF16)
    sel16_in = din("sel16", [16, 16 * 64], BF16)

    out_T = nc.dram_tensor("out_T", [D, OWN], F32, kind="ExternalOutput").ap()

    def scratch(name, shape, dt, tap):
        kind = "ExternalOutput" if tap in taps else "Internal"
        return nc.dram_tensor(name, shape, dt, kind=kind).ap()

    K_hbm = scratch("K_hbm", [D, SEQ], BF16, "KV")
    V_hbm = scratch("V_hbm", [D, SEQ], BF16, "KV")
    g_hbm = scratch("g_hbm", [128, 8 * OWN], BF16, "g")
    dbg = {}
    for t, shape, dt in [("xn_own", [128, 8 * OWN], BF16), ("logits", [128, 128], F32),
                         ("comb", [8, OWN], BF16), ("y", [128, 8 * OWN], F32),
                         ("Q", [128, 8 * OWN], BF16), ("O", [128, 8 * OWN], BF16)]:
        if t in taps:
            dbg[t] = nc.dram_tensor("dbg_" + t, shape, dt, kind="ExternalOutput").ap()

    ld = Sem(nc, es, "ld")
    ldx = Sem(nc, es, "ldx")
    gs = Sem(nc, es, "gs")
    sp = Sem(nc, es, "sp")
    pe = Sem(nc, es, "pe")
    ve = Sem(nc, es, "ve")
    se = Sem(nc, es, "se")

    _tapn = [0]
    def tap_dump(name_pairs):
        _tapn[0] += 1
        with nc.Block(f"dbg{_tapn[0]}") as blk:
            @blk.sync
            def _(s):
                n = ld.n
                for dst, src in name_pairs:
                    s.dma_start(dst, src).then_inc(ld.h, 16)
                    n += 16
                s.wait_ge(ld.h, n)
        ld.n += 16 * len(name_pairs)

    def finish():
        with nc.Block("fin") as blk:
            @blk.sync
            def _(s):
                s.wait_ge(sp.h, sp.n)
        es.close()
        return nc, dbg

    sbp = lambda name, shape, dt: es.enter_context(nc.sbuf_tensor(name, shape, dt))
    qkv_b_sb = sbp("qkv_b_sb", [128, 24], F32)
    fc_b_sb = sbp("fc_b_sb", [128, 8], F32)
    rt_b_sb = sbp("rt_b_sb", [1, NE], BF16)
    gup_b_sb = sbp("gup_b_sb", [128, NE * 32], F32)
    dn_b_sb = sbp("dn_b_sb", [8, D], BF16)
    ident = sbp("ident_sb", [128, 128], BF16)
    identf = sbp("identf_sb", [128, 128], F32)
    sel_sb = sbp("sel_sb", [NE, NE * 128], BF16)
    ones1_bf = sbp("ones1_bf", [1, 128], BF16)
    ones1_f = sbp("ones1_f", [1, 128], F32)
    qbs = sbp("qbs", [128, 8], F32)
    xn_own = sbp("xn_own", [128, 8 * OWN], BF16)
    c_eps = sbp("c_eps", [128, 1], F32)
    c_shift = sbp("c_shift", [128, 1], F32)
    nc.const_aps.register(F32, LN_EPS, c_eps[:, 0:1])
    nc.const_aps.register(F32, EXP_SHIFT, c_shift[:, 0:1])

    pA = ExitStack()
    psb = lambda name, shape, dt: pA.enter_context(nc.sbuf_tensor(name, shape, dt))
    pps = lambda name, shape, dt: pA.enter_context(nc.psum_tensor(name, shape, dt))
    xnT = psb("xnT", [128, 8 * SEQ], BF16)
    x_buf = [psb(f"x_buf{i}", [128, D], F32) for i in range(2)]
    sq_buf = psb("sq_buf", [128, D], F32)
    xn_buf = [psb(f"xn_buf{i}", [128, D], BF16) for i in range(2)]
    st_s1 = psb("st_s1", [128, 1], F32)
    st_s2 = psb("st_s2", [128, 1], F32)
    st_m = psb("st_m", [128, 1], F32)
    st_v = psb("st_v", [128, 1], F32)
    st_msq = psb("st_msq", [128, 1], F32)
    st_sd = psb("st_sd", [128, 1], F32)
    st_rstd = psb("st_rstd", [128, 1], F32)
    st_mrn = psb("st_mrn", [128, 1], F32)
    pA1ps = ExitStack()
    ps_tr = [pA1ps.enter_context(nc.psum_tensor(f"ps_tr{i}", [128, D], BF16)) for i in range(2)]

    ld0, pe0, ve0, se0 = ld.n, pe.n, ve.n, se.n
    NSMALL = 6
    ldx0 = ldx.n
    ldv = lambda tt: ldx0 + 16 * (tt + 1)
    ve_sq = lambda tt: ve0 + 7 * tt + 1
    ve_red = lambda tt: ve0 + 7 * tt + 2
    ve_msq = lambda tt: ve0 + 7 * tt + 3
    ve_st1 = lambda tt: ve0 + 7 * tt + 4
    ve_rcp = lambda tt: ve0 + 7 * tt + 5
    ve_st2 = lambda tt: ve0 + 7 * tt + 6
    ve_evac = lambda tt: ve0 + 7 * tt + 7
    se_sd = lambda tt: se0 + 2 * tt + 1
    se_xn = lambda tt: se0 + 2 * tt + 2
    pe_tr = lambda tt: pe0 + tt + 1

    with nc.Block("A1") as blk:
        @blk.sync
        def _(s):
            s.dma_start(qkv_b_sb[:, :], qkv_b).then_inc(ld.h, 16)
            s.dma_start(fc_b_sb[:, :], fc_b).then_inc(ld.h, 16)
            s.dma_start(gup_b_sb[:, :], gup_b).then_inc(ld.h, 16)
            s.dma_start(ident[:, :], ident_in).then_inc(ld.h, 16)
            s.dma_start(identf[:, :], identf_in).then_inc(ld.h, 16)
            s.dma_start(sel_sb[:, :], sel_in).then_inc(ld.h, 16)
            for tt in range(NTT):
                if tt >= 2:
                    s.wait_ge(se.h, se_xn(tt - 2))
                s.dma_start(x_buf[tt % 2][:, :],
                            x_full[tt * 128:(tt + 1) * 128, :]).then_inc(ldx.h, 16)

        @blk.gpsimd
        def _(g):
            g.dma_start(rt_b_sb[:, :], rt_b).then_inc(gs.h, 16)
            g.dma_start(dn_b_sb[:, :], dn_b).then_inc(gs.h, 16)

        @blk.vector
        def _(v):
            v.memset(ones1_bf[:, :], 1.0)
            v.memset(ones1_f[:, :], 1.0)
            v.memset(c_eps[:, :], LN_EPS)
            v.memset(c_shift[:, :], EXP_SHIFT)
            v.wait_ge(ld.h, ld0 + 16 * NSMALL)
            v.tensor_scalar(qbs[:, :], qkv_b_sb[:, 0:8], 0.125, None, ALU.mult)
            for tt in range(NTT):
                x = x_buf[tt % 2]
                v.wait_ge(ldx.h, ldv(tt))
                v.tensor_tensor(sq_buf[:, :], x[:, :], x[:, :], ALU.mult).then_inc(ve.h, 1)
                v.reduce_sum(st_s1[:, :], x[:, :], axis=AX.X)
                v.wait_ge(ve.h, ve_sq(tt))
                v.reduce_sum(st_s2[:, :], sq_buf[:, :], axis=AX.X).then_inc(ve.h, 1)
                v.wait_ge(ve.h, ve_red(tt))
                v.tensor_scalar(st_m[:, :], st_s1[:, :], 1.0 / D, None, ALU.mult)
                v.tensor_scalar(st_v[:, :], st_s2[:, :], 1.0 / D, None, ALU.mult)
                v.tensor_tensor(st_msq[:, :], st_m[:, :], st_m[:, :],
                                ALU.mult).then_inc(ve.h, 1)
                v.wait_ge(ve.h, ve_msq(tt))
                v.tensor_tensor(st_v[:, :], st_v[:, :], st_msq[:, :],
                                ALU.subtract).then_inc(ve.h, 1)
                v.wait_ge(se.h, se_sd(tt))
                v.reciprocal(st_rstd[:, :], st_sd[:, :]).then_inc(ve.h, 1)
                v.wait_ge(ve.h, ve_rcp(tt))
                v.scalar_tensor_tensor(st_mrn[:, :], st_m[:, :], -1.0, st_rstd[:, :],
                                       ALU.mult, ALU.mult).then_inc(ve.h, 1)
                v.wait_ge(pe.h, pe_tr(tt))
                src = ps_tr[tt % 2][:, :].rearrange("p (j t) -> p j t", j=8)
                dst = xnT[:, :].rearrange("p (j t) -> p j t", j=8)[:, :, tt * 128:(tt + 1) * 128]
                v.tensor_copy(dst, src).then_inc(ve.h, 1)

        @blk.scalar
        def _(sc):
            for tt in range(NTT):
                sc.wait_ge(ve.h, ve_st1(tt))
                sc.activation(st_sd[:, :], st_v[:, :], AF.Sqrt, bias=LN_EPS,
                              scale=1.0).then_inc(se.h, 1)
                sc.wait_ge(ve.h, ve_st2(tt))
                if tt >= 2:
                    sc.wait_ge(pe.h, pe_tr(tt - 2))
                sc.activation(xn_buf[tt % 2][:, :], x_buf[tt % 2][:, :], AF.Identity,
                              bias=st_mrn[:, 0:1], scale=st_rstd[:, 0:1]
                              ).then_inc(se.h, 1)

        @blk.tensor
        def _(t):
            for tt in range(NTT):
                t.wait_ge(se.h, se_xn(tt))
                if tt >= 2:
                    t.wait_ge(ve.h, ve_evac(tt - 2))
                for j in range(8):
                    ins = t.transpose(ps_tr[tt % 2][:, j * 128:(j + 1) * 128],
                                      xn_buf[tt % 2][:, j * 128:(j + 1) * 128],
                                      ident[:, :])
                ins.then_inc(pe.h, 1)

    ld.n = ld0 + 16 * NSMALL
    ldx.n = ldx0 + 16 * NTT
    gs.n += 32
    ve.n = ve0 + 7 * NTT
    se.n = se0 + 2 * NTT
    pe.n = pe0 + NTT
    pA1ps.close()

    pA2 = ExitStack()
    wkv_sb = pA2.enter_context(nc.sbuf_tensor("wkv_sb", [128, 8 * D], BF16))
    kv_stage = [pA2.enter_context(nc.sbuf_tensor(f"kv_stage{i}", [128, 2048], BF16))
                for i in range(2)]
    ps_kv = [pA2.enter_context(nc.psum_tensor(f"ps_kv{i}", [128, 2048], F32))
             for i in range(2)]

    gs0, pe0, se0, sp0, ve0 = gs.n, pe.n, se.n, sp.n, ve.n
    NG = 32
    pe_g = lambda g_: pe0 + g_ + 1
    se_g = lambda g_: se0 + g_ + 1
    sp_g = lambda g_: sp0 + 16 * (g_ + 1)

    with nc.Block("A2") as blk:
        @blk.vector
        def _(v):
            src = xnT[:, :].rearrange("p (j t) -> p j t", j=8)[:, :, 0:OWN]
            dst = xn_own[:, :].rearrange("p (j t) -> p j t", j=8)
            v.tensor_copy(dst, src).then_inc(ve.h, 1)

        @blk.gpsimd
        def _(g):
            _wt_dma(g, wkv_sb, qkv_wT[:, D:2 * D], 8).then_inc(gs.h, 16)
            g.wait_ge(pe.h, pe_g(NG - 1))
            _wt_dma(g, wkv_sb, qkv_wT[:, 2 * D:3 * D], 8).then_inc(gs.h, 16)

        @blk.tensor
        def _(t):
            for gi in range(2 * NG):
                proj, jb, tcg = gi // NG, (gi % NG) // 4, (gi % NG) % 4
                t.wait_ge(gs.h, gs0 + 16 * (proj + 1))
                if gi >= 2:
                    t.wait_ge(se.h, se_g(gi - 2))
                for j in range(8):
                    lhsT = wkv_sb[:, j * D + jb * 128: j * D + (jb + 1) * 128]
                    for tc in range(4):
                        col = tcg * 2048 + tc * 512
                        ins = t.matmul(ps_kv[gi % 2][:, tc * 512:(tc + 1) * 512],
                                       lhsT, xnT[:, j * SEQ + col: j * SEQ + col + 512],
                                       start=(j == 0), stop=(j == 7))
                ins.then_inc(pe.h, 1)

        @blk.scalar
        def _(sc):
            for gi in range(2 * NG):
                proj, jb, tcg = gi // NG, (gi % NG) // 4, (gi % NG) % 4
                sc.wait_ge(pe.h, pe_g(gi))
                if gi >= 2:
                    sc.wait_ge(sp.h, sp_g(gi - 2))
                bias_col = 8 * (proj + 1) + jb
                sc.activation(kv_stage[gi % 2][:, :], ps_kv[gi % 2][:, :], AF.Identity,
                              bias=qkv_b_sb[:, bias_col:bias_col + 1],
                              scale=1.0).then_inc(se.h, 1)
                sc.wait_ge(se.h, se_g(gi))
                tgt = K_hbm if proj == 0 else V_hbm
                sc.dma_start(tgt[jb * 128:(jb + 1) * 128, tcg * 2048:(tcg + 1) * 2048],
                             kv_stage[gi % 2][:, :]).then_inc(sp.h, 16)

    gs.n = gs0 + 32
    ve.n = ve0 + 1
    pe.n = pe0 + 2 * NG
    se.n = se0 + 2 * NG
    sp.n = sp0 + 16 * 2 * NG
    SP_KV_DONE = sp.n
    pA2.close()
    pA.close()

    if "xn_own" in taps:
        tap_dump([(dbg["xn_own"], xn_own[:, :])])
    if stop_after == "A":
        return finish()

    pB = ExitStack()
    psb = lambda name, shape, dt: pB.enter_context(nc.sbuf_tensor(name, shape, dt))
    y_sb = psb("y_sb", [128, 8 * OWN], F32)
    A_sb = psb("A_sb", [128, 16 * OWN], BF16)
    gup_buf = [psb(f"gup_buf{i}", [128, 8 * 256], BF16) for i in range(2)]
    dn_buf = [psb(f"dn_buf{i}", [128, 16 * 128], BF16) for i in range(2)]
    gsl_buf = [psb(f"gsl_buf{i}", [64, OWN], BF16) for i in range(2)]
    uc_buf = psb("uc_buf", [64, OWN], BF16)
    combB = psb("combB", [128, OWN], BF16)
    combT = psb("combT", [8, OWN], BF16)
    rt_w_sb = psb("rt_w_sb", [128, 8 * NE], BF16)
    logits_sb = psb("logits_sb", [128, 128], F32)
    lwork = psb("lwork", [128, 8], F32)
    m4 = psb("m4", [128, 4], F32)
    mtmp = psb("mtmp", [128, 8], F32)
    exb = psb("exb", [128, 8], F32)
    negm = psb("negm", [128, 1], F32)
    den = psb("den", [128, 1], F32)
    rden = psb("rden", [128, 1], F32)
    bsig = psb("bsig", [64, 32], F32)

    pB1ps = ExitStack()
    ps_r = pB1ps.enter_context(nc.psum_tensor("ps_r", [128, 128], F32))
    ps_ct = pB1ps.enter_context(nc.psum_tensor("ps_ct", [8, 512], F32))
    ps_yi = pB1ps.enter_context(nc.psum_tensor("ps_yi", [128, OWN], F32))

    gs0, pe0, ve0, se0 = gs.n, pe.n, ve.n, se.n
    VB = 19
    tb_base = lambda tb: ve0 + 1 + VB * tb
    ve_negm = lambda tb: tb_base(tb) + 15
    ve_exm = lambda tb: tb_base(tb) + 16
    ve_comb_all = ve0 + 1 + VB * 16
    ve_ct = lambda tbg: ve_comb_all + tbg + 1
    ve_combT = ve_comb_all + 4
    ve_yev = lambda db: ve_combT + db + 1
    se_exp = lambda tb: se0 + tb + 1
    pe_ct = lambda tbg: pe0 + 1 + tbg + 1
    pe_yi = lambda db: pe0 + 5 + db + 1

    with nc.Block("B1") as blk:
        @blk.gpsimd
        def _(g):
            _wt_dma(g, rt_w_sb, rt_wT[:, :], 8).then_inc(gs.h, 16)

        @blk.tensor
        def _(t):
            t.wait_ge(gs.h, gs0 + 16)
            for tb in range(16):
                t.matmul(ps_r[:, tb * 8:(tb + 1) * 8], ones1_bf[:, :], rt_b_sb[:, :],
                         start=True, stop=False)
                for j in range(8):
                    ins = t.matmul(ps_r[:, tb * 8:(tb + 1) * 8],
                                   xn_own[:, j * OWN + tb * 128: j * OWN + (tb + 1) * 128],
                                   rt_w_sb[:, j * NE:(j + 1) * NE],
                                   start=False, stop=(j == 7))
            ins.then_inc(pe.h, 1)
            t.wait_ge(ve.h, ve_comb_all)
            for tbg in range(4):
                if tbg > 0:
                    t.wait_ge(ve.h, ve_ct(tbg - 1))
                for i in range(4):
                    tb = tbg * 4 + i
                    ins = t.transpose(ps_ct[:, i * 128:(i + 1) * 128],
                                      logits_sb[:, tb * 8:(tb + 1) * 8], identf[:, :])
                ins.then_inc(pe.h, 1)
            t.wait_ge(ve.h, ve_combT)
            for db in range(8):
                if db >= 1:
                    t.wait_ge(ve.h, ve_yev(db - 1))
                for tc in range(4):
                    ins = t.matmul(ps_yi[:, tc * 512:(tc + 1) * 512],
                                   dn_b_sb[:, db * 128:(db + 1) * 128],
                                   combT[:, tc * 512:(tc + 1) * 512],
                                   start=True, stop=True)
                ins.then_inc(pe.h, 1)

        @blk.vector
        def _(v):
            v.wait_ge(pe.h, pe0 + 1)
            v.tensor_copy(logits_sb[:, :], ps_r[:, :]).then_inc(ve.h, 1)
            for tb in range(16):
                lg = logits_sb[:, tb * 8:(tb + 1) * 8]
                v.tensor_copy(lwork[:, :], lg).then_inc(ve.h, 1)
                v.wait_ge(ve.h, tb_base(tb) + 1)
                for i in range(4):
                    v.reduce_max(m4[:, i:i + 1], lwork[:, :], axis=AX.X).then_inc(ve.h, 1)
                    v.wait_ge(ve.h, tb_base(tb) + 2 + 3 * i)
                    v.tensor_scalar(mtmp[:, :], lwork[:, :], m4[:, i:i + 1], -2e30,
                                    ALU.is_ge, ALU.mult).then_inc(ve.h, 1)
                    v.wait_ge(ve.h, tb_base(tb) + 3 + 3 * i)
                    v.tensor_tensor(lwork[:, :], lwork[:, :], mtmp[:, :],
                                    ALU.add).then_inc(ve.h, 1)
                    v.wait_ge(ve.h, tb_base(tb) + 4 + 3 * i)
                v.tensor_scalar(mtmp[:, :], lg, m4[:, 3:4], None,
                                ALU.is_ge).then_inc(ve.h, 1)
                v.tensor_scalar(negm[:, :], m4[:, 0:1], -1.0, None,
                                ALU.mult).then_inc(ve.h, 1)
                v.wait_ge(se.h, se_exp(tb))
                v.tensor_tensor(exb[:, :], exb[:, :], mtmp[:, :],
                                ALU.mult).then_inc(ve.h, 1)
                v.wait_ge(ve.h, tb_base(tb) + 16)
                v.reduce_sum(den[:, :], exb[:, :], axis=AX.X).then_inc(ve.h, 1)
                v.wait_ge(ve.h, tb_base(tb) + 17)
                v.reciprocal(rden[:, :], den[:, :]).then_inc(ve.h, 1)
                v.wait_ge(ve.h, tb_base(tb) + 18)
                v.tensor_scalar(lg, exb[:, :], rden[:, 0:1], None,
                                ALU.mult).then_inc(ve.h, 1)
            for tbg in range(4):
                v.wait_ge(pe.h, pe_ct(tbg))
                v.tensor_copy(combT[:, tbg * 512:(tbg + 1) * 512],
                              ps_ct[:, :]).then_inc(ve.h, 1)
            for db in range(8):
                v.wait_ge(pe.h, pe_yi(db))
                v.tensor_copy(y_sb[:, db * OWN:(db + 1) * OWN],
                              ps_yi[:, :]).then_inc(ve.h, 1)

        @blk.scalar
        def _(sc):
            for tb in range(16):
                sc.wait_ge(ve.h, ve_negm(tb))
                if tb >= 1:
                    sc.wait_ge(ve.h, ve_exm(tb - 1))
                sc.activation(exb[:, :], logits_sb[:, tb * 8:(tb + 1) * 8], AF.Exp,
                              bias=negm[:, 0:1], scale=1.0).then_inc(se.h, 1)

    gs.n = gs0 + 16
    pe.n = pe0 + 5 + 8
    ve.n = ve_yev(7)
    se.n = se0 + 16
    pB1ps.close()

    pairs = []
    if "logits" in taps:
        pairs.append((dbg["logits"], logits_sb[:, :]))
    if "comb" in taps:
        pairs.append((dbg["comb"], combT[:, :]))
    if pairs:
        tap_dump(pairs)

    pB2ps = ExitStack()
    ps_gu = [pB2ps.enter_context(nc.psum_tensor(f"ps_gu{i}", [128, OWN], F32))
             for i in range(2)]

    gs0, pe0, ve0, se0 = gs.n, pe.n, ve.n, se.n
    PEE = 41
    pe_bcast = lambda e: pe0 + PEE * e + 1
    pe_gup = lambda e, b: pe0 + PEE * e + 1 + (b + 1)
    pe_dn = lambda e, db: pe0 + PEE * e + 33 + (db + 1)
    VEE = 138
    ve_combB = lambda e: ve0 + VEE * e + 1
    ve_bsig = lambda e: ve0 + VEE * e + 2
    ve_act = lambda e, b: ve0 + VEE * e + 2 + 4 * (b + 1)
    ve_yacc = lambda e, db: ve0 + VEE * e + 130 + (db + 1)
    se_gs = lambda e, b: se0 + 32 * e + b + 1
    gs_gup = lambda e, ch: gs0 + 16 * (24 * e + ch + 1)
    gs_dn = lambda e, db: gs0 + 16 * (24 * e + 16 + db + 1)

    with nc.Block("B2") as blk:
        @blk.gpsimd
        def _(g):
            for e in range(NE):
                for ch in range(16):
                    pv = (e, ch - 2) if ch >= 2 else (e - 1, ch + 14)
                    if pv[0] >= 0:
                        g.wait_ge(pe.h, pe_gup(pv[0], pv[1] * 2 + 1))
                    _wt_dma(g, gup_buf[ch % 2], gup_wT[e, :, ch * 256:(ch + 1) * 256],
                            8).then_inc(gs.h, 16)
                for db in range(8):
                    pv = (e, db - 2) if db >= 2 else (e - 1, db + 6)
                    if pv[0] >= 0:
                        g.wait_ge(pe.h, pe_dn(pv[0], pv[1]))
                    _wt_dma(g, dn_buf[db % 2], dn_wT[e, :, db * 128:(db + 1) * 128],
                            16).then_inc(gs.h, 16)

        @blk.tensor
        def _(t):
            for e in range(NE):
                if e > 0:
                    t.wait_ge(ve.h, ve_yacc(e - 1, 7))
                for tc in range(4):
                    ins = t.matmul(ps_gu[0][:, tc * 512:(tc + 1) * 512],
                                   sel_sb[:, e * 128:(e + 1) * 128],
                                   combT[:, tc * 512:(tc + 1) * 512],
                                   start=True, stop=True)
                ins.then_inc(pe.h, 1)
                t.wait_ge(ve.h, ve_combB(e))
                for b in range(32):
                    ch = b // 2
                    t.wait_ge(gs.h, gs_gup(e, ch))
                    if b >= 2:
                        t.wait_ge(ve.h, ve_act(e, b - 2))
                    for j in range(8):
                        lhsT = gup_buf[ch % 2][:, j * 256 + (b % 2) * 128:
                                               j * 256 + (b % 2) * 128 + 128]
                        for tc in range(4):
                            ins = t.matmul(ps_gu[b % 2][:, tc * 512:(tc + 1) * 512],
                                           lhsT,
                                           xn_own[:, j * OWN + tc * 512: j * OWN + (tc + 1) * 512],
                                           start=(j == 0), stop=(j == 7))
                    ins.then_inc(pe.h, 1)
                t.wait_ge(ve.h, ve_act(e, 31))
                for db in range(8):
                    t.wait_ge(gs.h, gs_dn(e, db))
                    if db >= 2:
                        t.wait_ge(ve.h, ve_yacc(e, db - 2))
                    for ic in range(16):
                        lhsT = dn_buf[db % 2][:, ic * 128:(ic + 1) * 128]
                        for tc in range(4):
                            ins = t.matmul(ps_gu[db % 2][:, tc * 512:(tc + 1) * 512],
                                           lhsT,
                                           A_sb[:, ic * OWN + tc * 512: ic * OWN + (tc + 1) * 512],
                                           start=(ic == 0), stop=(ic == 15))
                    ins.then_inc(pe.h, 1)

        @blk.scalar
        def _(sc):
            for e in range(NE):
                sc.wait_ge(ve.h, ve_bsig(e))
                for b in range(32):
                    sc.wait_ge(pe.h, pe_gup(e, b))
                    if b >= 2:
                        sc.wait_ge(ve.h, ve_act(e, b - 2))
                    sc.activation(gsl_buf[b % 2][:, :], ps_gu[b % 2][0:64, :], AF.Silu,
                                  bias=bsig[:, b:b + 1], scale=1.702).then_inc(se.h, 1)

        @blk.vector
        def _(v):
            for e in range(NE):
                v.wait_ge(pe.h, pe_bcast(e))
                v.tensor_scalar(combB[:, :], ps_gu[0][:, :], INV_1702, None,
                                ALU.mult).then_inc(ve.h, 1)
                v.tensor_scalar(bsig[:, :], gup_b_sb[0:64, e * 32:(e + 1) * 32], 1.702,
                                None, ALU.mult).then_inc(ve.h, 1)
                for b in range(32):
                    jj = b // 2
                    p0 = (b % 2) * 64
                    v.wait_ge(pe.h, pe_gup(e, b))
                    v.wait_ge(se.h, se_gs(e, b))
                    v.tensor_scalar(uc_buf[:, :], ps_gu[b % 2][64:128, :],
                                    gup_b_sb[64:128, e * 32 + b:e * 32 + b + 1], 7.0,
                                    ALU.add, ALU.min).then_inc(ve.h, 1)
                    v.wait_ge(ve.h, ve_act(e, b) - 3)
                    v.tensor_scalar(uc_buf[:, :], uc_buf[:, :], -7.0, None, ALU.max)
                    v.tensor_scalar(gsl_buf[b % 2][:, :], gsl_buf[b % 2][:, :],
                                    SILU_CLIP, None, ALU.min).then_inc(ve.h, 1)
                    v.wait_ge(ve.h, ve_act(e, b) - 2)
                    v.tensor_tensor(gsl_buf[b % 2][:, :], gsl_buf[b % 2][:, :],
                                    combB[0:64, :], ALU.mult).then_inc(ve.h, 1)
                    v.wait_ge(ve.h, ve_act(e, b) - 1)
                    v.scalar_tensor_tensor(A_sb[p0:p0 + 64, jj * OWN:(jj + 1) * OWN],
                                           uc_buf[:, :], 1.0, gsl_buf[b % 2][:, :],
                                           ALU.add, ALU.mult).then_inc(ve.h, 1)
                for db in range(8):
                    v.wait_ge(pe.h, pe_dn(e, db))
                    v.tensor_tensor(y_sb[:, db * OWN:(db + 1) * OWN],
                                    y_sb[:, db * OWN:(db + 1) * OWN],
                                    ps_gu[db % 2][:, :], ALU.add).then_inc(ve.h, 1)

    gs.n = gs0 + 16 * 24 * NE
    pe.n = pe0 + PEE * NE
    ve.n = ve0 + VEE * NE
    se.n = se0 + 32 * NE

    se0, sp0 = se.n, sp.n
    with nc.Block("Bg") as blk:
        @blk.scalar
        def _(sc):
            for jb in range(8):
                sc.wait_ge(ve.h, ve_yacc(NE - 1, jb))
                if jb >= 1:
                    sc.wait_ge(sp.h, sp0 + 32 * jb)
                sc.activation(gsl_buf[jb % 2][:, :], y_sb[0:64, jb * OWN:(jb + 1) * OWN],
                              AF.Silu, bias=0.0, scale=1.0)
                sc.activation(uc_buf[:, :], y_sb[64:128, jb * OWN:(jb + 1) * OWN],
                              AF.Silu, bias=0.0, scale=1.0).then_inc(se.h, 1)
                sc.wait_ge(se.h, se0 + jb + 1)
                sc.dma_start(g_hbm[0:64, jb * OWN:(jb + 1) * OWN],
                             gsl_buf[jb % 2][:, :]).then_inc(sp.h, 16)
                sc.dma_start(g_hbm[64:128, jb * OWN:(jb + 1) * OWN],
                             uc_buf[:, :]).then_inc(sp.h, 16)
    se.n = se0 + 8
    sp.n = sp0 + 16 * 16
    SP_G_DONE = sp.n

    if "y" in taps:
        tap_dump([(dbg["y"], y_sb[:, :])])
    pB2ps.close()
    pB.close()
    if stop_after == "B":
        return finish()

    Qp = es.enter_context(nc.sbuf_tensor("Qp", [128, 8 * OWN], BF16))
    O_all = es.enter_context(nc.sbuf_tensor("O_all", [128, 8 * OWN], BF16))

    pC1 = ExitStack()
    wq_sb = pC1.enter_context(nc.sbuf_tensor("wq_sb", [128, 8 * D], BF16))
    ps_q = [pC1.enter_context(nc.psum_tensor(f"ps_q{i}", [128, OWN], F32))
            for i in range(2)]

    gs0, pe0, se0 = gs.n, pe.n, se.n
    with nc.Block("C1") as blk:
        @blk.gpsimd
        def _(g):
            _wt_dma(g, wq_sb, qkv_wT[:, 0:D], 8).then_inc(gs.h, 16)

        @blk.tensor
        def _(t):
            t.wait_ge(gs.h, gs0 + 16)
            for jb in range(8):
                if jb >= 2:
                    t.wait_ge(se.h, se0 + jb - 1)
                for j in range(8):
                    lhsT = wq_sb[:, j * D + jb * 128: j * D + (jb + 1) * 128]
                    for tc in range(4):
                        ins = t.matmul(ps_q[jb % 2][:, tc * 512:(tc + 1) * 512],
                                       lhsT,
                                       xn_own[:, j * OWN + tc * 512: j * OWN + (tc + 1) * 512],
                                       start=(j == 0), stop=(j == 7))
                ins.then_inc(pe.h, 1)

        @blk.scalar
        def _(sc):
            for jb in range(8):
                sc.wait_ge(pe.h, pe0 + jb + 1)
                sc.activation(Qp[:, jb * OWN:(jb + 1) * OWN], ps_q[jb % 2][:, :],
                              AF.Identity, bias=qbs[:, jb:jb + 1],
                              scale=0.125).then_inc(se.h, 1)
    gs.n = gs0 + 16
    pe.n = pe0 + 8
    se.n = se0 + 8
    pC1.close()

    if "Q" in taps:
        tap_dump([(dbg["Q"], Qp[:, :])])
    if stop_after == "C1":
        return finish()

    pC2 = ExitStack()
    psb = lambda name, shape, dt: pC2.enter_context(nc.sbuf_tensor(name, shape, dt))
    K_sb = [psb(f"K_sb{i}", [128, SEQ], BF16) for i in range(2)]
    V_sb = psb("V_sb", [128, SEQ], BF16)
    VT_e = psb("VT_e", [128, 64 * 80], BF16)
    VT_o = psb("VT_o", [128, 64 * 80], BF16)
    PT = [psb(f"PT{i}", [128, 1024], BF16) for i in range(3)]
    D_all = psb("D_all", [16, OWN], BF16)
    D_st = [psb(f"D_st{i}", [1, 512], BF16) for i in range(4)]
    sel16_sb = psb("sel16_sb", [16, 16 * 64], BF16)
    pC2ps = ExitStack()
    ps_st = [pC2ps.enter_context(nc.psum_tensor(f"ps_st{i}", [128, 1024], F32))
             for i in range(3)]
    ps_oE = pC2ps.enter_context(nc.psum_tensor("ps_oE", [65, 512], F32))
    ps_oO = pC2ps.enter_context(nc.psum_tensor("ps_oO", [65, 512], F32))
    trview = ps_st[0][:, 0:512].bitcast(BF16)

    ld0, pe0, ve0, se0, gs0c = ld.n, pe.n, ve.n, se.n, gs.n
    NP = C2_HEADS // 2
    NQC = 4
    SE_ENTRY = se0
    PEP = 8 + NQC * 128
    SEP = NQC * 66
    tup = lambda G: (G // (NQC * 64),) + divmod(G % (NQC * 64), 64)
    pe_trb = lambda p, b2: pe0 + PEP * p + b2 + 1

    def pe_st_g(G):
        p, qc, kt = tup(G)
        return pe0 + PEP * p + 8 + qc * 128 + (1 if kt == 0 else 2 * kt)

    def pe_av_g(G):
        p, qc, kt = tup(G)
        return pe0 + PEP * p + 8 + qc * 128 + (2 * kt + 3 if kt < 63 else 128)

    def se_exp_g(G):
        p, qc, kt = tup(G)
        return se0 + SEP * p + 66 * qc + kt + 1

    se_oev = lambda p, qc, eo: se0 + SEP * p + 66 * qc + 64 + eo + 1
    VEP = 8 + 2 * NQC
    ve_vt = lambda p, b2: ve0 + VEP * p + b2 + 1
    ve_dst = lambda p, qc, eo: ve0 + VEP * p + 8 + 2 * qc + eo + 1
    ld_kv = lambda p: ld0 + 16 + 16 * 2 * (p + 1)
    gs_d = lambda lin: gs0c + 16 * (lin + 1)

    with nc.Block("C2") as blk:
        @blk.sync
        def _(s):
            s.dma_start(sel16_sb[:, :], sel16_in).then_inc(ld.h, 16)
            for p in range(NP):
                if p == 0:
                    s.wait_ge(sp.h, SP_KV_DONE)
                if p >= 2:
                    s.wait_ge(pe.h, pe_st_g(((p - 2) * NQC + NQC - 1) * 64 + 63))
                s.dma_start(K_sb[p % 2][:, :],
                            K_hbm[p * 128:(p + 1) * 128, :]).then_inc(ld.h, 16)
                if p >= 1:
                    s.wait_ge(pe.h, pe_trb(p - 1, 7))
                s.dma_start(V_sb[:, :],
                            V_hbm[p * 128:(p + 1) * 128, :]).then_inc(ld.h, 16)

        @blk.gpsimd
        def _(g):
            for p in range(NP):
                for qc in range(NQC):
                    for eo in range(2):
                        g.wait_ge(ve.h, ve_dst(p, qc, eo))
                        g.dma_start(D_all[2 * p + eo:2 * p + eo + 1,
                                          qc * 512:(qc + 1) * 512],
                                    D_st[2 * (qc % 2) + eo][:, :]).then_inc(gs.h, 16)

        @blk.tensor
        def _(t):
            for p in range(NP):
                t.wait_ge(ld.h, ld_kv(p))
                t.wait_ge(se.h, se_exp_g(p * NQC * 64 - 1) if p > 0 else SE_ENTRY)
                for b2 in range(8):
                    if b2 > 0:
                        t.wait_ge(ve.h, ve_vt(p, b2 - 1))
                    half = b2 // 4
                    for i in range(16):
                        tt_ = (b2 % 4) * 16 + i
                        src_ = V_sb[half * 64:half * 64 + 64, tt_ * 128:(tt_ + 1) * 128]
                        idn = ident[0:64, 0:64] if half == 0 else ident[64:128, 64:128]
                        ins = t.transpose(trview[:, i * 64:(i + 1) * 64], src_, idn)
                    ins.then_inc(pe.h, 1)
                t.wait_ge(ve.h, ve_vt(p, 7))

                def st_pair(G):
                    _, qc, kt = tup(G)
                    if G >= 3:
                        t.wait_ge(se.h, se_exp_g(G - 3))
                    pst = ps_st[G % 3]
                    for half in range(2):
                        lhsT = K_sb[p % 2][half * 64:half * 64 + 64,
                                           kt * 128:(kt + 1) * 128]
                        col = p * OWN + qc * 512
                        ins = t.matmul(pst[:, half * 512:(half + 1) * 512], lhsT,
                                       Qp[half * 64:half * 64 + 64, col:col + 512],
                                       start=True, stop=True)
                    ins.then_inc(pe.h, 1)

                def av_pair(G):
                    _, qc, kt = tup(G)
                    t.wait_ge(se.h, se_exp_g(G))
                    if kt == 0:
                        Q = p * NQC + qc
                        if Q >= 2:
                            pp, qq = divmod(Q - 2, NQC)
                            t.wait_ge(se.h, se_oev(pp, qq, 1))
                            t.wait_ge(ve.h, ve_dst(pp, qq, 1))
                    ptb = PT[G % 3]
                    for half, vt, po in ((0, VT_e, ps_oE), (1, VT_o, ps_oO)):
                        ins = t.matmul(po[:, :], vt[:, kt * 80:kt * 80 + 65],
                                       ptb[:, half * 512:(half + 1) * 512],
                                       start=(kt == 0), stop=(kt == 63))
                    ins.then_inc(pe.h, 1)

                G0 = p * NQC * 64
                for qc in range(NQC):
                    base = G0 + qc * 64
                    st_pair(base)
                    for kt in range(64):
                        if kt + 1 < 64:
                            st_pair(base + kt + 1)
                        av_pair(base + kt)

        @blk.scalar
        def _(sc):
            for p in range(NP):
                for qc in range(NQC):
                    for kt in range(64):
                        G = (p * NQC + qc) * 64 + kt
                        sc.wait_ge(pe.h, pe_st_g(G))
                        if G >= 3:
                            sc.wait_ge(pe.h, pe_av_g(G - 3))
                        sc.activation(PT[G % 3][:, :], ps_st[G % 3][:, :], AF.Exp,
                                      bias=EXP_SHIFT, scale=1.0).then_inc(se.h, 1)
                    sc.wait_ge(pe.h, pe_av_g((p * NQC + qc) * 64 + 63))
                    for eo, po in ((0, ps_oE), (1, ps_oO)):
                        sc.activation(O_all[eo * 64:eo * 64 + 64,
                                            p * OWN + qc * 512: p * OWN + qc * 512 + 512],
                                      po[0:64, :], AF.Identity,
                                      bias=0.0, scale=1.0).then_inc(se.h, 1)

        @blk.vector
        def _(v):
            v.memset(VT_e[:, :].rearrange("p (t c) -> p t c", c=80)[:, :, 64:65], 1.0)
            v.memset(VT_o[:, :].rearrange("p (t c) -> p t c", c=80)[:, :, 64:65], 1.0)
            for p in range(NP):
                for b2 in range(8):
                    v.wait_ge(pe.h, pe_trb(p, b2))
                    vt = VT_e if b2 < 4 else VT_o
                    src = trview[:, :].rearrange("p (t c) -> p t c", c=64)
                    dst = vt[:, :].rearrange("p (t c) -> p t c", c=80)[
                        :, (b2 % 4) * 16:(b2 % 4 + 1) * 16, 0:64]
                    v.tensor_copy(dst, src).then_inc(ve.h, 1)
                for qc in range(NQC):
                    v.wait_ge(pe.h, pe_av_g((p * NQC + qc) * 64 + 63))
                    for eo, po in ((0, ps_oE), (1, ps_oO)):
                        lin = 8 * p + 2 * qc + eo
                        if lin >= 4:
                            v.wait_ge(gs.h, gs_d(lin - 4))
                        with nc.allow_low_precision(reason="softmax denominators in bf16"):
                            v.tensor_copy(D_st[2 * (qc % 2) + eo][:, :],
                                          po[64:65, :]).then_inc(ve.h, 1)

    ld.n = ld0 + 16 + 16 * 2 * NP
    gs.n = gs0c + 16 * 2 * NQC * NP
    pe.n = pe0 + PEP * NP
    ve.n = ve0 + VEP * NP
    se.n = se0 + SEP * NP

    pC2ps.close()
    pCE = ExitStack()
    ps_rb = pCE.enter_context(nc.psum_tensor("ps_rb", [64, OWN], F32))
    pe0e, ve0e = pe.n, ve.n
    with nc.Block("C2E") as blk:
        @blk.vector
        def _(v):
            v.wait_ge(gs.h, gs.n)
            with nc.allow_low_precision(reason="bf16 reciprocal of softmax denominators"):
                v.reciprocal(D_all[:, :], D_all[:, :]).then_inc(ve.h, 1)
            v.wait_ge(ve.h, ve0e + 1)
            v.engine_nop().then_inc(ve.h, 1)
            for h in range(2 * NP):
                jb, p0 = h // 2, (h % 2) * 64
                v.wait_ge(pe.h, pe0e + h + 1)
                v.tensor_tensor(O_all[p0:p0 + 64, jb * OWN:(jb + 1) * OWN],
                                O_all[p0:p0 + 64, jb * OWN:(jb + 1) * OWN],
                                ps_rb[:, :], ALU.mult).then_inc(ve.h, 1)

        @blk.tensor
        def _(t):
            t.wait_ge(ld.h, ld0 + 16)
            t.wait_ge(se.h, se.n)
            t.wait_ge(ve.h, ve0e + 2)
            for h in range(2 * NP):
                if h >= 1:
                    t.wait_ge(ve.h, ve0e + 2 + h)
                for tc in range(4):
                    ins = t.matmul(ps_rb[:, tc * 512:(tc + 1) * 512],
                                   sel16_sb[:, h * 64:(h + 1) * 64],
                                   D_all[:, tc * 512:(tc + 1) * 512],
                                   start=True, stop=True)
                ins.then_inc(pe.h, 1)
    pe.n = pe0e + 2 * NP
    ve.n = ve0e + 2 + 2 * NP
    pCE.close()
    pC2.close()

    if "O" in taps:
        tap_dump([(dbg["O"], O_all[:, :])])
    if stop_after == "C":
        return finish()

    pD = ExitStack()
    psb = lambda name, shape, dt: pD.enter_context(nc.sbuf_tensor(name, shape, dt))
    fc_sb = psb("fc_sb", [128, 8 * D], BF16)
    g_buf = [psb(f"g_buf{i}", [128, OWN], BF16) for i in range(2)]
    xT_buf = [psb(f"xT_buf{i}", [128, OWN], F32) for i in range(2)]
    s_buf = psb("s_buf", [128, OWN], F32)
    o_stage = [psb(f"o_stage{i}", [128, OWN], F32) for i in range(2)]
    ps_fc = [pD.enter_context(nc.psum_tensor(f"ps_fc{i}", [128, OWN], F32))
             for i in range(2)]

    ld0, gs0, pe0, ve0, sp0, se0 = ld.n, gs.n, pe.n, ve.n, sp.n, se.n
    ve_D = lambda jb: ve0 + 3 * jb + 3

    with nc.Block("D") as blk:
        @blk.gpsimd
        def _(g):
            _wt_dma(g, fc_sb, fc_wT[:, :], 8).then_inc(gs.h, 16)

        @blk.sync
        def _(s):
            for jb in range(8):
                if jb == 0:
                    s.wait_ge(sp.h, SP_G_DONE)
                if jb >= 2:
                    s.wait_ge(ve.h, ve_D(jb - 2))
                s.dma_start(g_buf[jb % 2][:, :],
                            g_hbm[:, jb * OWN:(jb + 1) * OWN]).then_inc(ld.h, 16)
                s.dma_start(xT_buf[jb % 2][:, :],
                            xT_own[jb * 128:(jb + 1) * 128, :]).then_inc(ld.h, 16)

        @blk.tensor
        def _(t):
            t.wait_ge(gs.h, gs0 + 16)
            for jb in range(8):
                if jb >= 2:
                    t.wait_ge(ve.h, ve_D(jb - 2) - 1)
                for j in range(8):
                    lhsT = fc_sb[:, j * D + jb * 128: j * D + (jb + 1) * 128]
                    for tc in range(4):
                        ins = t.matmul(ps_fc[jb % 2][:, tc * 512:(tc + 1) * 512],
                                       lhsT,
                                       O_all[:, j * OWN + tc * 512: j * OWN + (tc + 1) * 512],
                                       start=(j == 0), stop=(j == 7))
                ins.then_inc(pe.h, 1)

        @blk.vector
        def _(v):
            for jb in range(8):
                v.wait_ge(pe.h, pe0 + jb + 1)
                v.wait_ge(ld.h, ld0 + 16 * (2 * jb + 2))
                if jb >= 2:
                    v.wait_ge(sp.h, sp0 + 16 * (jb - 1))
                v.scalar_tensor_tensor(s_buf[:, :], ps_fc[jb % 2][:, :],
                                       fc_b_sb[:, jb:jb + 1],
                                       xn_own[:, jb * OWN:(jb + 1) * OWN],
                                       ALU.add, ALU.add).then_inc(ve.h, 1)
                v.wait_ge(ve.h, ve_D(jb) - 2)
                v.tensor_tensor(s_buf[:, :], s_buf[:, :], g_buf[jb % 2][:, :], ALU.mult
                                ).then_inc(ve.h, 1)
                v.wait_ge(ve.h, ve_D(jb) - 1)
                v.tensor_tensor(o_stage[jb % 2][:, :], s_buf[:, :], xT_buf[jb % 2][:, :],
                                ALU.add).then_inc(ve.h, 1)

        @blk.scalar
        def _(sc):
            for jb in range(8):
                sc.wait_ge(ve.h, ve_D(jb))
                sc.dma_start(out_T[jb * 128:(jb + 1) * 128, :],
                             o_stage[jb % 2][:, :]).then_inc(sp.h, 16)

    ld.n = ld0 + 16 * 16
    gs.n = gs0 + 16
    pe.n = pe0 + 8
    ve.n = ve0 + 24
    sp.n = sp0 + 16 * 8
    pD.close()
    return finish()


def _prep_weights(qkv_w, qkv_bv, fc_w, fc_bv, rt_w, rt_bv, gup, gup_bv, dn, dn_bv):
    qperm = np.concatenate([np.arange(h * 192, h * 192 + 64) for h in range(16)])
    perm = np.concatenate([qperm, qperm + 64, qperm + 128])
    qkv_wT = np.ascontiguousarray(qkv_w[perm].T)
    qkv_b_arr = np.ascontiguousarray(qkv_bv[perm].reshape(24, 128).T)
    fc_wT = np.ascontiguousarray(fc_w.T)
    fc_b_arr = np.ascontiguousarray(fc_bv.reshape(8, 128).T)
    rt_wT = np.ascontiguousarray(rt_w.T)
    rt_b_arr = np.ascontiguousarray(rt_bv.reshape(1, 8))
    iperm = np.empty(4096, dtype=np.int64)
    blk = np.arange(64)
    for b in range(32):
        iperm[b * 128: b * 128 + 64] = b * 64 + blk
        iperm[b * 128 + 64: (b + 1) * 128] = 2048 + b * 64 + blk
    gup_wT = np.ascontiguousarray(gup[:, iperm, :].transpose(0, 2, 1))
    gup_b_arr = np.ascontiguousarray(
        gup_bv[:, iperm].reshape(8, 32, 128).transpose(2, 0, 1).reshape(128, 256))
    dn_wT = np.ascontiguousarray(dn.transpose(0, 2, 1))
    return dict(qkv_wT=qkv_wT, qkv_b=qkv_b_arr, fc_wT=fc_wT, fc_b=fc_b_arr,
                rt_wT=rt_wT, rt_b=rt_b_arr, gup_wT=gup_wT, gup_b=gup_b_arr,
                dn_wT=dn_wT, dn_b=np.ascontiguousarray(dn_bv),
                ident=np.eye(128, dtype=np.float32).astype(ml_dtypes.bfloat16),
                identf=np.eye(128, dtype=np.float32),
                sel=np.repeat(np.eye(8, dtype=np.float32), 128, axis=1
                              ).astype(ml_dtypes.bfloat16),
                sel16=np.repeat(np.eye(16, dtype=np.float32), 64, axis=1
                                ).astype(ml_dtypes.bfloat16))


def make_in_maps(inputs):
    x = np.asarray(inputs["L0_attn_x"], np.float32)
    w = _prep_weights(
        np.asarray(inputs["L0_attn_qkv_w"], np.float32),
        np.asarray(inputs["L0_attn_qkv_b"], np.float32),
        np.asarray(inputs["L0_attn_fc_w"], np.float32),
        np.asarray(inputs["L0_attn_fc_b"], np.float32),
        np.asarray(inputs["L0_ffn_router_weight"], np.float32),
        np.asarray(inputs["L0_ffn_router_bias"], np.float32),
        np.asarray(inputs["L0_ffn_gate_up_proj"], np.float32),
        np.asarray(inputs["L0_ffn_gate_up_proj_bias"], np.float32),
        np.asarray(inputs["L0_ffn_down_proj"], np.float32),
        np.asarray(inputs["L0_ffn_down_proj_bias"], np.float32))
    in_maps = []
    for c in range(8):
        b, q0 = c // 4, (c % 4) * OWN
        xb = x[b]
        x_rot = np.ascontiguousarray(np.concatenate([xb[q0:], xb[:q0]], axis=0))
        m = dict(w)
        m["x_full"] = x_rot
        m["xT_own"] = np.ascontiguousarray(x_rot[:OWN].T)
        in_maps.append(m)
    return in_maps


def kernel(**inputs):
    in_maps = make_in_maps(inputs)
    nc, _ = build_program()
    res = run_bass_kernel_spmd(nc, in_maps, core_ids=list(range(8)))
    out = np.empty((2, SEQ, D), np.float32)
    for c in range(8):
        b, q0 = c // 4, (c % 4) * OWN
        out[b, q0:q0 + OWN, :] = np.ascontiguousarray(res.results[c]["out_T"]).T
    return out


# version 12
# speedup vs baseline: 1.3208x; 1.0308x over previous
import math
import numpy as np
import ml_dtypes
from contextlib import ExitStack

import concourse.bass as bass
import concourse.mybir as mybir
from concourse.bass_utils import run_bass_kernel_spmd

F32 = mybir.dt.float32
BF16 = mybir.dt.bfloat16
AF = mybir.ActivationFunctionType
ALU = mybir.AluOpType
AX = mybir.AxisListType

D = 1024
SEQ = 8192
OWN = 2048
H = 16
NE = 8
NTT = SEQ // 128
EXP_SHIFT = -20.0
SILU_CLIP = 11.914 / (1.0 + math.exp(-11.914))
INV_1702 = 1.0 / 1.702
LN_EPS = 1e-5
C2_HEADS = H



def _wt_dma(eng, dst_sb, src_ap, nj):
    src3 = src_ap.rearrange("(j p) c -> p j c", p=128)
    dst3 = dst_sb[:, :].rearrange("p (j c) -> p j c", j=nj)
    return eng.dma_start(dst3, src3)

class Sem:
    def __init__(self, nc, es, name):
        self.h = es.enter_context(nc.semaphore(name))
        self.n = 0


def build_program(stop_after=None, taps=()):
    nc = bass.Bass("TRN2", target_bir_lowering=False)
    es = ExitStack()

    def din(name, shape, dt=F32):
        return nc.dram_tensor(name, shape, dt, kind="ExternalInput").ap()

    x_full = din("x_full", [SEQ, D])
    xT_own = din("xT_own", [D, OWN])
    qkv_wT = din("qkv_wT", [D, 3 * D])
    qkv_b = din("qkv_b", [128, 24])
    fc_wT = din("fc_wT", [D, D])
    fc_b = din("fc_b", [128, 8])
    rt_wT = din("rt_wT", [D, NE])
    rt_b = din("rt_b", [1, NE])
    gup_wT = din("gup_wT", [NE, D, 4096])
    gup_b = din("gup_b", [128, NE * 32])
    dn_wT = din("dn_wT", [NE, 2048, D])
    dn_b = din("dn_b", [NE, D])
    ident_in = din("ident", [128, 128], BF16)
    identf_in = din("identf", [128, 128], F32)
    sel_in = din("sel", [NE, NE * 128], BF16)
    sel16_in = din("sel16", [16, 16 * 64], BF16)

    out_T = nc.dram_tensor("out_T", [D, OWN], F32, kind="ExternalOutput").ap()

    def scratch(name, shape, dt, tap):
        kind = "ExternalOutput" if tap in taps else "Internal"
        return nc.dram_tensor(name, shape, dt, kind=kind).ap()

    K_hbm = scratch("K_hbm", [D, SEQ], BF16, "KV")
    V_hbm = scratch("V_hbm", [D, SEQ], BF16, "KV")
    Q_hbm = scratch("Q_hbm", [D, OWN], BF16, "Qs")
    g_hbm = scratch("g_hbm", [128, 8 * OWN], BF16, "g")
    dbg = {}
    for t, shape, dt in [("xn_own", [128, 8 * OWN], BF16), ("logits", [128, 128], F32),
                         ("comb", [8, OWN], BF16), ("y", [128, 8 * OWN], F32),
                         ("Q", [128, 8 * OWN], BF16), ("O", [128, 8 * OWN], BF16)]:
        if t in taps:
            dbg[t] = nc.dram_tensor("dbg_" + t, shape, dt, kind="ExternalOutput").ap()

    ld = Sem(nc, es, "ld")
    ldx = Sem(nc, es, "ldx")
    gs = Sem(nc, es, "gs")
    sp = Sem(nc, es, "sp")
    pe = Sem(nc, es, "pe")
    ve = Sem(nc, es, "ve")
    se = Sem(nc, es, "se")

    _tapn = [0]
    def tap_dump(name_pairs):
        _tapn[0] += 1
        with nc.Block(f"dbg{_tapn[0]}") as blk:
            @blk.sync
            def _(s):
                n = ld.n
                for dst, src in name_pairs:
                    s.dma_start(dst, src).then_inc(ld.h, 16)
                    n += 16
                s.wait_ge(ld.h, n)
        ld.n += 16 * len(name_pairs)

    def finish():
        with nc.Block("fin") as blk:
            @blk.sync
            def _(s):
                s.wait_ge(sp.h, sp.n)
        es.close()
        return nc, dbg

    sbp = lambda name, shape, dt: es.enter_context(nc.sbuf_tensor(name, shape, dt))
    qkv_b_sb = sbp("qkv_b_sb", [128, 24], F32)
    fc_b_sb = sbp("fc_b_sb", [128, 8], F32)
    rt_b_sb = sbp("rt_b_sb", [1, NE], BF16)
    gup_b_sb = sbp("gup_b_sb", [128, NE * 32], F32)
    dn_b_sb = sbp("dn_b_sb", [8, D], BF16)
    ident = sbp("ident_sb", [128, 128], BF16)
    identf = sbp("identf_sb", [128, 128], F32)
    sel_sb = sbp("sel_sb", [NE, NE * 128], BF16)
    ones1_bf = sbp("ones1_bf", [1, 128], BF16)
    ones1_f = sbp("ones1_f", [1, 128], F32)
    qbs = sbp("qbs", [128, 8], F32)
    xn_own = sbp("xn_own", [128, 8 * OWN], BF16)
    c_eps = sbp("c_eps", [128, 1], F32)
    c_shift = sbp("c_shift", [128, 1], F32)
    nc.const_aps.register(F32, LN_EPS, c_eps[:, 0:1])
    nc.const_aps.register(F32, EXP_SHIFT, c_shift[:, 0:1])

    combT = sbp("combT", [8, OWN], BF16)
    pWQ = ExitStack()
    wq_sb = pWQ.enter_context(nc.sbuf_tensor("wq_sb", [128, 8 * D], BF16))
    pA = ExitStack()
    psb = lambda name, shape, dt: pA.enter_context(nc.sbuf_tensor(name, shape, dt))
    pps = lambda name, shape, dt: pA.enter_context(nc.psum_tensor(name, shape, dt))
    xnR = psb("xnR", [128, 3 * 8 * 1024], BF16)
    x_buf = [psb(f"x_buf{i}", [128, D], F32) for i in range(4)]
    sq_buf = psb("sq_buf", [128, D], F32)
    xn_buf = [psb(f"xn_buf{i}", [128, D], BF16) for i in range(8)]
    st_s1 = psb("st_s1", [128, 1], F32)
    st_s2 = psb("st_s2", [128, 1], F32)
    st_m = psb("st_m", [128, 1], F32)
    st_v = psb("st_v", [128, 1], F32)
    st_msq = psb("st_msq", [128, 1], F32)
    st_sd = psb("st_sd", [128, 1], F32)
    st_rstd = psb("st_rstd", [128, 1], F32)
    st_mrn = psb("st_mrn", [128, 1], F32)
    wkv_sb = psb("wkv_sb", [128, 16 * D], BF16)
    kv_stage = [psb(f"kv_stage{i}", [128, 1024], BF16) for i in range(2)]
    rt_w_sb = psb("rt_w_sb", [128, 8 * NE], BF16)
    logits_sb = psb("logits_sb", [128, 128], F32)
    lwork = psb("lwork", [128, 8], F32)
    m4 = psb("m4", [128, 4], F32)
    mtmp = psb("mtmp", [128, 8], F32)
    exb = psb("exb", [128, 8], F32)
    negm = psb("negm", [128, 1], F32)
    den = psb("den", [128, 1], F32)
    rden = psb("rden", [128, 1], F32)
    pA1ps = ExitStack()
    ps_tr = [pA1ps.enter_context(nc.psum_tensor(f"ps_tr{i}", [128, D], BF16)) for i in range(2)]
    ps_kv = [pA1ps.enter_context(nc.psum_tensor(f"ps_kv{i}", [128, 1024], F32)) for i in range(2)]
    ps_r = pA1ps.enter_context(nc.psum_tensor("ps_r", [128, 128], F32))
    ps_ct = pA1ps.enter_context(nc.psum_tensor("ps_ct", [8, 512], F32))

    NSMALL = 6
    D_EV = 6
    ld_small = ld.n + 16 * NSMALL
    GS_RTB = gs.n + 16
    GS_DNB = gs.n + 32
    GS_WK = gs.n + 48
    GS_WV = gs.n + 64
    GS_RTW = gs.n + 80
    GS_WQ = gs.n + 96

    PE, VE, SE, SPD, LDX = {}, {}, {}, {}, {}
    LGT = 22
    CTMAP = {60: (0, 1), 61: (2, 3)}
    tbpos = {24 + 2 * b: b for b in range(16)}
    n = pe.n
    for tch in range(8):
        for ti in range(8):
            n += 1; PE['tr', tch * 8 + ti] = n
            if tch == 2 and ti == 1:
                n += 1; PE['rt'] = n
        if tch == 6:
            for i in range(4):
                n += 1; PE['ct', i] = n
        for g_ in range(tch * 16, tch * 16 + 16):
            n += 1; PE['kv', g_] = n
    PE_A_END = n
    TBK = ('cp', 'max0', 'msk0', 'sub0', 'max1', 'msk1', 'sub1', 'max2', 'msk2',
           'sub2', 'max3', 'msk3', 'sub3', 'selm', 'negm', 'exm', 'den', 'rden', 'wr')
    n = ve.n
    n += 1; VE['init'] = n
    for tt in range(NTT):
        for k in ('sq', 'red', 'msq', 'st1', 'rcp', 'st2'):
            n += 1; VE[k, tt] = n
        for i_ in CTMAP.get(tt, ()):
            n += 1; VE['ctcp', i_] = n
        if tt >= D_EV:
            n += 1; VE['evac', tt - D_EV] = n
            if tt - D_EV == 7:
                n += 1; VE['xncp', 0] = n
            if tt - D_EV == 15:
                n += 1; VE['xncp', 1] = n
        if tt == LGT:
            n += 1; VE['lgcp'] = n
        if tt in tbpos:
            for k in TBK:
                n += 1; VE[k, tbpos[tt]] = n
    for tt in range(NTT - D_EV, NTT):
        n += 1; VE['evac', tt] = n
    VE['comb_all'] = VE['wr', 15]
    VE_A_END = n
    n = se.n
    for tt in range(NTT):
        n += 1; SE['sd', tt] = n
        n += 1; SE['xn', tt] = n
        tch, ti = tt // 8, tt % 8
        if ti == 5 and tch >= 1:
            for g_ in range((tch - 1) * 16, tch * 16):
                n += 1; SE['kve', g_] = n
        if tt == NTT - 1:
            for g_ in range(7 * 16, 8 * 16):
                n += 1; SE['kve', g_] = n
        if tt in tbpos:
            n += 1; SE['exp', tbpos[tt]] = n
    SE_A_END = n
    n = sp.n
    for g_ in range(128):
        n += 16; SPD['kv', g_] = n
    SP_A_END = n
    n = ldx.n
    for tt in range(NTT):
        n += 16; LDX[tt] = n

    def _ring(s):
        return xnR[:, s * 8192:(s + 1) * 8192].rearrange("p (j t) -> p j t", j=8)

    with nc.Block("A") as blk:
        @blk.sync
        def _(s):
            s.dma_start(qkv_b_sb[:, :], qkv_b).then_inc(ld.h, 16)
            s.dma_start(fc_b_sb[:, :], fc_b).then_inc(ld.h, 16)
            s.dma_start(gup_b_sb[:, :], gup_b).then_inc(ld.h, 16)
            s.dma_start(ident[:, :], ident_in).then_inc(ld.h, 16)
            s.dma_start(identf[:, :], identf_in).then_inc(ld.h, 16)
            s.dma_start(sel_sb[:, :], sel_in).then_inc(ld.h, 16)
            for tt in range(NTT):
                if tt >= 4:
                    s.wait_ge(se.h, SE['xn', tt - 4])
                s.dma_start(x_buf[tt % 4][:, :],
                            x_full[tt * 128:(tt + 1) * 128, :]).then_inc(ldx.h, 16)

        @blk.gpsimd
        def _(g):
            g.dma_start(rt_b_sb[:, :], rt_b).then_inc(gs.h, 16)
            g.dma_start(dn_b_sb[:, :], dn_b).then_inc(gs.h, 16)
            _wt_dma(g, wkv_sb[:, 0:8 * D], qkv_wT[:, D:2 * D], 8).then_inc(gs.h, 16)
            _wt_dma(g, wkv_sb[:, 8 * D:16 * D], qkv_wT[:, 2 * D:3 * D], 8).then_inc(gs.h, 16)
            _wt_dma(g, rt_w_sb[:, :], rt_wT[:, :], 8).then_inc(gs.h, 16)
            _wt_dma(g, wq_sb[:, :], qkv_wT[:, 0:D], 8).then_inc(gs.h, 16)

        @blk.tensor
        def _(t):
            t.wait_ge(ld.h, ld_small)
            for tch in range(8):
                for ti in range(8):
                    tt = tch * 8 + ti
                    t.wait_ge(se.h, SE['xn', tt])
                    if tt >= 2:
                        t.wait_ge(ve.h, VE['evac', tt - 2])
                    for j in range(8):
                        ins = t.transpose(ps_tr[tt % 2][:, j * 128:(j + 1) * 128],
                                          xn_buf[tt % 8][:, j * 128:(j + 1) * 128],
                                          ident[:, :])
                    ins.then_inc(pe.h, 1)
                    if tch == 2 and ti == 1:
                        t.wait_ge(gs.h, GS_RTW)
                        t.wait_ge(ve.h, VE['xncp', 1])
                        for tb in range(16):
                            t.matmul(ps_r[:, tb * 8:(tb + 1) * 8], ones1_bf[:, :],
                                     rt_b_sb[:, :], start=True, stop=False)
                            for j in range(8):
                                ins = t.matmul(ps_r[:, tb * 8:(tb + 1) * 8],
                                               xn_own[:, j * OWN + tb * 128: j * OWN + (tb + 1) * 128],
                                               rt_w_sb[:, j * NE:(j + 1) * NE],
                                               start=False, stop=(j == 7))
                        ins.then_inc(pe.h, 1)
                if tch == 6:
                    for tbg in range(4):
                        t.wait_ge(ve.h, VE['comb_all'] if tbg == 0
                                  else VE['ctcp', tbg - 1])
                        for i in range(4):
                            tb = tbg * 4 + i
                            ins = t.transpose(ps_ct[:, i * 128:(i + 1) * 128],
                                              logits_sb[:, tb * 8:(tb + 1) * 8],
                                              identf[:, :])
                        ins.then_inc(pe.h, 1)
                t.wait_ge(gs.h, GS_WV)
                t.wait_ge(ve.h, VE['evac', tch * 8 + 7])
                for g_ in range(tch * 16, tch * 16 + 16):
                    proj, jb = (g_ % 16) // 8, g_ % 8
                    if g_ >= 2:
                        t.wait_ge(se.h, SE['kve', g_ - 2])
                    base = (tch % 3) * 8192
                    for j in range(8):
                        lhsT = wkv_sb[:, proj * 8 * D + j * D + jb * 128:
                                      proj * 8 * D + j * D + (jb + 1) * 128]
                        for tc in range(2):
                            ins = t.matmul(ps_kv[g_ % 2][:, tc * 512:(tc + 1) * 512],
                                           lhsT,
                                           xnR[:, base + j * 1024 + tc * 512:
                                               base + j * 1024 + (tc + 1) * 512],
                                           start=(j == 0), stop=(j == 7))
                    ins.then_inc(pe.h, 1)

        @blk.scalar
        def _(sc):
            for tt in range(NTT):
                tch, ti = tt // 8, tt % 8
                sc.wait_ge(ve.h, VE['st1', tt])
                sc.activation(st_sd[:, :], st_v[:, :], AF.Sqrt, bias=LN_EPS,
                              scale=1.0).then_inc(se.h, 1)
                sc.wait_ge(ve.h, VE['st2', tt])
                if tt >= 8:
                    sc.wait_ge(pe.h, PE['tr', tt - 8])
                sc.activation(xn_buf[tt % 8][:, :], x_buf[tt % 4][:, :], AF.Identity,
                              bias=st_mrn[:, 0:1], scale=st_rstd[:, 0:1]
                              ).then_inc(se.h, 1)
                kvs = []
                if ti == 5 and tch >= 1:
                    kvs = list(range((tch - 1) * 16, tch * 16))
                if tt == NTT - 1:
                    kvs += list(range(7 * 16, 8 * 16))
                for g_ in kvs:
                    proj, jb = (g_ % 16) // 8, g_ % 8
                    gtch = g_ // 16
                    sc.wait_ge(pe.h, PE['kv', g_])
                    if g_ >= 2:
                        sc.wait_ge(sp.h, SPD['kv', g_ - 2])
                    bias_col = 8 * (proj + 1) + jb
                    sc.activation(kv_stage[g_ % 2][:, :], ps_kv[g_ % 2][:, :],
                                  AF.Identity, bias=qkv_b_sb[:, bias_col:bias_col + 1],
                                  scale=1.0).then_inc(se.h, 1)
                    tgt = K_hbm if proj == 0 else V_hbm
                    sc.dma_start(tgt[jb * 128:(jb + 1) * 128,
                                     gtch * 1024:(gtch + 1) * 1024],
                                 kv_stage[g_ % 2][:, :]).then_inc(sp.h, 16)
                if tt in tbpos:
                    b = tbpos[tt]
                    sc.wait_ge(ve.h, VE['negm', b])
                    sc.activation(exb[:, :], logits_sb[:, b * 8:(b + 1) * 8], AF.Exp,
                                  bias=negm[:, 0:1], scale=1.0).then_inc(se.h, 1)

        @blk.vector
        def _(v):
            v.memset(ones1_bf[:, :], 1.0)
            v.memset(ones1_f[:, :], 1.0)
            v.memset(c_eps[:, :], LN_EPS)
            v.memset(c_shift[:, :], EXP_SHIFT)
            v.wait_ge(ld.h, ld_small)
            v.tensor_scalar(qbs[:, :], qkv_b_sb[:, 0:8], 0.125, None,
                            ALU.mult).then_inc(ve.h, 1)

            def evac(k):
                v.wait_ge(pe.h, PE['tr', k])
                src = ps_tr[k % 2][:, :].rearrange("p (j t) -> p j t", j=8)
                dst = _ring((k // 8) % 3)[:, :, (k % 8) * 128:(k % 8 + 1) * 128]
                v.tensor_copy(dst, src).then_inc(ve.h, 1)

            def tb_unit(b):
                lg = logits_sb[:, b * 8:(b + 1) * 8]
                v.tensor_copy(lwork[:, :], lg).then_inc(ve.h, 1)
                v.wait_ge(ve.h, VE['cp', b])
                for i in range(4):
                    v.reduce_max(m4[:, i:i + 1], lwork[:, :], axis=AX.X).then_inc(ve.h, 1)
                    v.wait_ge(ve.h, VE[f'max{i}', b])
                    v.tensor_scalar(mtmp[:, :], lwork[:, :], m4[:, i:i + 1], -2e30,
                                    ALU.is_ge, ALU.mult).then_inc(ve.h, 1)
                    v.wait_ge(ve.h, VE[f'msk{i}', b])
                    v.tensor_tensor(lwork[:, :], lwork[:, :], mtmp[:, :],
                                    ALU.add).then_inc(ve.h, 1)
                    v.wait_ge(ve.h, VE[f'sub{i}', b])
                v.tensor_scalar(mtmp[:, :], lg, m4[:, 3:4], None,
                                ALU.is_ge).then_inc(ve.h, 1)
                v.tensor_scalar(negm[:, :], m4[:, 0:1], -1.0, None,
                                ALU.mult).then_inc(ve.h, 1)
                v.wait_ge(se.h, SE['exp', b])
                v.tensor_tensor(exb[:, :], exb[:, :], mtmp[:, :],
                                ALU.mult).then_inc(ve.h, 1)
                v.wait_ge(ve.h, VE['exm', b])
                v.reduce_sum(den[:, :], exb[:, :], axis=AX.X).then_inc(ve.h, 1)
                v.wait_ge(ve.h, VE['den', b])
                v.reciprocal(rden[:, :], den[:, :]).then_inc(ve.h, 1)
                v.wait_ge(ve.h, VE['rden', b])
                v.tensor_scalar(lg, exb[:, :], rden[:, 0:1], None,
                                ALU.mult).then_inc(ve.h, 1)

            for tt in range(NTT):
                x = x_buf[tt % 4]
                v.wait_ge(ldx.h, LDX[tt])
                v.tensor_tensor(sq_buf[:, :], x[:, :], x[:, :], ALU.mult).then_inc(ve.h, 1)
                v.reduce_sum(st_s1[:, :], x[:, :], axis=AX.X)
                v.wait_ge(ve.h, VE['sq', tt])
                v.reduce_sum(st_s2[:, :], sq_buf[:, :], axis=AX.X).then_inc(ve.h, 1)
                v.wait_ge(ve.h, VE['red', tt])
                v.tensor_scalar(st_m[:, :], st_s1[:, :], 1.0 / D, None, ALU.mult)
                v.tensor_scalar(st_v[:, :], st_s2[:, :], 1.0 / D, None, ALU.mult)
                v.tensor_tensor(st_msq[:, :], st_m[:, :], st_m[:, :],
                                ALU.mult).then_inc(ve.h, 1)
                v.wait_ge(ve.h, VE['msq', tt])
                v.tensor_tensor(st_v[:, :], st_v[:, :], st_msq[:, :],
                                ALU.subtract).then_inc(ve.h, 1)
                v.wait_ge(se.h, SE['sd', tt])
                v.reciprocal(st_rstd[:, :], st_sd[:, :]).then_inc(ve.h, 1)
                v.wait_ge(ve.h, VE['rcp', tt])
                v.scalar_tensor_tensor(st_mrn[:, :], st_m[:, :], -1.0, st_rstd[:, :],
                                       ALU.mult, ALU.mult).then_inc(ve.h, 1)
                for i in CTMAP.get(tt, ()):
                    v.wait_ge(pe.h, PE['ct', i])
                    v.tensor_copy(combT[:, i * 512:(i + 1) * 512],
                                  ps_ct[:, :]).then_inc(ve.h, 1)
                if tt >= D_EV:
                    k = tt - D_EV
                    evac(k)
                    if k == 7 or k == 15:
                        s_ = k // 8
                        dst = xn_own[:, :].rearrange("p (j t) -> p j t", j=8)[
                            :, :, s_ * 1024:(s_ + 1) * 1024]
                        v.tensor_copy(dst, _ring(s_)).then_inc(ve.h, 1)
                if tt == LGT:
                    v.wait_ge(pe.h, PE['rt'])
                    v.tensor_copy(logits_sb[:, :], ps_r[:, :]).then_inc(ve.h, 1)
                if tt in tbpos:
                    tb_unit(tbpos[tt])
            for k in range(NTT - D_EV, NTT):
                evac(k)

    ld.n = ld_small
    ldx.n = LDX[NTT - 1]
    gs.n = GS_WQ
    ve.n = VE_A_END
    se.n = SE_A_END
    pe.n = PE_A_END
    sp.n = SP_A_END
    SP_KV_DONE = sp.n
    pA1ps.close()
    pA.close()

    if "xn_own" in taps:
        tap_dump([(dbg["xn_own"], xn_own[:, :])])
    if stop_after == "A":
        pWQ.close()
        return finish()

    pC1 = ExitStack()
    q_stage = [pC1.enter_context(nc.sbuf_tensor(f"q_stage{i}", [128, OWN], BF16))
               for i in range(2)]
    ps_q = [pC1.enter_context(nc.psum_tensor(f"ps_q{i}", [128, OWN], F32))
            for i in range(2)]
    pe0, se0, sp0 = pe.n, se.n, sp.n
    with nc.Block("C1") as blk:
        @blk.tensor
        def _(t):
            t.wait_ge(gs.h, GS_WQ)
            t.wait_ge(se.h, se0)
            t.wait_ge(ve.h, VE_A_END)
            for jb in range(8):
                if jb >= 2:
                    t.wait_ge(se.h, se0 + jb - 1)
                for j in range(8):
                    lhsT = wq_sb[:, j * D + jb * 128: j * D + (jb + 1) * 128]
                    for tc in range(4):
                        ins = t.matmul(ps_q[jb % 2][:, tc * 512:(tc + 1) * 512],
                                       lhsT,
                                       xn_own[:, j * OWN + tc * 512: j * OWN + (tc + 1) * 512],
                                       start=(j == 0), stop=(j == 7))
                ins.then_inc(pe.h, 1)

        @blk.scalar
        def _(sc):
            for jb in range(8):
                sc.wait_ge(pe.h, pe0 + jb + 1)
                if jb >= 2:
                    sc.wait_ge(sp.h, sp0 + 16 * (jb - 1))
                sc.activation(q_stage[jb % 2][:, :], ps_q[jb % 2][:, :],
                              AF.Identity, bias=qbs[:, jb:jb + 1],
                              scale=0.125).then_inc(se.h, 1)
                sc.dma_start(Q_hbm[jb * 128:(jb + 1) * 128, :],
                             q_stage[jb % 2][:, :]).then_inc(sp.h, 16)
    pe.n = pe0 + 8
    se.n = se0 + 8
    sp.n = sp0 + 16 * 8
    SP_Q_DONE = sp.n
    pC1.close()
    pWQ.close()
    if stop_after == "C1":
        return finish()

    pB = ExitStack()
    psb = lambda name, shape, dt: pB.enter_context(nc.sbuf_tensor(name, shape, dt))
    y_sb = psb("y_sb", [128, 8 * OWN], F32)
    A_sb = psb("A_sb", [128, 16 * OWN], BF16)
    gup_buf = [psb(f"gup_buf{i}", [128, 8 * 256], BF16) for i in range(2)]
    dn_buf = [psb(f"dn_buf{i}", [128, 16 * 128], BF16) for i in range(2)]
    gsl_buf = [psb(f"gsl_buf{i}", [64, OWN], BF16) for i in range(2)]
    uc_buf = psb("uc_buf", [64, OWN], BF16)
    combB = psb("combB", [128, OWN], BF16)
    bsig = psb("bsig", [64, 32], F32)

    pB2ps = ExitStack()
    ps_gu = [pB2ps.enter_context(nc.psum_tensor(f"ps_gu{i}", [128, OWN], F32))
             for i in range(2)]

    pe0, ve0 = pe.n, ve.n
    with nc.Block("Y") as blk:
        @blk.tensor
        def _(t):
            t.wait_ge(gs.h, GS_DNB)
            t.wait_ge(se.h, se.n)
            t.wait_ge(ve.h, VE['ctcp', 3])
            for db in range(8):
                if db >= 2:
                    t.wait_ge(ve.h, ve0 + db - 1)
                for tc in range(4):
                    ins = t.matmul(ps_gu[db % 2][:, tc * 512:(tc + 1) * 512],
                                   dn_b_sb[:, db * 128:(db + 1) * 128],
                                   combT[:, tc * 512:(tc + 1) * 512],
                                   start=True, stop=True)
                ins.then_inc(pe.h, 1)

        @blk.vector
        def _(v):
            for db in range(8):
                v.wait_ge(pe.h, pe0 + db + 1)
                v.tensor_copy(y_sb[:, db * OWN:(db + 1) * OWN],
                              ps_gu[db % 2][:, :]).then_inc(ve.h, 1)
    pe.n = pe0 + 8
    ve.n = ve0 + 8
    VE_Y_END = ve.n

    gs0, pe0, ve0, se0 = gs.n, pe.n, ve.n, se.n
    PEE = 41
    pe_bcast = lambda e: pe0 + PEE * e + 1
    pe_gup = lambda e, b: pe0 + PEE * e + 1 + (b + 1)
    pe_dn = lambda e, db: pe0 + PEE * e + 33 + (db + 1)
    VEE = 138
    ve_combB = lambda e: ve0 + VEE * e + 1
    ve_bsig = lambda e: ve0 + VEE * e + 2
    ve_act = lambda e, b: ve0 + VEE * e + 2 + 4 * (b + 1)
    ve_yacc = lambda e, db: ve0 + VEE * e + 130 + (db + 1)
    se_gs = lambda e, b: se0 + 32 * e + b + 1
    gs_gup = lambda e, ch: gs0 + 16 * (24 * e + ch + 1)
    gs_dn = lambda e, db: gs0 + 16 * (24 * e + 16 + db + 1)

    with nc.Block("B2") as blk:
        @blk.gpsimd
        def _(g):
            for e in range(NE):
                for ch in range(16):
                    pv = (e, ch - 2) if ch >= 2 else (e - 1, ch + 14)
                    if pv[0] >= 0:
                        g.wait_ge(pe.h, pe_gup(pv[0], pv[1] * 2 + 1))
                    _wt_dma(g, gup_buf[ch % 2], gup_wT[e, :, ch * 256:(ch + 1) * 256],
                            8).then_inc(gs.h, 16)
                for db in range(8):
                    pv = (e, db - 2) if db >= 2 else (e - 1, db + 6)
                    if pv[0] >= 0:
                        g.wait_ge(pe.h, pe_dn(pv[0], pv[1]))
                    _wt_dma(g, dn_buf[db % 2], dn_wT[e, :, db * 128:(db + 1) * 128],
                            16).then_inc(gs.h, 16)

        @blk.tensor
        def _(t):
            for e in range(NE):
                if e > 0:
                    t.wait_ge(ve.h, ve_yacc(e - 1, 7))
                else:
                    t.wait_ge(ve.h, VE_Y_END)
                for tc in range(4):
                    ins = t.matmul(ps_gu[0][:, tc * 512:(tc + 1) * 512],
                                   sel_sb[:, e * 128:(e + 1) * 128],
                                   combT[:, tc * 512:(tc + 1) * 512],
                                   start=True, stop=True)
                ins.then_inc(pe.h, 1)
                t.wait_ge(ve.h, ve_combB(e))
                for b in range(32):
                    ch = b // 2
                    t.wait_ge(gs.h, gs_gup(e, ch))
                    if b >= 2:
                        t.wait_ge(ve.h, ve_act(e, b - 2))
                    for j in range(8):
                        lhsT = gup_buf[ch % 2][:, j * 256 + (b % 2) * 128:
                                               j * 256 + (b % 2) * 128 + 128]
                        for tc in range(4):
                            ins = t.matmul(ps_gu[b % 2][:, tc * 512:(tc + 1) * 512],
                                           lhsT,
                                           xn_own[:, j * OWN + tc * 512: j * OWN + (tc + 1) * 512],
                                           start=(j == 0), stop=(j == 7))
                    ins.then_inc(pe.h, 1)
                t.wait_ge(ve.h, ve_act(e, 31))
                for db in range(8):
                    t.wait_ge(gs.h, gs_dn(e, db))
                    if db >= 2:
                        t.wait_ge(ve.h, ve_yacc(e, db - 2))
                    for ic in range(16):
                        lhsT = dn_buf[db % 2][:, ic * 128:(ic + 1) * 128]
                        for tc in range(4):
                            ins = t.matmul(ps_gu[db % 2][:, tc * 512:(tc + 1) * 512],
                                           lhsT,
                                           A_sb[:, ic * OWN + tc * 512: ic * OWN + (tc + 1) * 512],
                                           start=(ic == 0), stop=(ic == 15))
                    ins.then_inc(pe.h, 1)

        @blk.scalar
        def _(sc):
            for e in range(NE):
                sc.wait_ge(ve.h, ve_bsig(e))
                for b in range(32):
                    sc.wait_ge(pe.h, pe_gup(e, b))
                    if b >= 2:
                        sc.wait_ge(ve.h, ve_act(e, b - 2))
                    sc.activation(gsl_buf[b % 2][:, :], ps_gu[b % 2][0:64, :], AF.Silu,
                                  bias=bsig[:, b:b + 1], scale=1.702).then_inc(se.h, 1)

        @blk.vector
        def _(v):
            for e in range(NE):
                v.wait_ge(pe.h, pe_bcast(e))
                v.tensor_scalar(combB[:, :], ps_gu[0][:, :], INV_1702, None,
                                ALU.mult).then_inc(ve.h, 1)
                v.tensor_scalar(bsig[:, :], gup_b_sb[0:64, e * 32:(e + 1) * 32], 1.702,
                                None, ALU.mult).then_inc(ve.h, 1)
                for b in range(32):
                    jj = b // 2
                    p0 = (b % 2) * 64
                    v.wait_ge(pe.h, pe_gup(e, b))
                    v.wait_ge(se.h, se_gs(e, b))
                    v.tensor_scalar(uc_buf[:, :], ps_gu[b % 2][64:128, :],
                                    gup_b_sb[64:128, e * 32 + b:e * 32 + b + 1], 7.0,
                                    ALU.add, ALU.min).then_inc(ve.h, 1)
                    v.wait_ge(ve.h, ve_act(e, b) - 3)
                    v.tensor_scalar(uc_buf[:, :], uc_buf[:, :], -7.0, None, ALU.max)
                    v.tensor_scalar(gsl_buf[b % 2][:, :], gsl_buf[b % 2][:, :],
                                    SILU_CLIP, None, ALU.min).then_inc(ve.h, 1)
                    v.wait_ge(ve.h, ve_act(e, b) - 2)
                    v.tensor_tensor(gsl_buf[b % 2][:, :], gsl_buf[b % 2][:, :],
                                    combB[0:64, :], ALU.mult).then_inc(ve.h, 1)
                    v.wait_ge(ve.h, ve_act(e, b) - 1)
                    v.scalar_tensor_tensor(A_sb[p0:p0 + 64, jj * OWN:(jj + 1) * OWN],
                                           uc_buf[:, :], 1.0, gsl_buf[b % 2][:, :],
                                           ALU.add, ALU.mult).then_inc(ve.h, 1)
                for db in range(8):
                    v.wait_ge(pe.h, pe_dn(e, db))
                    v.tensor_tensor(y_sb[:, db * OWN:(db + 1) * OWN],
                                    y_sb[:, db * OWN:(db + 1) * OWN],
                                    ps_gu[db % 2][:, :], ALU.add).then_inc(ve.h, 1)

    gs.n = gs0 + 16 * 24 * NE
    pe.n = pe0 + PEE * NE
    ve.n = ve0 + VEE * NE
    se.n = se0 + 32 * NE

    se0, sp0 = se.n, sp.n
    with nc.Block("Bg") as blk:
        @blk.scalar
        def _(sc):
            for jb in range(8):
                sc.wait_ge(ve.h, ve_yacc(NE - 1, jb))
                if jb >= 1:
                    sc.wait_ge(sp.h, sp0 + 32 * jb)
                sc.activation(gsl_buf[jb % 2][:, :], y_sb[0:64, jb * OWN:(jb + 1) * OWN],
                              AF.Silu, bias=0.0, scale=1.0)
                sc.activation(uc_buf[:, :], y_sb[64:128, jb * OWN:(jb + 1) * OWN],
                              AF.Silu, bias=0.0, scale=1.0).then_inc(se.h, 1)
                sc.wait_ge(se.h, se0 + jb + 1)
                sc.dma_start(g_hbm[0:64, jb * OWN:(jb + 1) * OWN],
                             gsl_buf[jb % 2][:, :]).then_inc(sp.h, 16)
                sc.dma_start(g_hbm[64:128, jb * OWN:(jb + 1) * OWN],
                             uc_buf[:, :]).then_inc(sp.h, 16)
    se.n = se0 + 8
    sp.n = sp0 + 16 * 16
    SP_G_DONE = sp.n

    if "y" in taps:
        tap_dump([(dbg["y"], y_sb[:, :])])
    pB2ps.close()
    pB.close()
    if stop_after == "B":
        return finish()

    O_all = es.enter_context(nc.sbuf_tensor("O_all", [128, 8 * OWN], BF16))
    pC2 = ExitStack()
    psb = lambda name, shape, dt: pC2.enter_context(nc.sbuf_tensor(name, shape, dt))
    K_sb = [psb(f"K_sb{i}", [128, SEQ], BF16) for i in range(2)]
    Q_sb = [psb(f"Q_sb{i}", [128, OWN], BF16) for i in range(2)]
    V_sb = psb("V_sb", [128, SEQ], BF16)
    VT_e = psb("VT_e", [128, 64 * 80], BF16)
    VT_o = psb("VT_o", [128, 64 * 80], BF16)
    PT = [psb(f"PT{i}", [128, 1024], BF16) for i in range(3)]
    D_all = psb("D_all", [16, OWN], BF16)
    D_st = [psb(f"D_st{i}", [1, 512], BF16) for i in range(4)]
    sel16_sb = psb("sel16_sb", [16, 16 * 64], BF16)
    pC2ps = ExitStack()
    ps_st = [pC2ps.enter_context(nc.psum_tensor(f"ps_st{i}", [128, 1024], F32))
             for i in range(3)]
    ps_oE = pC2ps.enter_context(nc.psum_tensor("ps_oE", [65, 512], F32))
    ps_oO = pC2ps.enter_context(nc.psum_tensor("ps_oO", [65, 512], F32))
    trview = ps_st[0][:, 0:512].bitcast(BF16)

    ld0, pe0, ve0, se0, gs0c = ld.n, pe.n, ve.n, se.n, gs.n
    NP = C2_HEADS // 2
    NQC = 4
    SE_ENTRY = se0
    PEP = 8 + NQC * 128
    SEP = NQC * 66
    tup = lambda G: (G // (NQC * 64),) + divmod(G % (NQC * 64), 64)
    pe_trb = lambda p, b2: pe0 + PEP * p + b2 + 1

    def pe_st_g(G):
        p, qc, kt = tup(G)
        return pe0 + PEP * p + 8 + qc * 128 + (1 if kt == 0 else 2 * kt)

    def pe_av_g(G):
        p, qc, kt = tup(G)
        return pe0 + PEP * p + 8 + qc * 128 + (2 * kt + 3 if kt < 63 else 128)

    def se_exp_g(G):
        p, qc, kt = tup(G)
        return se0 + SEP * p + 66 * qc + kt + 1

    se_oev = lambda p, qc, eo: se0 + SEP * p + 66 * qc + 64 + eo + 1
    VEP = 8 + 2 * NQC
    ve_vt = lambda p, b2: ve0 + VEP * p + b2 + 1
    ve_dst = lambda p, qc, eo: ve0 + VEP * p + 8 + 2 * qc + eo + 1
    ld_kv = lambda p: ld0 + 16 + 16 * 3 * (p + 1)
    gs_d = lambda lin: gs0c + 16 * (lin + 1)

    with nc.Block("C2") as blk:
        @blk.sync
        def _(s):
            s.dma_start(sel16_sb[:, :], sel16_in).then_inc(ld.h, 16)
            for p in range(NP):
                if p == 0:
                    s.wait_ge(sp.h, SP_KV_DONE)
                    s.wait_ge(sp.h, SP_Q_DONE)
                if p >= 2:
                    s.wait_ge(pe.h, pe_st_g(((p - 2) * NQC + NQC - 1) * 64 + 63))
                s.dma_start(K_sb[p % 2][:, :],
                            K_hbm[p * 128:(p + 1) * 128, :]).then_inc(ld.h, 16)
                s.dma_start(Q_sb[p % 2][:, :],
                            Q_hbm[p * 128:(p + 1) * 128, :]).then_inc(ld.h, 16)
                if p >= 1:
                    s.wait_ge(pe.h, pe_trb(p - 1, 7))
                s.dma_start(V_sb[:, :],
                            V_hbm[p * 128:(p + 1) * 128, :]).then_inc(ld.h, 16)

        @blk.gpsimd
        def _(g):
            for p in range(NP):
                for qc in range(NQC):
                    for eo in range(2):
                        g.wait_ge(ve.h, ve_dst(p, qc, eo))
                        g.dma_start(D_all[2 * p + eo:2 * p + eo + 1,
                                          qc * 512:(qc + 1) * 512],
                                    D_st[2 * (qc % 2) + eo][:, :]).then_inc(gs.h, 16)

        @blk.tensor
        def _(t):
            for p in range(NP):
                t.wait_ge(ld.h, ld_kv(p))
                t.wait_ge(se.h, se_exp_g(p * NQC * 64 - 1) if p > 0 else SE_ENTRY)
                for b2 in range(8):
                    if b2 > 0:
                        t.wait_ge(ve.h, ve_vt(p, b2 - 1))
                    half = b2 // 4
                    for i in range(16):
                        tt_ = (b2 % 4) * 16 + i
                        src_ = V_sb[half * 64:half * 64 + 64, tt_ * 128:(tt_ + 1) * 128]
                        idn = ident[0:64, 0:64] if half == 0 else ident[64:128, 64:128]
                        ins = t.transpose(trview[:, i * 64:(i + 1) * 64], src_, idn)
                    ins.then_inc(pe.h, 1)
                t.wait_ge(ve.h, ve_vt(p, 7))

                def st_pair(G):
                    _, qc, kt = tup(G)
                    if G >= 3:
                        t.wait_ge(se.h, se_exp_g(G - 3))
                    pst = ps_st[G % 3]
                    for half in range(2):
                        lhsT = K_sb[p % 2][half * 64:half * 64 + 64,
                                           kt * 128:(kt + 1) * 128]
                        col = qc * 512
                        ins = t.matmul(pst[:, half * 512:(half + 1) * 512], lhsT,
                                       Q_sb[p % 2][half * 64:half * 64 + 64,
                                                   col:col + 512],
                                       start=True, stop=True)
                    ins.then_inc(pe.h, 1)

                def av_pair(G):
                    _, qc, kt = tup(G)
                    t.wait_ge(se.h, se_exp_g(G))
                    if kt == 0:
                        Q = p * NQC + qc
                        if Q >= 2:
                            pp, qq = divmod(Q - 2, NQC)
                            t.wait_ge(se.h, se_oev(pp, qq, 1))
                            t.wait_ge(ve.h, ve_dst(pp, qq, 1))
                    ptb = PT[G % 3]
                    for half, vt, po in ((0, VT_e, ps_oE), (1, VT_o, ps_oO)):
                        ins = t.matmul(po[:, :], vt[:, kt * 80:kt * 80 + 65],
                                       ptb[:, half * 512:(half + 1) * 512],
                                       start=(kt == 0), stop=(kt == 63))
                    ins.then_inc(pe.h, 1)

                G0 = p * NQC * 64
                for qc in range(NQC):
                    base = G0 + qc * 64
                    st_pair(base)
                    for kt in range(64):
                        if kt + 1 < 64:
                            st_pair(base + kt + 1)
                        av_pair(base + kt)

        @blk.scalar
        def _(sc):
            for p in range(NP):
                for qc in range(NQC):
                    for kt in range(64):
                        G = (p * NQC + qc) * 64 + kt
                        sc.wait_ge(pe.h, pe_st_g(G))
                        if G >= 3:
                            sc.wait_ge(pe.h, pe_av_g(G - 3))
                        sc.activation(PT[G % 3][:, :], ps_st[G % 3][:, :], AF.Exp,
                                      bias=EXP_SHIFT, scale=1.0).then_inc(se.h, 1)
                    sc.wait_ge(pe.h, pe_av_g((p * NQC + qc) * 64 + 63))
                    for eo, po in ((0, ps_oE), (1, ps_oO)):
                        sc.activation(O_all[eo * 64:eo * 64 + 64,
                                            p * OWN + qc * 512: p * OWN + qc * 512 + 512],
                                      po[0:64, :], AF.Identity,
                                      bias=0.0, scale=1.0).then_inc(se.h, 1)

        @blk.vector
        def _(v):
            v.memset(VT_e[:, :].rearrange("p (t c) -> p t c", c=80)[:, :, 64:65], 1.0)
            v.memset(VT_o[:, :].rearrange("p (t c) -> p t c", c=80)[:, :, 64:65], 1.0)
            for p in range(NP):
                for b2 in range(8):
                    v.wait_ge(pe.h, pe_trb(p, b2))
                    vt = VT_e if b2 < 4 else VT_o
                    src = trview[:, :].rearrange("p (t c) -> p t c", c=64)
                    dst = vt[:, :].rearrange("p (t c) -> p t c", c=80)[
                        :, (b2 % 4) * 16:(b2 % 4 + 1) * 16, 0:64]
                    v.tensor_copy(dst, src).then_inc(ve.h, 1)
                for qc in range(NQC):
                    v.wait_ge(pe.h, pe_av_g((p * NQC + qc) * 64 + 63))
                    for eo, po in ((0, ps_oE), (1, ps_oO)):
                        lin = 8 * p + 2 * qc + eo
                        if lin >= 4:
                            v.wait_ge(gs.h, gs_d(lin - 4))
                        with nc.allow_low_precision(reason="softmax denominators in bf16"):
                            v.tensor_copy(D_st[2 * (qc % 2) + eo][:, :],
                                          po[64:65, :]).then_inc(ve.h, 1)

    ld.n = ld0 + 16 + 16 * 3 * NP
    gs.n = gs0c + 16 * 2 * NQC * NP
    pe.n = pe0 + PEP * NP
    ve.n = ve0 + VEP * NP
    se.n = se0 + SEP * NP

    pC2ps.close()
    pCE = ExitStack()
    ps_rb = pCE.enter_context(nc.psum_tensor("ps_rb", [64, OWN], F32))
    pe0e, ve0e = pe.n, ve.n
    with nc.Block("C2E") as blk:
        @blk.vector
        def _(v):
            v.wait_ge(gs.h, gs.n)
            with nc.allow_low_precision(reason="bf16 reciprocal of softmax denominators"):
                v.reciprocal(D_all[:, :], D_all[:, :]).then_inc(ve.h, 1)
            v.wait_ge(ve.h, ve0e + 1)
            v.engine_nop().then_inc(ve.h, 1)
            for h in range(2 * NP):
                jb, p0 = h // 2, (h % 2) * 64
                v.wait_ge(pe.h, pe0e + h + 1)
                v.tensor_tensor(O_all[p0:p0 + 64, jb * OWN:(jb + 1) * OWN],
                                O_all[p0:p0 + 64, jb * OWN:(jb + 1) * OWN],
                                ps_rb[:, :], ALU.mult).then_inc(ve.h, 1)

        @blk.tensor
        def _(t):
            t.wait_ge(ld.h, ld0 + 16)
            t.wait_ge(se.h, se.n)
            t.wait_ge(ve.h, ve0e + 2)
            for h in range(2 * NP):
                if h >= 1:
                    t.wait_ge(ve.h, ve0e + 2 + h)
                for tc in range(4):
                    ins = t.matmul(ps_rb[:, tc * 512:(tc + 1) * 512],
                                   sel16_sb[:, h * 64:(h + 1) * 64],
                                   D_all[:, tc * 512:(tc + 1) * 512],
                                   start=True, stop=True)
                ins.then_inc(pe.h, 1)
    pe.n = pe0e + 2 * NP
    ve.n = ve0e + 2 + 2 * NP
    pCE.close()
    pC2.close()

    if "O" in taps:
        tap_dump([(dbg["O"], O_all[:, :])])
    if stop_after == "C":
        return finish()

    pD = ExitStack()
    psb = lambda name, shape, dt: pD.enter_context(nc.sbuf_tensor(name, shape, dt))
    fc_sb = psb("fc_sb", [128, 8 * D], BF16)
    g_buf = [psb(f"g_buf{i}", [128, OWN], BF16) for i in range(2)]
    xT_buf = [psb(f"xT_buf{i}", [128, OWN], F32) for i in range(2)]
    s_buf = psb("s_buf", [128, OWN], F32)
    o_stage = [psb(f"o_stage{i}", [128, OWN], F32) for i in range(2)]
    ps_fc = [pD.enter_context(nc.psum_tensor(f"ps_fc{i}", [128, OWN], F32))
             for i in range(2)]

    ld0, gs0, pe0, ve0, sp0, se0 = ld.n, gs.n, pe.n, ve.n, sp.n, se.n
    ve_D = lambda jb: ve0 + 3 * jb + 3

    with nc.Block("D") as blk:
        @blk.gpsimd
        def _(g):
            _wt_dma(g, fc_sb, fc_wT[:, :], 8).then_inc(gs.h, 16)

        @blk.sync
        def _(s):
            for jb in range(8):
                if jb == 0:
                    s.wait_ge(sp.h, SP_G_DONE)
                if jb >= 2:
                    s.wait_ge(ve.h, ve_D(jb - 2))
                s.dma_start(g_buf[jb % 2][:, :],
                            g_hbm[:, jb * OWN:(jb + 1) * OWN]).then_inc(ld.h, 16)
                s.dma_start(xT_buf[jb % 2][:, :],
                            xT_own[jb * 128:(jb + 1) * 128, :]).then_inc(ld.h, 16)

        @blk.tensor
        def _(t):
            t.wait_ge(gs.h, gs0 + 16)
            for jb in range(8):
                if jb >= 2:
                    t.wait_ge(ve.h, ve_D(jb - 2) - 1)
                for j in range(8):
                    lhsT = fc_sb[:, j * D + jb * 128: j * D + (jb + 1) * 128]
                    for tc in range(4):
                        ins = t.matmul(ps_fc[jb % 2][:, tc * 512:(tc + 1) * 512],
                                       lhsT,
                                       O_all[:, j * OWN + tc * 512: j * OWN + (tc + 1) * 512],
                                       start=(j == 0), stop=(j == 7))
                ins.then_inc(pe.h, 1)

        @blk.vector
        def _(v):
            for jb in range(8):
                v.wait_ge(pe.h, pe0 + jb + 1)
                v.wait_ge(ld.h, ld0 + 16 * (2 * jb + 2))
                if jb >= 2:
                    v.wait_ge(sp.h, sp0 + 16 * (jb - 1))
                v.scalar_tensor_tensor(s_buf[:, :], ps_fc[jb % 2][:, :],
                                       fc_b_sb[:, jb:jb + 1],
                                       xn_own[:, jb * OWN:(jb + 1) * OWN],
                                       ALU.add, ALU.add).then_inc(ve.h, 1)
                v.wait_ge(ve.h, ve_D(jb) - 2)
                v.tensor_tensor(s_buf[:, :], s_buf[:, :], g_buf[jb % 2][:, :], ALU.mult
                                ).then_inc(ve.h, 1)
                v.wait_ge(ve.h, ve_D(jb) - 1)
                v.tensor_tensor(o_stage[jb % 2][:, :], s_buf[:, :], xT_buf[jb % 2][:, :],
                                ALU.add).then_inc(ve.h, 1)

        @blk.scalar
        def _(sc):
            for jb in range(8):
                sc.wait_ge(ve.h, ve_D(jb))
                sc.dma_start(out_T[jb * 128:(jb + 1) * 128, :],
                             o_stage[jb % 2][:, :]).then_inc(sp.h, 16)

    ld.n = ld0 + 16 * 16
    gs.n = gs0 + 16
    pe.n = pe0 + 8
    ve.n = ve0 + 24
    sp.n = sp0 + 16 * 8
    pD.close()
    return finish()


def _prep_weights(qkv_w, qkv_bv, fc_w, fc_bv, rt_w, rt_bv, gup, gup_bv, dn, dn_bv):
    qperm = np.concatenate([np.arange(h * 192, h * 192 + 64) for h in range(16)])
    perm = np.concatenate([qperm, qperm + 64, qperm + 128])
    qkv_wT = np.ascontiguousarray(qkv_w[perm].T)
    qkv_b_arr = np.ascontiguousarray(qkv_bv[perm].reshape(24, 128).T)
    fc_wT = np.ascontiguousarray(fc_w.T)
    fc_b_arr = np.ascontiguousarray(fc_bv.reshape(8, 128).T)
    rt_wT = np.ascontiguousarray(rt_w.T)
    rt_b_arr = np.ascontiguousarray(rt_bv.reshape(1, 8))
    iperm = np.empty(4096, dtype=np.int64)
    blk = np.arange(64)
    for b in range(32):
        iperm[b * 128: b * 128 + 64] = b * 64 + blk
        iperm[b * 128 + 64: (b + 1) * 128] = 2048 + b * 64 + blk
    gup_wT = np.ascontiguousarray(gup[:, iperm, :].transpose(0, 2, 1))
    gup_b_arr = np.ascontiguousarray(
        gup_bv[:, iperm].reshape(8, 32, 128).transpose(2, 0, 1).reshape(128, 256))
    dn_wT = np.ascontiguousarray(dn.transpose(0, 2, 1))
    return dict(qkv_wT=qkv_wT, qkv_b=qkv_b_arr, fc_wT=fc_wT, fc_b=fc_b_arr,
                rt_wT=rt_wT, rt_b=rt_b_arr, gup_wT=gup_wT, gup_b=gup_b_arr,
                dn_wT=dn_wT, dn_b=np.ascontiguousarray(dn_bv),
                ident=np.eye(128, dtype=np.float32).astype(ml_dtypes.bfloat16),
                identf=np.eye(128, dtype=np.float32),
                sel=np.repeat(np.eye(8, dtype=np.float32), 128, axis=1
                              ).astype(ml_dtypes.bfloat16),
                sel16=np.repeat(np.eye(16, dtype=np.float32), 64, axis=1
                                ).astype(ml_dtypes.bfloat16))


def make_in_maps(inputs):
    x = np.asarray(inputs["L0_attn_x"], np.float32)
    w = _prep_weights(
        np.asarray(inputs["L0_attn_qkv_w"], np.float32),
        np.asarray(inputs["L0_attn_qkv_b"], np.float32),
        np.asarray(inputs["L0_attn_fc_w"], np.float32),
        np.asarray(inputs["L0_attn_fc_b"], np.float32),
        np.asarray(inputs["L0_ffn_router_weight"], np.float32),
        np.asarray(inputs["L0_ffn_router_bias"], np.float32),
        np.asarray(inputs["L0_ffn_gate_up_proj"], np.float32),
        np.asarray(inputs["L0_ffn_gate_up_proj_bias"], np.float32),
        np.asarray(inputs["L0_ffn_down_proj"], np.float32),
        np.asarray(inputs["L0_ffn_down_proj_bias"], np.float32))
    in_maps = []
    for c in range(8):
        b, q0 = c // 4, (c % 4) * OWN
        xb = x[b]
        x_rot = np.ascontiguousarray(np.concatenate([xb[q0:], xb[:q0]], axis=0))
        m = dict(w)
        m["x_full"] = x_rot
        m["xT_own"] = np.ascontiguousarray(x_rot[:OWN].T)
        in_maps.append(m)
    return in_maps


def kernel(**inputs):
    in_maps = make_in_maps(inputs)
    nc, _ = build_program()
    res = run_bass_kernel_spmd(nc, in_maps, core_ids=list(range(8)))
    out = np.empty((2, SEQ, D), np.float32)
    for c in range(8):
        b, q0 = c // 4, (c % 4) * OWN
        out[b, q0:q0 + OWN, :] = np.ascontiguousarray(res.results[c]["out_T"]).T
    return out
